# version 12
# baseline (speedup 1.0000x reference)
import contextlib
import numpy as np
import ml_dtypes
import concourse.bass as bass
import concourse.mybir as mybir
from concourse.bass_utils import run_bass_kernel_spmd

F32 = mybir.dt.float32
BF16 = mybir.dt.bfloat16
I8 = mybir.dt.int8
AF = mybir.ActivationFunctionType
ALU = mybir.AluOpType
AX = mybir.AxisListType
bfd = ml_dtypes.bfloat16

MB = 100.0
EPS = 1e-5
DM = 2048
INTER = 8192
SQK = 8.0 ** -0.5


def sigma_perm(n=4096):
    t = np.arange(n)
    ht, wb = t // 128, t % 128
    d, c0 = ht // 4, ht % 4
    wbb, e = wb // 8, wb % 8
    return ((c0 * 16 + wbb) * 8 + d) * 8 + e

SIG = sigma_perm()

def aux_input_rows():
    tss = SIG[np.arange(4096)]
    w1 = (tss >> 3) & 15
    w2 = (tss & 7) | (((tss >> 6) & 1) << 3)
    rows = [np.ones(4096)]
    rows += [(w1 >> k) & 1 for k in range(4)]
    rows += [(w2 >> k) & 1 for k in range(4)]
    return np.asarray(rows, dtype=np.float32)

def qk_maps(qkv_w):
    maps = []
    for which in (0, 1):
        m = np.zeros((64, 128), np.float32)
        wsub = qkv_w[:, which * 16:(which + 1) * 16]
        for s in range(2):
            r0, cc = 32 * s, 64 * s
            for br in range(2):
                cb = cc + 32 * br
                m[r0:r0 + 16, cb:cb + 8] = wsub[:, br * 8:br * 8 + 8]
                for k in range(4):
                    bitrow = r0 + 17 + 4 * br + k
                    onerow = r0 + 16
                    a, b_, c_ = cb + 8 + 3 * k, cb + 9 + 3 * k, cb + 10 + 3 * k
                    if which == 0:
                        m[bitrow, a] = -MB
                        m[onerow, b_] = -MB
                        m[bitrow, c_] = 2 * MB
                    else:
                        m[onerow, a] = 1.0
                        m[bitrow, b_] = 1.0
                        m[bitrow, c_] = 1.0
        maps.append(m)
    return maps

def v_map(qkv_w):
    m = np.zeros((64, 128), np.float32)
    wv = qkv_w[:, 32:48]
    for s in range(2):
        r0, cc = 32 * s, 32 * s
        m[r0:r0 + 16, cc:cc + 8] = wv[:, 0:8]
        m[r0 + 16, cc + 8] = 1.0
        m[r0:r0 + 16, cc + 9:cc + 17] = wv[:, 8:16]
    return m

def lep_proj_map(proj_w):
    m = np.zeros((128, 64), np.float32)
    for s in range(4):
        r0, cc = 32 * s, 16 * s
        m[r0:r0 + 8, cc:cc + 16] = proj_w[0:8, :]
        m[r0 + 9:r0 + 17, cc:cc + 16] = proj_w[8:16, :]
    return m

def proj_bd8(proj_w):
    m = np.zeros((128, 128), np.float32)
    for g in range(8):
        m[g * 16:(g + 1) * 16, g * 16:(g + 1) * 16] = proj_w
    return m

def bd16_avg():
    m = np.zeros((128, 128), np.float32)
    for g in range(8):
        m[g * 16:(g + 1) * 16, g * 16:(g + 1) * 16] = 1.0 / 16.0
    return m

def cw_cols(cw0, cb0, cw1, cb1):
    cols = np.zeros((128, 6), np.float32)
    for s in range(4):
        r0 = 32 * s
        for c in range(8):
            cols[r0 + c, 0] = cw0[c, 0, 1, 1]
            cols[r0 + c, 1] = cb0[c]
            cols[r0 + c, 2] = cw0[c, 0, 1, 0]
            cols[r0 + c, 3] = cw0[c, 0, 1, 2]
            cols[r0 + 9 + c, 0] = cw1[c, 0, 1, 1]
            cols[r0 + 9 + c, 1] = cb1[c]
            cols[r0 + 9 + c, 4] = cw1[c, 0, 0, 1]
            cols[r0 + 9 + c, 5] = cw1[c, 0, 2, 1]
    return cols

def ln_cols(ln_w, ln_b, proj_b):
    cols = np.zeros((128, 3), np.float32)
    for p_ in range(128):
        c = p_ % 16
        cols[p_] = (ln_w[c], ln_b[c], proj_b[c])
    return cols

def ffn_weight_tiles(wg, wu, wd):
    gu = np.zeros((64, 128, 2, 16, 128), np.int8)
    for ic in range(64):
        gu[ic, :, 0] = wg[:, ic * 128:(ic + 1) * 128].reshape(16, 128, 128).transpose(1, 0, 2)
        gu[ic, :, 1] = wu[:, ic * 128:(ic + 1) * 128].reshape(16, 128, 128).transpose(1, 0, 2)
    dn = np.zeros((4, 32, 128, 2, 512), np.int8)
    for oc in range(4):
        for ic in range(64):
            dn[oc, ic // 2, :, ic % 2] = wd[ic * 128:(ic + 1) * 128,
                                            oc * 512:(oc + 1) * 512]
    return (np.ascontiguousarray(gu.reshape(64 * 128, 4096)),
            np.ascontiguousarray(dn.reshape(4 * 32 * 128, 1024)))


class Sched:
    def __init__(self):
        self.ops = []
        self.counts = {}

    def emit(self, engine, fn, waits=(), incs=()):
        self.ops.append((engine, fn, [w for w in waits if w is not None], list(incs)))

    def inc(self, sem, amt):
        k = id(sem)
        self.counts[k] = self.counts.get(k, 0) + amt
        return (sem, amt)

    def at(self, sem):
        return (sem, self.counts.get(id(sem), 0))

    def run(self, engine, eng):
        for e, fn, waits, incs in self.ops:
            if e != engine:
                continue
            for sem, val in waits:
                if val > 0:
                    eng.wait_ge(sem, val)
            inst = fn(eng)
            for sem, amt in incs:
                inst.then_inc(sem, amt)


def apx(base, offset, dims):
    t = getattr(base, "tensor", base)
    fsz = int(np.prod(t.shape[1:]))
    np_ = int(t.shape[0])
    ap = [[fsz, np_]] + [list(d) for d in dims]
    return bass.AP(t, offset, ap)


class Prog:
    def __init__(self, nrows=2048, nbatch=4, taps=(), stage=9, rd=False):
        assert nrows % (32 * nbatch) == 0 and nrows % 512 == 0
        self.stage = stage
        self.rd = rd
        self.nrows = nrows
        self.nbatch = nbatch
        self.rpb = nrows // nbatch
        self.nsub = nrows // 32
        self.ntok = nrows * 128
        self.taps = set(taps)

    def build(self):
        p = self
        NR, NS, NT = p.nrows, p.nsub, p.ntok
        NSEG = NS // 4
        L16S = min(8, NS)
        RT = NR // 128
        MT = p.rpb // 16
        nc = bass.Bass("TRN2", target_bir_lowering=False, detect_race_conditions=self.rd)

        ein, eout = "ExternalInput", "ExternalOutput"
        xin = nc.dram_tensor("xin", [NR, DM], F32, kind=ein).ap()
        xperm = nc.dram_tensor("xperm", [NR, DM], F32, kind=ein).ap()
        qmap_d = nc.dram_tensor("qmapw", [64, 128], F32, kind=ein).ap()
        kmap_d = nc.dram_tensor("kmapw", [64, 128], F32, kind=ein).ap()
        vmap_d = nc.dram_tensor("vmapw", [64, 128], F32, kind=ein).ap()
        auxin_d = nc.dram_tensor("auxin", [9, 4096], F32, kind=ein).ap()
        ident_d = nc.dram_tensor("identw", [128, 128], F32, kind=ein).ap()
        lepmap_d = nc.dram_tensor("lepmapw", [128, 64], F32, kind=ein).ap()
        projbd_d = nc.dram_tensor("projbdw", [128, 128], F32, kind=ein).ap()
        bd16_d = nc.dram_tensor("bd16w", [128, 128], F32, kind=ein).ap()
        cw_d = nc.dram_tensor("cwcols", [128, 6], F32, kind=ein).ap()
        lnc_d = nc.dram_tensor("lncols", [128, 3], F32, kind=ein).ap()
        wgu_d = nc.dram_tensor("wgu", [64 * 128, 4096], I8, kind=ein).ap()
        wdn_d = nc.dram_tensor("wdn", [4 * 32 * 128, 1024], I8, kind=ein).ap()
        sg_d = nc.dram_tensor("sgcol", [128, 64], F32, kind=ein).ap()
        su_d = nc.dram_tensor("sucol", [128, 64], F32, kind=ein).ap()
        sdn_d = nc.dram_tensor("sdnw", [128, DM], F32, kind=ein).ap()
        eps_d = nc.dram_tensor("epscol", [128, 1], F32, kind=ein).ap()

        out_d = nc.dram_tensor("out", [NR, DM], F32, kind=eout).ap()

        lz = nc.dram_tensor("lz", [NR, DM], BF16, kind="Internal").ap()
        avp = nc.dram_tensor("avp", [16, NT], BF16, kind="Internal").ap()
        lepp = nc.dram_tensor("lepp", [16, NT], BF16, kind="Internal").ap()
        attnd = nc.dram_tensor("attnd", [16, NT], BF16, kind="Internal").ap()
        x1d = nc.dram_tensor("x1d", [NR, DM], F32, kind="Internal").ap()

        tapd = {}
        def mktap(name, shape, dtype=F32):
            if name in p.taps:
                tapd[name] = nc.dram_tensor("tap_" + name, shape, dtype, kind=eout).ap()
            return tapd.get(name)

        t_lz = mktap("lz", [NR, DM])
        t_qbuf = mktap("qbuf", [128, 4096])
        t_kbuf = mktap("kbuf", [128, 4096])
        t_vsb = mktap("vsb", [128, 4096])
        t_lep = mktap("lep", [128, 4096])
        t_asb = mktap("asb", [128, 2048])
        t_att1 = mktap("att1", [128, 128])
        t_avp = mktap("avp", [16, NT])
        t_lepp = mktap("lepp", [16, NT])
        t_attn = mktap("attn", [16, NT])
        t_x1 = mktap("x1", [NR, DM])
        t_fm = mktap("fm", [128, 16 * NR])

        S = Sched()
        FMW = 16 * NR

        with contextlib.ExitStack() as _st:
            _en = _st.enter_context
            qmap = _en(nc.sbuf_tensor("c_qmap", [64, 128], BF16))
            kmap = _en(nc.sbuf_tensor("c_kmap", [64, 128], BF16))
            vmap = _en(nc.sbuf_tensor("c_vmap", [64, 128], BF16))
            ident = _en(nc.sbuf_tensor("c_ident", [128, 128], BF16))
            lepmap = _en(nc.sbuf_tensor("c_lepmap", [128, 64], BF16))
            projbd = _en(nc.sbuf_tensor("c_projbd", [128, 128], BF16))
            bd16 = _en(nc.sbuf_tensor("c_bd16", [128, 128], BF16))
            cw = _en(nc.sbuf_tensor("c_cw", [128, 6], F32))
            lnc = _en(nc.sbuf_tensor("c_lnc", [128, 3], F32))
            epsc = _en(nc.sbuf_tensor("c_eps", [128, 1], F32))
            sg = _en(nc.sbuf_tensor("c_sg", [128, 64], F32))
            su = _en(nc.sbuf_tensor("c_su", [128, 64], F32))
            sdn = _en(nc.sbuf_tensor("c_sdn", [128, DM], BF16))
            af32 = _en(nc.sbuf_tensor("af32", [128, 6176], F32))
            ab = _en(nc.sbuf_tensor("ab", [128, 75776], BF16))
            gsil = _en(nc.sbuf_tensor("gsil", [128, 512], BF16))
            wdi0 = _en(nc.sbuf_tensor("wdi0", [128, 1024], I8))
            wdi1 = _en(nc.sbuf_tensor("wdi1", [128, 1024], I8))
            wgi0 = _en(nc.sbuf_tensor("wgi0", [128, 4096], I8))
            wgi1 = _en(nc.sbuf_tensor("wgi1", [128, 4096], I8))
            rsb = _en(nc.sbuf_tensor("rsb", [128, 16], F32))
            wg0 = _en(nc.sbuf_tensor("wg0", [128, 4096], BF16))
            wg1 = _en(nc.sbuf_tensor("wg1", [128, 4096], BF16))

            pb0 = _en(nc.psum_tensor("pb0", [128, 512], F32))
            pb1 = _en(nc.psum_tensor("pb1", [128, 512], F32))
            pb2 = _en(nc.psum_tensor("pb2", [128, 512], F32))
            psgrp = _en(nc.psum_tensor("psgrp", [128, 2048], F32))
            pb7 = _en(nc.psum_tensor("pb7", [128, 512], F32))
            sdma = _en(nc.semaphore("sdma"))
            gdma = _en(nc.semaphore("gdma"))
            tdma = _en(nc.semaphore("tdma"))
            smm = _en(nc.semaphore("smm"))
            sev = _en(nc.semaphore("sev"))
            ssc = _en(nc.semaphore("ssc"))
            sgp = _en(nc.semaphore("sgp"))
            wcv = _en(nc.semaphore("wcv"))
            sbar = _en(nc.semaphore("sbar"))
            blk = _en(nc.Block())

            def A(c0_, w):
                return ab[:, c0_:c0_ + w]
            fmsb = A(0, FMW)
            B0 = FMW
            fuse = A(B0, 32768)
            qb0 = A(B0, 4096); qb1 = A(B0 + 4096, 4096)
            kb0 = A(B0 + 8192, 4096); kb1 = A(B0 + 12288, 4096)
            vsb = A(B0 + 16384, 4096); lepb = A(B0 + 20480, 4096)
            vtsb = A(B0 + 24576, 4096)
            asb0 = A(B0 + 28672, 1024); asb1 = A(B0 + 29696, 1024)

            l16a = A(B0, 4096); l16w = A(B0 + 4096, 4096)
            l16d = A(B0 + 8192, 4096); l16o = A(B0 + 12288, 4096)
            C0 = B0 + 32768
            ffnsb = A(C0, 4 * DM)
            fla = ab[0:64, C0:C0 + 4096]
            flb = ab[0:64, C0 + 4096:C0 + 8192]
            rowsb = A(C0 + 8192, 2048)
            att1 = A(C0 + 8192, 128)
            cmsb = A(C0 + 8320, 128)
            avps = A(C0 + 8448, 128)
            lepps = A(C0 + 8576, 512)
            wd0 = A(C0 + 8192, 1024)
            wd1 = A(C0 + 9216, 1024)
            rows = af32[:, 0:2048]
            rows2 = af32[:, 2048:4096]
            rows3 = af32[:, 4096:6144]
            stat = af32[:, 6144:6176]
            l16r = af32[:, 0:4096]

            psq, psk, psv = pb0, pb1, pb2
            psav, psm, psm2 = pb0, pb2, pb7
            pstr = pb1[:, :].bitcast(BF16)

            nbar = [0]

            def barrier():
                nbar[0] += 1
                tgt = 5 * nbar[0]
                ws = {"sync": [S.at(sdma)], "gpsimd": [S.at(gdma)],
                      "scalar": [S.at(tdma)], "tensor": [], "vector": []}
                for en in ("sync", "gpsimd", "tensor", "vector", "scalar"):
                    S.emit(en, lambda e: e.nop(), ws[en], [S.inc(sbar, 1)])
                for en in ("sync", "gpsimd", "tensor", "vector", "scalar"):
                    S.emit(en, lambda e: e.nop(), [(sbar, tgt)], [])

            def dma(engine, sem, dst, src, waits=()):
                S.emit(engine, lambda e, d=dst, s=src: e.dma_start(out=d, in_=s),
                       waits, [S.inc(sem, 16)])
                return S.at(sem)

            for d, s_ in ((qmap, qmap_d), (kmap, kmap_d), (vmap, vmap_d),
                          (ident, ident_d), (lepmap, lepmap_d),
                          (projbd, projbd_d), (bd16, bd16_d),
                          (sdn, sdn_d)):
                dma("gpsimd", gdma, d[:], s_)
            for d, s_ in ((cw, cw_d), (lnc, lnc_d), (sg, sg_d), (su, su_d), (epsc, eps_d)):
                dma("sync", sdma, d[:], s_)
            S.emit("vector", lambda e: e.memset(fla[:], 0.0), [], [S.inc(sev, 1)])
            S.emit("vector", lambda e: e.memset(flb[:], 0.0), [], [S.inc(sev, 1)])
            mz = S.at(sev)
            for buf in (fla, flb):
                for st in (0, 32):
                    dma("gpsimd", gdma, buf[st + 16:st + 25, :], auxin_d, [mz])
            barrier()

            def ln_stats(src, junk, waits=()):
                waits = list(waits)
                S.emit("vector", lambda e: e.reduce_sum(stat[:, 0:1], src, AX.X),
                       waits + [S.at(ssc)], [S.inc(sev, 1)])
                S.emit("scalar", lambda e: e.activation(
                    junk, src, AF.Square, accum_out=stat[:, 1:2]),
                    waits + [S.at(sev)], [S.inc(ssc, 1)])
                S.emit("vector", lambda e: e.tensor_scalar(
                    stat[:, 2:4], stat[:, 0:2], 1.0 / DM, None, ALU.mult),
                    [S.at(ssc), S.at(sev)], [S.inc(sev, 1)])
                S.emit("vector", lambda e: e.tensor_tensor(
                    stat[:, 4:5], stat[:, 2:3], stat[:, 2:3], ALU.mult),
                    [S.at(sev)], [S.inc(sev, 1)])
                S.emit("vector", lambda e: e.tensor_sub(
                    stat[:, 5:6], stat[:, 3:4], stat[:, 4:5]), [S.at(sev)],
                    [S.inc(sev, 1)])
                S.emit("scalar", lambda e: e.activation(
                    stat[:, 6:7], stat[:, 5:6], AF.Ln, bias=epsc[:, 0:1]),
                    [S.at(sev)], [S.inc(ssc, 1)])
                S.emit("scalar", lambda e: e.activation(
                    stat[:, 7:8], stat[:, 6:7], AF.Exp, scale=-0.5), [S.at(ssc)],
                    [S.inc(ssc, 1)])
                return S.at(ssc)

            for rt in range(RT):
                w_ld = dma("sync", sdma, rows[:], xin[rt * 128:(rt + 1) * 128, :],
                           [S.at(sev)])
                w_rstd = ln_stats(rows[:], rows2[:], [w_ld])
                S.emit("vector", lambda e: e.tensor_scalar(
                    rowsb.rearrange("p (c w) -> p w c", c=16),
                    rows.rearrange("p (w c) -> p w c", c=16),
                    stat[:, 2:3], stat[:, 7:8], ALU.subtract, ALU.mult),
                    [w_rstd, S.at(sdma), S.at(gdma)], [S.inc(sev, 1)])
                dma("sync", sdma, lz[rt * 128:(rt + 1) * 128, :], rowsb[:],
                    [S.at(sev)])
                if t_lz is not None:
                    dma("gpsimd", gdma, t_lz[rt * 128:(rt + 1) * 128, :], rowsb[:],
                        [S.at(sev)])
            barrier()

            def lz_cAP(sub):
                return lz.rearrange("r (c w) -> c r w", c=16)[
                    :, sub * 32:(sub + 1) * 32, :]

            def sig_out(buf, cc):
                v = buf.rearrange("p (a b d e) -> p a b d e", a=4, b=16, d=8, e=8)
                return v[:, :, :, cc, :]

            w_qkv_mm_end = None
            w_exp_par = [None, None]
            for seg in range(NSEG if p.stage >= 2 else 0):
                sub0 = seg * 4
                wfl = []
                flw = [w_qkv_mm_end] if w_qkv_mm_end else []
                for ipair, buf in ((0, fla), (1, flb)):
                    for sl_ in range(2):
                        sub = sub0 + 2 * ipair + sl_
                        wfl.append(dma("gpsimd", gdma,
                                       buf[sl_ * 32:sl_ * 32 + 16, :], lz_cAP(sub),
                                       flw))
                first_ev = True
                for ipair, buf, qb, kb in ((0, fla, qb0, kb0), (1, flb, qb1, kb1)):
                    for cc in range(8):
                        sl_ = slice(cc * 512, (cc + 1) * 512)
                        S.emit("tensor", lambda e, b=buf, s=sl_: e.matmul(
                            psq[:, 0:512], qmap[:], b[:, s], start=True, stop=True),
                            [wfl[2 * ipair], wfl[2 * ipair + 1], S.at(sev)],
                            [S.inc(smm, 1)])
                        S.emit("tensor", lambda e, b=buf, s=sl_: e.matmul(
                            psk[:, 0:512], kmap[:], b[:, s], start=True, stop=True),
                            [], [S.inc(smm, 1)])
                        wmm = S.at(smm)
                        evw = [wmm, S.at(smm)] if first_ev else [wmm]
                        first_ev = False
                        S.emit("vector", lambda e, d_=qb, c=cc: e.tensor_copy(
                            sig_out(d_, c), psq[:, 0:512]), evw, [S.inc(sev, 1)])
                        S.emit("vector", lambda e, d_=kb, c=cc: e.tensor_copy(
                            sig_out(d_, c), psk[:, 0:512]), [], [S.inc(sev, 1)])
                for cc in range(8):
                    sl_ = slice(cc * 512, (cc + 1) * 512)
                    S.emit("tensor", lambda e, s=sl_: e.matmul(
                        psv[0:64, 0:512], vmap[0:64, 0:64], fla[:, s],
                        start=True, stop=True), [S.at(sev)], [S.inc(smm, 1)])
                    S.emit("tensor", lambda e, s=sl_: e.matmul(
                        psv[64:128, 0:512], vmap[0:64, 0:64], flb[:, s],
                        start=True, stop=True, tile_position=(0, 64)),
                        [], [S.inc(smm, 1)])
                    S.emit("vector", lambda e, c=cc: e.tensor_copy(
                        sig_out(vsb, c), psv[:, 0:512]), [S.at(smm)],
                        [S.inc(sev, 1)])
                w_qkv_mm_end = S.at(smm)
                w_qkv = S.at(sev)
                if t_qbuf is not None and seg == 0:
                    dma("gpsimd", gdma, t_qbuf, qb0[:], [w_qkv])
                    dma("gpsimd", gdma, t_kbuf, kb0[:], [w_qkv])
                    dma("gpsimd", gdma, t_vsb, vsb[:], [w_qkv])

                for b4 in range(8):
                    for j in range(4):
                        beta = b4 * 4 + j
                        S.emit("tensor", lambda e, b=beta, jj=j: e.transpose(
                            pstr[:, jj * 128:(jj + 1) * 128],
                            vsb[:, b * 128:(b + 1) * 128], ident[:]),
                            [w_qkv] + ([S.at(sev)] if j == 0 else []),
                            [S.inc(smm, 1)])
                    S.emit("vector", lambda e, b=b4: e.tensor_copy(
                        vtsb[:, b * 512:(b + 1) * 512], pstr[:, 0:512]),
                        [S.at(smm)], [S.inc(sev, 1)])
                w_vt = S.at(sev)
                S.emit("vector", lambda e: e.tensor_scalar(
                    lepb[:], vsb[:], cw[:, 0:1], cw[:, 1:2], ALU.mult, ALU.add),
                    [w_qkv, S.at(smm)], [S.inc(sev, 1)])
                vv = vsb.rearrange("p (w e) -> p w e", e=8)
                ll = lepb.rearrange("p (w e) -> p w e", e=8)
                S.emit("vector", lambda e: e.scalar_tensor_tensor(
                    ll[:, :, 1:8], vv[:, :, 0:7], cw[:, 2:3], ll[:, :, 1:8],
                    ALU.mult, ALU.add), [S.at(sev)], [S.inc(sev, 1)])
                S.emit("vector", lambda e: e.scalar_tensor_tensor(
                    ll[:, :, 0:7], vv[:, :, 1:8], cw[:, 3:4], ll[:, :, 0:7],
                    ALU.mult, ALU.add), [S.at(sev)], [S.inc(sev, 1)])
                vv8 = vsb.rearrange("p (b d e) -> p b d e", d=8, e=8)
                ll8 = lepb.rearrange("p (b d e) -> p b d e", d=8, e=8)
                S.emit("vector", lambda e: e.scalar_tensor_tensor(
                    ll8[:, :, 1:8, :], vv8[:, :, 0:7, :], cw[:, 4:5],
                    ll8[:, :, 1:8, :], ALU.mult, ALU.add), [S.at(sev)],
                    [S.inc(sev, 1)])
                S.emit("vector", lambda e: e.scalar_tensor_tensor(
                    ll8[:, :, 0:7, :], vv8[:, :, 1:8, :], cw[:, 5:6],
                    ll8[:, :, 0:7, :], ALU.mult, ALU.add), [S.at(sev)],
                    [S.inc(sev, 1)])
                w_lep = S.at(sev)
                if t_lep is not None and seg == 0:
                    dma("gpsimd", gdma, t_lep, lepb[:], [w_lep])

                for s4 in range(4 if p.stage >= 3 else 0):
                    sub = sub0 + s4
                    qb, kb = (qb0, kb0) if s4 < 2 else (qb1, kb1)
                    st1 = (s4 % 2) * 64
                    st2 = st1 + 32

                    def emit_grams(bg):
                        half = bg % 2
                        hof = half * 1024
                        a_sb = asb0 if half == 0 else asb1
                        w_war = [w_exp_par[half]] if w_exp_par[half] else []
                        for j in range(4):
                            beta = bg * 4 + j
                            csl = slice(beta * 128, (beta + 1) * 128)
                            for br, st in ((0, st1), (1, st2)):
                                u = br * 4 + j
                                S.emit("tensor", lambda e, k_=kb, q_=qb, s=csl,
                                       r0=st, uu=u, ho=hof: e.matmul(
                                    psgrp[:, ho + uu * 128:ho + (uu + 1) * 128],
                                    k_[r0:r0 + 20, s], q_[r0:r0 + 20, s],
                                    start=True, stop=True, tile_position=(r0, 0)),
                                    w_war if (j == 0 and br == 0) else [],
                                    [S.inc(smm, 1)])
                        S.emit("scalar", lambda e, ho=hof, ab_=a_sb: e.activation(
                            ab_[:, 0:1024], psgrp[:, ho:ho + 1024], AF.Exp,
                            scale=SQK), [S.at(smm)], [S.inc(ssc, 1)])
                        w_exp_par[half] = S.at(ssc)
                        return (bg, S.at(ssc), a_sb)

                    def emit_av(info):
                        bg, w_exp, a_sb = info
                        if t_asb is not None and seg == 0 and s4 == 0 and bg < 2:
                            dma("gpsimd", gdma, t_asb[:, bg * 1024:(bg + 1) * 1024],
                                a_sb[:, 0:1024], [w_exp])
                        for j in range(4):
                            beta = bg * 4 + j
                            jj = (bg % 2) * 4 + j
                            for br in range(2):
                                u = br * 4 + j
                                r0_ = beta * 128 + s4 * 32 + br * 8
                                S.emit("tensor", lambda e, uu=u, jx=jj, b_=br,
                                       r=r0_, ab_=a_sb: e.matmul(
                                    psav[:, jx * 18 + b_ * 9:jx * 18 + b_ * 9 + 9],
                                    ab_[:, uu * 128:(uu + 1) * 128],
                                    vtsb[:, r:r + 9], start=True, stop=True),
                                    [w_exp, w_vt, S.at(sev)]
                                    if (j == 0 and br == 0) else [],
                                    [S.inc(smm, 1)])

                    def emit_pairpost(bgodd):
                        w_av = S.at(smm)
                        S.emit("vector", lambda e: e.reciprocal(
                            rsb[:, 0:16].rearrange("p (a b) -> p a b", b=2),
                            apx(psav, 8, [[18, 8], [1, 2]])), [w_av],
                            [S.inc(sev, 1)])
                        S.emit("vector", lambda e: e.tensor_tensor(
                            att1.rearrange("p (j b c) -> p j b c", b=2, c=8),
                            apx(psav, 0, [[18, 8], [10, 2], [1, 8]]),
                            apx(rsb, 0, [[2, 8], [1, 2], [0, 8]]),
                            ALU.mult), [S.at(smm), S.at(sev)], [S.inc(sev, 1)])
                        w_n = S.at(sev)
                        if t_att1 is not None and seg == 0 and s4 == 0 and bgodd == 1:
                            dma("gpsimd", gdma, t_att1, att1[:], [w_n])
                        S.emit("tensor", lambda e: e.transpose(
                            pstr[:, 512:640], att1[:], ident[:]),
                            [w_n], [S.inc(smm, 1)])
                        S.emit("vector", lambda e: e.tensor_copy(
                            cmsb[:], pstr[:, 512:640]), [S.at(smm)],
                            [S.inc(sev, 1)])
                        S.emit("tensor", lambda e: e.matmul(
                            psm2[:, 0:128], projbd[:], cmsb[:],
                            start=True, stop=True), [S.at(sev)], [S.inc(smm, 1)])
                        S.emit("vector", lambda e: e.tensor_copy(
                            avps[:], psm2[:, 0:128]), [S.at(smm), S.at(sdma)],
                            [S.inc(sev, 1)])
                        bgp = bgodd // 2
                        dst = avp.rearrange("c (u b t) -> u b c t", u=NS, t=128)[
                            sub, bgp * 8:(bgp + 1) * 8, :, :]
                        dma("sync", sdma, dst, avps[:], [S.at(sev)])

                    pend = None
                    for bg in range(8):
                        info = emit_grams(bg)
                        if pend is not None:
                            emit_av(pend)
                            if pend[0] % 2 == 1:
                                emit_pairpost(pend[0])
                        pend = info
                    emit_av(pend)
                    emit_pairpost(7)
                for cc in range(8):
                    sl_ = slice(cc * 512, (cc + 1) * 512)
                    S.emit("tensor", lambda e, s=sl_: e.matmul(
                        psm[0:64, 0:512], lepmap[:], lepb[:, s], start=True,
                        stop=True), [w_lep, w_qkv, S.at(sev)], [S.inc(smm, 1)])
                    S.emit("vector", lambda e: e.tensor_copy(
                        lepps[0:64, :], psm[0:64, 0:512]), [S.at(smm), S.at(sdma)],
                        [S.inc(sev, 1)])
                    dst = lepp.rearrange("c (u t) -> u c t", t=4096)[
                        sub0:sub0 + 4, :, cc * 512:(cc + 1) * 512]
                    dma("sync", sdma, dst, lepps[0:64, :], [S.at(sev)])


            barrier()

            if t_avp is not None:
                dma("gpsimd", gdma, t_avp, avp)
            if t_lepp is not None:
                dma("gpsimd", gdma, t_lepp, lepp)

            for g in range(NS // L16S if p.stage >= 7 else 0):
                u0 = g * L16S
                pr = L16S * 16
                srcA = avp.rearrange("c (u t) -> u c t", t=4096)[u0:u0 + L16S]
                srcB = lepp.rearrange("c (u t) -> u c t", t=4096)[u0:u0 + L16S]
                wa = dma("gpsimd", gdma, l16a[0:pr, :], srcA, [S.at(sev)])
                wb_ = dma("gpsimd", gdma, l16d[0:pr, :], srcB, [S.at(sev)])
                S.emit("vector", lambda e: e.scalar_tensor_tensor(
                    l16w[0:pr, :], l16a[0:pr, :], lnc[0:pr, 2:3], l16d[0:pr, :],
                    ALU.add, ALU.add), [wa, wb_], [S.inc(sev, 1)])
                w_w = S.at(sev)
                for cc in range(8):
                    sl_ = slice(cc * 512, (cc + 1) * 512)
                    S.emit("tensor", lambda e, s=sl_: e.matmul(
                        psm[:, 0:512], bd16[:], l16w[:, s], start=True, stop=True),
                        [w_w, S.at(sev)], [S.inc(smm, 1)])
                    S.emit("vector", lambda e, s=sl_: e.tensor_sub(
                        l16d[:, s], l16w[:, s], psm[:, 0:512]), [S.at(smm)],
                        [S.inc(sev, 1)])
                    S.emit("scalar", lambda e, s=sl_: e.activation(
                        l16a[:, s], l16d[:, s], AF.Square), [S.at(sev)],
                        [S.inc(ssc, 1)])
                    S.emit("tensor", lambda e, s=sl_: e.matmul(
                        psm2[:, 0:512], bd16[:], l16a[:, s], start=True,
                        stop=True), [S.at(ssc)], [S.inc(smm, 1)])
                    S.emit("scalar", lambda e, s=sl_: e.activation(
                        l16r[:, s], psm2[:, 0:512], AF.Ln, bias=epsc[:, 0:1]),
                        [S.at(smm)], [S.inc(ssc, 1)])
                    S.emit("scalar", lambda e, s=sl_: e.activation(
                        l16r[:, s], l16r[:, s], AF.Exp, scale=-0.5), [S.at(ssc)],
                        [S.inc(ssc, 1)])
                    S.emit("vector", lambda e, s=sl_: e.tensor_mul(
                        l16d[:, s], l16d[:, s], l16r[:, s]), [S.at(ssc)],
                        [S.inc(sev, 1)])
                    lam = cc
                    o4 = l16o.rearrange("p (d cz wh wl e) -> p d cz wh wl e",
                                        d=8, cz=4, wh=2, wl=8, e=8)
                    ov = o4[:, :, lam >> 1, lam & 1, :, :].transpose([0, 2, 1, 3])
                    S.emit("vector", lambda e, o=ov, s=sl_: e.tensor_scalar(
                        o, l16d[:, s].rearrange("p (w d e) -> p w d e", d=8, e=8),
                        lnc[:, 0:1], lnc[:, 1:2], ALU.mult, ALU.add),
                        [S.at(sev)] + ([S.at(sdma)] if cc == 0 else []),
                        [S.inc(sev, 1)])
                dst = attnd.rearrange("c (u t) -> u c t", t=4096)[u0:u0 + L16S]
                dma("sync", sdma, dst, l16o[0:pr, :], [S.at(sev)])
            barrier()
            if t_attn is not None:
                dma("gpsimd", gdma, t_attn, attnd)
                barrier()

            ncv = 128 // MT
            p3s1 = A(B0, 2048)
            p3s2 = A(B0 + 2048, 2048)
            fm_rdy = []
            x1_rdy = []
            for rt in range(RT if p.stage >= 8 else 0):
                r0 = rt * 128
                bi = r0 // p.rpb
                ci = (r0 % p.rpb) // MT
                w_pm = []
                for j in range(ncv):
                    c = ci + j
                    asrc = attnd[c:c + 1, :].rearrange(
                        "o (b m w) -> o b m w", b=p.nbatch, w=DM)[0, bi]
                    w_pm.append(dma("sync", sdma,
                                    p3s1[j * MT:(j + 1) * MT, :], asrc,
                                    [S.at(sev)]))
                    lsrc = lz.rearrange("(b r) (c w) -> b r c w",
                                        b=p.nbatch, c=16)[bi, :, c, :].rearrange(
                        "(m h) w -> m h w", h=16)
                    w_pm.append(dma("scalar", tdma,
                                    p3s2[j * MT:(j + 1) * MT, :].rearrange(
                                        "p (h w) -> p h w", w=128), lsrc,
                                    [S.at(sev)]))
                w_x0 = dma("sync", sdma, rows[:], xperm[r0:r0 + 128, :],
                           [S.at(sev)])
                S.emit("vector", lambda e: e.tensor_add(
                    rows2[:], p3s1[:], p3s2[:]), w_pm, [S.inc(sev, 1)])
                w_rstd = ln_stats(rows2[:], rows3[:], [S.at(sev)])
                S.emit("vector", lambda e: e.tensor_scalar(
                    rows2[:], rows2[:], stat[:, 2:3], stat[:, 7:8],
                    ALU.subtract, ALU.mult), [w_rstd], [S.inc(sev, 1)])
                S.emit("vector", lambda e: e.tensor_add(
                    rows[:], rows[:], rows2[:]), [w_x0, S.at(sev)],
                    [S.inc(sev, 1)])
                w_x1 = S.at(sev)
                w_x1d = dma("sync", sdma, x1d[r0:r0 + 128, :], rows[:], [w_x1])
                if t_x1 is not None:
                    dma("sync", sdma, t_x1[r0:r0 + 128, :], rows[:], [w_x1])
                w_rstd = ln_stats(rows[:], rows3[:])
                S.emit("vector", lambda e: e.tensor_scalar(
                    rowsb[:], rows[:], stat[:, 2:3], stat[:, 7:8],
                    ALU.subtract, ALU.mult), [w_rstd, S.at(tdma)],
                    [S.inc(sev, 1)])
                w_ln2 = S.at(sev)
                for kc in range(16):
                    S.emit("scalar", lambda e, k=kc, r=rt: e.dma_start_transpose(
                        out=fmsb[:, k * NR + r * 128:k * NR + (r + 1) * 128],
                        in_=rowsb[:, k * 128:(k + 1) * 128]),
                        [w_ln2], [S.inc(tdma, 16)])
                x1_rdy.append(w_x1d)
                if rt % 4 == 3:
                    fm_rdy.append(S.at(tdma))
            w_p3t = S.at(tdma) if (RT if p.stage >= 8 else 0) else None
            barrier()
            if t_fm is not None:
                dma("gpsimd", gdma, t_fm, fmsb[:], [w_p3t])
                barrier()

            NTT = (NR // 512) if p.stage >= 9 else 0
            wgub = [wg0, wg1]
            wgi = [wgi0, wgi1]
            wdb = [wd0, wd1]
            wdi = [wdi0, wdi1]
            w_giwar = [[], []]
            w_gcast = [[], []]
            w_gmm = [[], []]
            w_wiwar = [[], []]
            w_wdwar = [[w_p3t], [w_p3t]] if p.stage >= 8 else [[], []]
            psg_par = [pb0, pb2]
            psu_par = [pb1, pb7]
            w_sil_par = [None, None]
            w_stt_par = [None, None]
            gc = {}

            def g_loadcast(tt, ic):
                par = ic % 2
                wla = dma("sync", sdma, wgi[par][:, 0:2048],
                          wgu_d[ic * 128:(ic + 1) * 128, 0:2048],
                          w_giwar[par])
                wlb = dma("scalar", tdma, wgi[par][:, 2048:4096],
                          wgu_d[ic * 128:(ic + 1) * 128, 2048:4096],
                          w_giwar[par])
                S.emit("vector", lambda e, b=par: e.tensor_copy(
                    wgub[b][:], wgi[b][:]), [wla, wlb] + w_gmm[par],
                    [S.inc(sev, 1)])
                w_giwar[par] = [S.at(sev)]
                gc[(tt, ic)] = S.at(sev)

            for tt in range(NTT):
                g_loadcast(tt, 0)
                g_loadcast(tt, 1)
                for ic in range(64):
                    par = ic % 2
                    pg, pu = psg_par[par], psu_par[par]
                    war = [w for w in (w_sil_par[par], w_stt_par[par]) if w]
                    for kc in range(16):
                        S.emit("tensor", lambda e, k=kc, b=par, t=tt, pp=pg: e.matmul(
                            pp[:, 0:512], wgub[b][:, k * 128:(k + 1) * 128],
                            fmsb[:, k * NR + t * 512:k * NR + (t + 1) * 512],
                            start=(k == 0), stop=(k == 15)),
                            ([gc[(tt, ic)]] + war +
                             ([fm_rdy[tt]] if ic == 0 else []))
                            if kc == 0 else [], [])
                        S.emit("tensor", lambda e, k=kc, b=par, t=tt, pp=pu: e.matmul(
                            pp[:, 0:512], wgub[b][:, 2048 + k * 128:2048 + (k + 1) * 128],
                            fmsb[:, k * NR + t * 512:k * NR + (t + 1) * 512],
                            start=(k == 0), stop=(k == 15)),
                            [], [S.inc(smm, 1)] if kc == 15 else [])
                    wmm = S.at(smm)
                    w_gmm[par] = [wmm]
                    S.emit("scalar", lambda e, i=ic, pp=pg: e.activation(
                        gsil[:], pp[:, 0:512], AF.Silu, scale=sg[:, i:i + 1]),
                        [wmm, S.at(sev)], [S.inc(ssc, 1)])
                    w_sil_par[par] = S.at(ssc)
                    if ic + 2 < 64:
                        g_loadcast(tt, ic + 2)
                    S.emit("vector", lambda e, i=ic, pp=pu: e.scalar_tensor_tensor(
                        fuse[:, i * 512:(i + 1) * 512], pp[:, 0:512],
                        su[:, i:i + 1], gsil[:], ALU.mult, ALU.mult),
                        [S.at(ssc)], [S.inc(sev, 1)])
                    w_stt_par[par] = S.at(sev)
                w_fuse = S.at(sev)
                for oc in range(4):
                    for icp in range(32):
                        par = icp % 2
                        wld = dma("sync" if par == 0 else "scalar",
                                  sdma if par == 0 else tdma, wdi[par][:],
                                  wdn_d[(oc * 32 + icp) * 128:
                                        (oc * 32 + icp + 1) * 128, :],
                                  w_wiwar[par])
                        S.emit("vector", lambda e, b=par: e.tensor_copy(
                            wdb[b][:], wdi[b][:]), [wld] + w_wdwar[par],
                            [S.inc(sev, 1)])
                        w_wiwar[par] = [S.at(sev)]
                        wcp = S.at(sev)
                        for half in range(2):
                            ic = icp * 2 + half
                            for tb in range(4):
                                S.emit("tensor", lambda e, i=ic, t=tb, b=par,
                                       h=half: e.matmul(
                                    psgrp[:, t * 512:(t + 1) * 512],
                                    fuse[:, i * 512 + t * 128:i * 512 + (t + 1) * 128],
                                    wdb[b][:, h * 512:(h + 1) * 512],
                                    start=(i == 0), stop=(i == 63)),
                                    ([wcp, w_fuse, S.at(sev)]
                                     if (icp == 0 and half == 0 and tb == 0) else
                                     ([wcp] if (half == 0 and tb == 0) else [])),
                                    [S.inc(smm, 1)] if (half == 1 and tb == 3)
                                    else [])
                        w_wdwar[par] = [S.at(smm)]
                    wmm = S.at(smm)
                    for tb in range(4):
                        S.emit("vector", lambda e, t=tb, o=oc: e.tensor_copy(
                            ffnsb[:, t * DM + o * 512:t * DM + (o + 1) * 512],
                            psgrp[:, t * 512:(t + 1) * 512]), [wmm],
                            [S.inc(sev, 1)])
                w_dn = S.at(sev)

                for tb in range(4):
                    rt = tt * 4 + tb
                    r0 = rt * 128
                    S.emit("vector", lambda e, t=tb: e.tensor_mul(
                        rows2[:], ffnsb[:, t * DM:(t + 1) * DM], sdn[:]),
                        [w_dn], [S.inc(sev, 1)])
                    w_xr = dma("gpsimd", gdma, rows[:], x1d[r0:r0 + 128, :],
                               [S.at(sev), x1_rdy[rt]])
                    w_rstd = ln_stats(rows2[:], rows3[:], [S.at(sev)])
                    S.emit("vector", lambda e: e.tensor_scalar(
                        rows2[:], rows2[:], stat[:, 2:3], stat[:, 7:8],
                        ALU.subtract, ALU.mult), [w_rstd], [S.inc(sev, 1)])
                    S.emit("vector", lambda e: e.tensor_add(
                        rows[:], rows[:], rows2[:]), [w_xr, S.at(sev)],
                        [S.inc(sev, 1)])
                    dma("gpsimd", gdma, out_d[r0:r0 + 128, :], rows[:],
                        [S.at(sev)])
            S.emit("sync", lambda e: e.nop(), [S.at(sdma)], [])
            S.emit("gpsimd", lambda e: e.nop(), [S.at(gdma)], [])
            S.emit("scalar", lambda e: e.nop(), [S.at(tdma)], [])

            @blk.sync
            def _(eng):
                S.run("sync", eng)
            @blk.gpsimd
            def _(eng):
                S.run("gpsimd", eng)
            @blk.tensor
            def _(eng):
                S.run("tensor", eng)
            @blk.vector
            def _(eng):
                S.run("vector", eng)
            @blk.scalar
            def _(eng):
                S.run("scalar", eng)
        return nc, tapd



def host_consts(inputs):
    qkv_w = np.asarray(inputs["L0_attn_qkv_w"], np.float32)
    proj_w = np.asarray(inputs["L0_attn_proj_w"], np.float32)
    qm, km = qk_maps(qkv_w)
    wgu, wdn = ffn_weight_tiles(np.asarray(inputs["L0_ffn_w_gate"]),
                                np.asarray(inputs["L0_ffn_w_up"]),
                                np.asarray(inputs["L0_ffn_w_down"]))
    return {
        "qmapw": qm, "kmapw": km, "vmapw": v_map(qkv_w),
        "auxin": aux_input_rows(),
        "identw": np.eye(128, dtype=np.float32),
        "lepmapw": lep_proj_map(proj_w),
        "projbdw": proj_bd8(proj_w),
        "bd16w": bd16_avg(),
        "cwcols": cw_cols(np.asarray(inputs["L0_attn_conv_w0"], np.float32),
                          np.asarray(inputs["L0_attn_conv_b0"], np.float32),
                          np.asarray(inputs["L0_attn_conv_w1"], np.float32),
                          np.asarray(inputs["L0_attn_conv_b1"], np.float32)),
        "lncols": ln_cols(np.asarray(inputs["L0_attn_ln_w"], np.float32),
                          np.asarray(inputs["L0_attn_ln_b"], np.float32),
                          np.asarray(inputs["L0_attn_proj_b"], np.float32)),
        "sgcol": np.ascontiguousarray(
            np.asarray(inputs["L0_ffn_s_gate"], np.float32).reshape(64, 128).T),
        "sucol": np.ascontiguousarray(
            np.asarray(inputs["L0_ffn_s_up"], np.float32).reshape(64, 128).T),
        "epscol": np.full((128, 1), EPS, np.float32),
        "sdnw": np.ascontiguousarray(np.broadcast_to(np.asarray(inputs["L0_ffn_s_down"], np.float32), (128, DM))),
        "wgu": wgu, "wdn": wdn,
    }


def perm_rows(k, nbatch=4, rpb=512):
    mt = rpb // 16
    ridx = np.empty(nbatch * rpb, np.int64)
    i = 0
    for b in range(nbatch):
        for c in range(16):
            for m in range(mt):
                ridx[i] = b * 4096 + 256 * c + mt * k + m
                i += 1
    return ridx


def kernel(**inputs):
    x = np.asarray(inputs["L0_attn_x"], np.float32)
    B, H, Wd = x.shape
    consts = host_consts(inputs)
    p = Prog(nrows=2048, nbatch=4)
    nc, _ = p.build()
    xf = x.reshape(B * H, Wd)
    in_maps = []
    for k in range(8):
        xin = np.ascontiguousarray(np.concatenate(
            [x[b, 512 * k:512 * (k + 1), :] for b in range(B)], axis=0))
        ridx = perm_rows(k)
        xperm = np.ascontiguousarray(xf[ridx])
        m = {"xin": xin, "xperm": xperm}
        m.update(consts)
        in_maps.append(m)
    res = run_bass_kernel_spmd(nc, in_maps, core_ids=list(range(8)))
    out = np.zeros((B * H, Wd), np.float32)
    for k in range(8):
        out[perm_rows(k)] = res.results[k]["out"]
    return out.reshape(B, H, Wd)


# version 13
# speedup vs baseline: 1.0575x; 1.0575x over previous
import contextlib
import numpy as np
import ml_dtypes
import concourse.bass as bass
import concourse.mybir as mybir
from concourse.bass_utils import run_bass_kernel_spmd

F32 = mybir.dt.float32
BF16 = mybir.dt.bfloat16
I8 = mybir.dt.int8
AF = mybir.ActivationFunctionType
ALU = mybir.AluOpType
AX = mybir.AxisListType
bfd = ml_dtypes.bfloat16

MB = 100.0
EPS = 1e-5
DM = 2048
INTER = 8192
SQK = 8.0 ** -0.5


def sigma_perm(n=4096):
    t = np.arange(n)
    ht, wb = t // 128, t % 128
    d, c0 = ht // 4, ht % 4
    wbb, e = wb // 8, wb % 8
    return ((c0 * 16 + wbb) * 8 + d) * 8 + e

SIG = sigma_perm()

def aux_input_rows():
    tss = SIG[np.arange(4096)]
    w1 = (tss >> 3) & 15
    w2 = (tss & 7) | (((tss >> 6) & 1) << 3)
    rows = [np.ones(4096)]
    rows += [(w1 >> k) & 1 for k in range(4)]
    rows += [(w2 >> k) & 1 for k in range(4)]
    return np.asarray(rows, dtype=np.float32)

def qk_maps(qkv_w):
    maps = []
    for which in (0, 1):
        m = np.zeros((64, 128), np.float32)
        wsub = qkv_w[:, which * 16:(which + 1) * 16]
        for s in range(2):
            r0, cc = 32 * s, 64 * s
            for br in range(2):
                cb = cc + 32 * br
                m[r0:r0 + 16, cb:cb + 8] = wsub[:, br * 8:br * 8 + 8]
                for k in range(4):
                    bitrow = r0 + 17 + 4 * br + k
                    onerow = r0 + 16
                    a, b_, c_ = cb + 8 + 3 * k, cb + 9 + 3 * k, cb + 10 + 3 * k
                    if which == 0:
                        m[bitrow, a] = -MB
                        m[onerow, b_] = -MB
                        m[bitrow, c_] = 2 * MB
                    else:
                        m[onerow, a] = 1.0
                        m[bitrow, b_] = 1.0
                        m[bitrow, c_] = 1.0
        maps.append(m)
    return maps

def v_map(qkv_w):
    m = np.zeros((64, 128), np.float32)
    wv = qkv_w[:, 32:48]
    for s in range(2):
        r0, cc = 32 * s, 32 * s
        m[r0:r0 + 16, cc:cc + 8] = wv[:, 0:8]
        m[r0 + 16, cc + 8] = 1.0
        m[r0:r0 + 16, cc + 9:cc + 17] = wv[:, 8:16]
    return m

def lep_proj_map(proj_w):
    m = np.zeros((128, 64), np.float32)
    for s in range(4):
        r0, cc = 32 * s, 16 * s
        m[r0:r0 + 8, cc:cc + 16] = proj_w[0:8, :]
        m[r0 + 9:r0 + 17, cc:cc + 16] = proj_w[8:16, :]
    return m

def proj_bd8(proj_w):
    m = np.zeros((128, 128), np.float32)
    for g in range(8):
        m[g * 16:(g + 1) * 16, g * 16:(g + 1) * 16] = proj_w
    return m

def bd16_avg():
    m = np.zeros((128, 128), np.float32)
    for g in range(8):
        m[g * 16:(g + 1) * 16, g * 16:(g + 1) * 16] = 1.0 / 16.0
    return m

def cw_cols(cw0, cb0, cw1, cb1):
    cols = np.zeros((128, 6), np.float32)
    for s in range(4):
        r0 = 32 * s
        for c in range(8):
            cols[r0 + c, 0] = cw0[c, 0, 1, 1]
            cols[r0 + c, 1] = cb0[c]
            cols[r0 + c, 2] = cw0[c, 0, 1, 0]
            cols[r0 + c, 3] = cw0[c, 0, 1, 2]
            cols[r0 + 9 + c, 0] = cw1[c, 0, 1, 1]
            cols[r0 + 9 + c, 1] = cb1[c]
            cols[r0 + 9 + c, 4] = cw1[c, 0, 0, 1]
            cols[r0 + 9 + c, 5] = cw1[c, 0, 2, 1]
    return cols

def ln_cols(ln_w, ln_b, proj_b):
    cols = np.zeros((128, 3), np.float32)
    for p_ in range(128):
        c = p_ % 16
        cols[p_] = (ln_w[c], ln_b[c], proj_b[c])
    return cols

def ffn_weight_tiles(wg, wu, wd):
    gu = np.zeros((64, 128, 2, 16, 128), np.int8)
    for ic in range(64):
        gu[ic, :, 0] = wg[:, ic * 128:(ic + 1) * 128].reshape(16, 128, 128).transpose(1, 0, 2)
        gu[ic, :, 1] = wu[:, ic * 128:(ic + 1) * 128].reshape(16, 128, 128).transpose(1, 0, 2)
    dn = np.zeros((4, 32, 128, 2, 512), np.int8)
    for oc in range(4):
        for ic in range(64):
            dn[oc, ic // 2, :, ic % 2] = wd[ic * 128:(ic + 1) * 128,
                                            oc * 512:(oc + 1) * 512]
    return (np.ascontiguousarray(gu.reshape(64 * 128, 4096)),
            np.ascontiguousarray(dn.reshape(4 * 32 * 128, 1024)))


class Sched:
    def __init__(self):
        self.ops = []
        self.counts = {}

    def emit(self, engine, fn, waits=(), incs=()):
        self.ops.append((engine, fn, [w for w in waits if w is not None], list(incs)))

    def inc(self, sem, amt):
        k = id(sem)
        self.counts[k] = self.counts.get(k, 0) + amt
        return (sem, amt)

    def at(self, sem):
        return (sem, self.counts.get(id(sem), 0))

    def run(self, engine, eng):
        for e, fn, waits, incs in self.ops:
            if e != engine:
                continue
            for sem, val in waits:
                if val > 0:
                    eng.wait_ge(sem, val)
            inst = fn(eng)
            for sem, amt in incs:
                inst.then_inc(sem, amt)


def apx(base, offset, dims):
    t = getattr(base, "tensor", base)
    fsz = int(np.prod(t.shape[1:]))
    np_ = int(t.shape[0])
    ap = [[fsz, np_]] + [list(d) for d in dims]
    return bass.AP(t, offset, ap)


class Prog:
    def __init__(self, nrows=2048, nbatch=4, taps=(), stage=9, rd=False):
        assert nrows % (32 * nbatch) == 0 and nrows % 512 == 0
        self.stage = stage
        self.rd = rd
        self.nrows = nrows
        self.nbatch = nbatch
        self.rpb = nrows // nbatch
        self.nsub = nrows // 32
        self.ntok = nrows * 128
        self.taps = set(taps)

    def build(self):
        p = self
        NR, NS, NT = p.nrows, p.nsub, p.ntok
        NSEG = NS // 4
        L16S = min(8, NS)
        RT = NR // 128
        MT = p.rpb // 16
        nc = bass.Bass("TRN2", target_bir_lowering=False, detect_race_conditions=self.rd)

        ein, eout = "ExternalInput", "ExternalOutput"
        xin = nc.dram_tensor("xin", [NR, DM], F32, kind=ein).ap()
        xperm = nc.dram_tensor("xperm", [NR, DM], F32, kind=ein).ap()
        qmap_d = nc.dram_tensor("qmapw", [64, 128], F32, kind=ein).ap()
        kmap_d = nc.dram_tensor("kmapw", [64, 128], F32, kind=ein).ap()
        vmap_d = nc.dram_tensor("vmapw", [64, 128], F32, kind=ein).ap()
        auxin_d = nc.dram_tensor("auxin", [9, 4096], F32, kind=ein).ap()
        ident_d = nc.dram_tensor("identw", [128, 128], F32, kind=ein).ap()
        lepmap_d = nc.dram_tensor("lepmapw", [128, 64], F32, kind=ein).ap()
        projbd_d = nc.dram_tensor("projbdw", [128, 128], F32, kind=ein).ap()
        bd16_d = nc.dram_tensor("bd16w", [128, 128], F32, kind=ein).ap()
        cw_d = nc.dram_tensor("cwcols", [128, 6], F32, kind=ein).ap()
        lnc_d = nc.dram_tensor("lncols", [128, 3], F32, kind=ein).ap()
        wgu_d = nc.dram_tensor("wgu", [64 * 128, 4096], I8, kind=ein).ap()
        wdn_d = nc.dram_tensor("wdn", [4 * 32 * 128, 1024], I8, kind=ein).ap()
        sg_d = nc.dram_tensor("sgcol", [128, 64], F32, kind=ein).ap()
        su_d = nc.dram_tensor("sucol", [128, 64], F32, kind=ein).ap()
        sdn_d = nc.dram_tensor("sdnw", [128, DM], F32, kind=ein).ap()
        eps_d = nc.dram_tensor("epscol", [128, 1], F32, kind=ein).ap()

        out_d = nc.dram_tensor("out", [NR, DM], F32, kind=eout).ap()

        lz = nc.dram_tensor("lz", [NR, DM], BF16, kind="Internal").ap()
        avp = nc.dram_tensor("avp", [16, NT], BF16, kind="Internal").ap()
        lepp = nc.dram_tensor("lepp", [16, NT], BF16, kind="Internal").ap()
        attnd = nc.dram_tensor("attnd", [16, NT], BF16, kind="Internal").ap()
        x1d = nc.dram_tensor("x1d", [NR, DM], F32, kind="Internal").ap()

        tapd = {}
        def mktap(name, shape, dtype=F32):
            if name in p.taps:
                tapd[name] = nc.dram_tensor("tap_" + name, shape, dtype, kind=eout).ap()
            return tapd.get(name)

        t_lz = mktap("lz", [NR, DM])
        t_qbuf = mktap("qbuf", [128, 4096])
        t_kbuf = mktap("kbuf", [128, 4096])
        t_vsb = mktap("vsb", [128, 4096])
        t_lep = mktap("lep", [128, 4096])
        t_asb = mktap("asb", [128, 2048])
        t_att1 = mktap("att1", [128, 128])
        t_avp = mktap("avp", [16, NT])
        t_lepp = mktap("lepp", [16, NT])
        t_attn = mktap("attn", [16, NT])
        t_x1 = mktap("x1", [NR, DM])
        t_fm = mktap("fm", [128, 16 * NR])

        S = Sched()
        FMW = 16 * NR

        with contextlib.ExitStack() as _st:
            _en = _st.enter_context
            qmap = _en(nc.sbuf_tensor("c_qmap", [64, 128], BF16))
            kmap = _en(nc.sbuf_tensor("c_kmap", [64, 128], BF16))
            vmap = _en(nc.sbuf_tensor("c_vmap", [64, 128], BF16))
            ident = _en(nc.sbuf_tensor("c_ident", [128, 128], BF16))
            lepmap = _en(nc.sbuf_tensor("c_lepmap", [128, 64], BF16))
            projbd = _en(nc.sbuf_tensor("c_projbd", [128, 128], BF16))
            bd16 = _en(nc.sbuf_tensor("c_bd16", [128, 128], BF16))
            cw = _en(nc.sbuf_tensor("c_cw", [128, 6], F32))
            lnc = _en(nc.sbuf_tensor("c_lnc", [128, 3], F32))
            epsc = _en(nc.sbuf_tensor("c_eps", [128, 1], F32))
            sg = _en(nc.sbuf_tensor("c_sg", [128, 64], F32))
            su = _en(nc.sbuf_tensor("c_su", [128, 64], F32))
            sdn = _en(nc.sbuf_tensor("c_sdn", [128, DM], BF16))
            af32 = _en(nc.sbuf_tensor("af32", [128, 6176], F32))
            ab = _en(nc.sbuf_tensor("ab", [128, 75776], BF16))
            gsil = _en(nc.sbuf_tensor("gsil", [128, 512], BF16))
            wdi0 = _en(nc.sbuf_tensor("wdi0", [128, 1024], I8))
            wdi1 = _en(nc.sbuf_tensor("wdi1", [128, 1024], I8))
            wgi0 = _en(nc.sbuf_tensor("wgi0", [128, 4096], I8))
            wgi1 = _en(nc.sbuf_tensor("wgi1", [128, 4096], I8))
            rsb = _en(nc.sbuf_tensor("rsb", [128, 16], F32))
            wg0 = _en(nc.sbuf_tensor("wg0", [128, 4096], BF16))
            wg1 = _en(nc.sbuf_tensor("wg1", [128, 4096], BF16))

            pb0 = _en(nc.psum_tensor("pb0", [128, 512], F32))
            pb1 = _en(nc.psum_tensor("pb1", [128, 512], F32))
            pb2 = _en(nc.psum_tensor("pb2", [128, 512], F32))
            psgrp = _en(nc.psum_tensor("psgrp", [128, 2048], F32))
            pb7 = _en(nc.psum_tensor("pb7", [128, 512], F32))
            sdma = _en(nc.semaphore("sdma"))
            gdma = _en(nc.semaphore("gdma"))
            tdma = _en(nc.semaphore("tdma"))
            smm = _en(nc.semaphore("smm"))
            sev = _en(nc.semaphore("sev"))
            ssc = _en(nc.semaphore("ssc"))
            sgp = _en(nc.semaphore("sgp"))
            wcv = _en(nc.semaphore("wcv"))
            sbar = _en(nc.semaphore("sbar"))
            blk = _en(nc.Block())

            def A(c0_, w):
                return ab[:, c0_:c0_ + w]
            fmsb = A(0, FMW)
            B0 = FMW
            fuse = A(B0, 32768)
            qb0 = A(B0, 4096); qb1 = A(B0 + 4096, 4096)
            kb0 = A(B0 + 8192, 4096); kb1 = A(B0 + 12288, 4096)
            vsb = A(B0 + 16384, 4096); lepb = A(B0 + 20480, 4096)
            vtsb = A(B0 + 24576, 4096)
            asb0 = A(B0 + 28672, 1024); asb1 = A(B0 + 29696, 1024)

            l16a = A(B0, 4096); l16w = A(B0 + 4096, 4096)
            l16d = A(B0 + 8192, 4096); l16o = A(B0 + 12288, 4096)
            C0 = B0 + 32768
            ffnsb = A(C0, 4 * DM)
            fla = ab[0:64, C0:C0 + 4096]
            flb = ab[0:64, C0 + 4096:C0 + 8192]
            rowsb = A(C0 + 8192, 2048)
            att1 = A(C0 + 8192, 128)
            cmsb = A(C0 + 8320, 128)
            avps = A(C0 + 8448, 128)
            lepps = A(C0 + 8576, 512)
            wd0 = A(C0 + 8192, 1024)
            wd1 = A(C0 + 9216, 1024)
            rows = af32[:, 0:2048]
            rows2 = af32[:, 2048:4096]
            rows3 = af32[:, 4096:6144]
            stat = af32[:, 6144:6176]
            l16r = af32[:, 0:4096]

            psq, psk, psv = pb0, pb1, pb2
            psav, psm, psm2 = pb0, pb2, pb7
            pstr = pb1[:, :].bitcast(BF16)

            nbar = [0]

            def barrier():
                nbar[0] += 1
                tgt = 5 * nbar[0]
                ws = {"sync": [S.at(sdma)], "gpsimd": [S.at(gdma)],
                      "scalar": [S.at(tdma)], "tensor": [], "vector": []}
                for en in ("sync", "gpsimd", "tensor", "vector", "scalar"):
                    S.emit(en, lambda e: e.nop(), ws[en], [S.inc(sbar, 1)])
                for en in ("sync", "gpsimd", "tensor", "vector", "scalar"):
                    S.emit(en, lambda e: e.nop(), [(sbar, tgt)], [])

            def dma(engine, sem, dst, src, waits=()):
                S.emit(engine, lambda e, d=dst, s=src: e.dma_start(out=d, in_=s),
                       waits, [S.inc(sem, 16)])
                return S.at(sem)

            for d, s_ in ((qmap, qmap_d), (kmap, kmap_d), (vmap, vmap_d),
                          (ident, ident_d), (lepmap, lepmap_d),
                          (projbd, projbd_d), (bd16, bd16_d),
                          (sdn, sdn_d)):
                dma("gpsimd", gdma, d[:], s_)
            for d, s_ in ((cw, cw_d), (lnc, lnc_d), (sg, sg_d), (su, su_d), (epsc, eps_d)):
                dma("sync", sdma, d[:], s_)
            S.emit("vector", lambda e: e.memset(fla[:], 0.0), [], [S.inc(sev, 1)])
            S.emit("vector", lambda e: e.memset(flb[:], 0.0), [], [S.inc(sev, 1)])
            mz = S.at(sev)
            for buf in (fla, flb):
                for st in (0, 32):
                    dma("gpsimd", gdma, buf[st + 16:st + 25, :], auxin_d, [mz])
            barrier()

            def ln_stats(src, junk, waits=()):
                waits = list(waits)
                S.emit("vector", lambda e: e.reduce_sum(stat[:, 0:1], src, AX.X),
                       waits + [S.at(ssc)], [S.inc(sev, 1)])
                S.emit("scalar", lambda e: e.activation(
                    junk, src, AF.Square, accum_out=stat[:, 1:2]),
                    waits + [S.at(sev)], [S.inc(ssc, 1)])
                S.emit("vector", lambda e: e.tensor_scalar(
                    stat[:, 2:4], stat[:, 0:2], 1.0 / DM, None, ALU.mult),
                    [S.at(ssc), S.at(sev)], [S.inc(sev, 1)])
                S.emit("vector", lambda e: e.tensor_tensor(
                    stat[:, 4:5], stat[:, 2:3], stat[:, 2:3], ALU.mult),
                    [S.at(sev)], [S.inc(sev, 1)])
                S.emit("vector", lambda e: e.tensor_sub(
                    stat[:, 5:6], stat[:, 3:4], stat[:, 4:5]), [S.at(sev)],
                    [S.inc(sev, 1)])
                S.emit("scalar", lambda e: e.activation(
                    stat[:, 6:7], stat[:, 5:6], AF.Ln, bias=epsc[:, 0:1]),
                    [S.at(sev)], [S.inc(ssc, 1)])
                S.emit("scalar", lambda e: e.activation(
                    stat[:, 7:8], stat[:, 6:7], AF.Exp, scale=-0.5), [S.at(ssc)],
                    [S.inc(ssc, 1)])
                return S.at(ssc)

            for rt in range(RT):
                w_ld = dma("sync", sdma, rows[:], xin[rt * 128:(rt + 1) * 128, :],
                           [S.at(sev)])
                w_rstd = ln_stats(rows[:], rows2[:], [w_ld])
                S.emit("vector", lambda e: e.tensor_scalar(
                    rowsb.rearrange("p (c w) -> p w c", c=16),
                    rows.rearrange("p (w c) -> p w c", c=16),
                    stat[:, 2:3], stat[:, 7:8], ALU.subtract, ALU.mult),
                    [w_rstd, S.at(sdma), S.at(gdma)], [S.inc(sev, 1)])
                dma("sync", sdma, lz[rt * 128:(rt + 1) * 128, :], rowsb[:],
                    [S.at(sev)])
                if t_lz is not None:
                    dma("gpsimd", gdma, t_lz[rt * 128:(rt + 1) * 128, :], rowsb[:],
                        [S.at(sev)])
            barrier()

            def lz_cAP(sub):
                return lz.rearrange("r (c w) -> c r w", c=16)[
                    :, sub * 32:(sub + 1) * 32, :]

            def sig_out(buf, cc):
                v = buf.rearrange("p (a b d e) -> p a b d e", a=4, b=16, d=8, e=8)
                return v[:, :, :, cc, :]

            w_qkv_mm_end = None
            w_exp_par = [None, None]
            for seg in range(NSEG if p.stage >= 2 else 0):
                sub0 = seg * 4
                wfl = []
                flw = [w_qkv_mm_end] if w_qkv_mm_end else []
                for ipair, buf in ((0, fla), (1, flb)):
                    for sl_ in range(2):
                        sub = sub0 + 2 * ipair + sl_
                        wfl.append(dma("gpsimd", gdma,
                                       buf[sl_ * 32:sl_ * 32 + 16, :], lz_cAP(sub),
                                       flw))
                psq_r = [psq[:, 0:512], psgrp[:, 0:512]]
                psk_r = [psk[:, 0:512], psgrp[:, 512:1024]]
                psv_r = [psv, psgrp[:, 1024:1536]]
                wq_war = [None, None]
                wk_war = [None, None]
                wv_war = [None, None]
                first_ev = True
                for ipair, buf, qb, kb in ((0, fla, qb0, kb0), (1, flb, qb1, kb1)):
                    for cc in range(8):
                        par = cc % 2
                        gcc = ipair * 8 + cc
                        sl_ = slice(cc * 512, (cc + 1) * 512)
                        qw = [wfl[2 * ipair], wfl[2 * ipair + 1]]
                        if gcc == 0:
                            qw.append(S.at(sev))
                        elif gcc == 1:
                            qw += [w_exp_par[0]] if w_exp_par[0] else []
                        qw += wq_war[par] or []
                        S.emit("tensor", lambda e, b=buf, s=sl_, pp=psq_r[par]:
                               e.matmul(pp, qmap[:], b[:, s],
                                        start=True, stop=True),
                               qw, [S.inc(smm, 1)])
                        S.emit("tensor", lambda e, b=buf, s=sl_, pp=psk_r[par]:
                               e.matmul(pp, kmap[:], b[:, s],
                                        start=True, stop=True),
                               wk_war[par] or [], [S.inc(smm, 1)])
                        wmm = S.at(smm)
                        evw = [wmm, S.at(smm)] if first_ev else [wmm]
                        first_ev = False
                        S.emit("vector", lambda e, d_=qb, c=cc, pp=psq_r[par]:
                               e.tensor_copy(sig_out(d_, c), pp), evw,
                               [S.inc(sev, 1)])
                        wq_war[par] = [S.at(sev)]
                        S.emit("vector", lambda e, d_=kb, c=cc, pp=psk_r[par]:
                               e.tensor_copy(sig_out(d_, c), pp), [],
                               [S.inc(sev, 1)])
                        wk_war[par] = [S.at(sev)]
                for cc in range(8):
                    par = cc % 2
                    sl_ = slice(cc * 512, (cc + 1) * 512)
                    vw = []
                    if cc == 0:
                        vw.append(S.at(sev))
                    elif cc == 1:
                        vw += [w_exp_par[1]] if w_exp_par[1] else []
                    vw += wv_war[par] or []
                    S.emit("tensor", lambda e, s=sl_, pp=psv_r[par]: e.matmul(
                        pp[0:64, 0:512], vmap[0:64, 0:64], fla[:, s],
                        start=True, stop=True), vw, [S.inc(smm, 1)])
                    S.emit("tensor", lambda e, s=sl_, pp=psv_r[par]: e.matmul(
                        pp[64:128, 0:512], vmap[0:64, 0:64], flb[:, s],
                        start=True, stop=True, tile_position=(0, 64)),
                        [], [S.inc(smm, 1)])
                    S.emit("vector", lambda e, c=cc, pp=psv_r[par]: e.tensor_copy(
                        sig_out(vsb, c), pp[:, 0:512]), [S.at(smm)],
                        [S.inc(sev, 1)])
                    wv_war[par] = [S.at(sev)]
                w_qkv_mm_end = S.at(smm)
                w_qkv = S.at(sev)
                w_psgrp_extra = {0: [w_qkv], 1: [w_qkv]}
                if t_qbuf is not None and seg == 0:
                    dma("gpsimd", gdma, t_qbuf, qb0[:], [w_qkv])
                    dma("gpsimd", gdma, t_kbuf, kb0[:], [w_qkv])
                    dma("gpsimd", gdma, t_vsb, vsb[:], [w_qkv])

                for b4 in range(8):
                    for j in range(4):
                        beta = b4 * 4 + j
                        S.emit("tensor", lambda e, b=beta, jj=j: e.transpose(
                            pstr[:, jj * 128:(jj + 1) * 128],
                            vsb[:, b * 128:(b + 1) * 128], ident[:]),
                            [w_qkv] + ([S.at(sev)] if j == 0 else []),
                            [S.inc(smm, 1)])
                    S.emit("vector", lambda e, b=b4: e.tensor_copy(
                        vtsb[:, b * 512:(b + 1) * 512], pstr[:, 0:512]),
                        [S.at(smm)], [S.inc(sev, 1)])
                w_vt = S.at(sev)
                S.emit("vector", lambda e: e.tensor_scalar(
                    lepb[:], vsb[:], cw[:, 0:1], cw[:, 1:2], ALU.mult, ALU.add),
                    [w_qkv, S.at(smm)], [S.inc(sev, 1)])
                vv = vsb.rearrange("p (w e) -> p w e", e=8)
                ll = lepb.rearrange("p (w e) -> p w e", e=8)
                S.emit("vector", lambda e: e.scalar_tensor_tensor(
                    ll[:, :, 1:8], vv[:, :, 0:7], cw[:, 2:3], ll[:, :, 1:8],
                    ALU.mult, ALU.add), [S.at(sev)], [S.inc(sev, 1)])
                S.emit("vector", lambda e: e.scalar_tensor_tensor(
                    ll[:, :, 0:7], vv[:, :, 1:8], cw[:, 3:4], ll[:, :, 0:7],
                    ALU.mult, ALU.add), [S.at(sev)], [S.inc(sev, 1)])
                vv8 = vsb.rearrange("p (b d e) -> p b d e", d=8, e=8)
                ll8 = lepb.rearrange("p (b d e) -> p b d e", d=8, e=8)
                S.emit("vector", lambda e: e.scalar_tensor_tensor(
                    ll8[:, :, 1:8, :], vv8[:, :, 0:7, :], cw[:, 4:5],
                    ll8[:, :, 1:8, :], ALU.mult, ALU.add), [S.at(sev)],
                    [S.inc(sev, 1)])
                S.emit("vector", lambda e: e.scalar_tensor_tensor(
                    ll8[:, :, 0:7, :], vv8[:, :, 1:8, :], cw[:, 5:6],
                    ll8[:, :, 0:7, :], ALU.mult, ALU.add), [S.at(sev)],
                    [S.inc(sev, 1)])
                w_lep = S.at(sev)
                if t_lep is not None and seg == 0:
                    dma("gpsimd", gdma, t_lep, lepb[:], [w_lep])

                for s4 in range(4 if p.stage >= 3 else 0):
                    sub = sub0 + s4
                    qb, kb = (qb0, kb0) if s4 < 2 else (qb1, kb1)
                    st1 = (s4 % 2) * 64
                    st2 = st1 + 32

                    def emit_grams(bg):
                        half = bg % 2
                        hof = half * 1024
                        a_sb = asb0 if half == 0 else asb1
                        w_war = [w_exp_par[half]] if w_exp_par[half] else []
                        w_war += w_psgrp_extra.pop(half, [])
                        for j in range(4):
                            beta = bg * 4 + j
                            csl = slice(beta * 128, (beta + 1) * 128)
                            for br, st in ((0, st1), (1, st2)):
                                u = br * 4 + j
                                S.emit("tensor", lambda e, k_=kb, q_=qb, s=csl,
                                       r0=st, uu=u, ho=hof: e.matmul(
                                    psgrp[:, ho + uu * 128:ho + (uu + 1) * 128],
                                    k_[r0:r0 + 20, s], q_[r0:r0 + 20, s],
                                    start=True, stop=True, tile_position=(r0, 0)),
                                    w_war if (j == 0 and br == 0) else [],
                                    [S.inc(smm, 1)])
                        S.emit("scalar", lambda e, ho=hof, ab_=a_sb: e.activation(
                            ab_[:, 0:1024], psgrp[:, ho:ho + 1024], AF.Exp,
                            scale=SQK), [S.at(smm)], [S.inc(ssc, 1)])
                        w_exp_par[half] = S.at(ssc)
                        return (bg, S.at(ssc), a_sb)

                    def emit_av(info):
                        bg, w_exp, a_sb = info
                        if t_asb is not None and seg == 0 and s4 == 0 and bg < 2:
                            dma("gpsimd", gdma, t_asb[:, bg * 1024:(bg + 1) * 1024],
                                a_sb[:, 0:1024], [w_exp])
                        for j in range(4):
                            beta = bg * 4 + j
                            jj = (bg % 2) * 4 + j
                            for br in range(2):
                                u = br * 4 + j
                                r0_ = beta * 128 + s4 * 32 + br * 8
                                S.emit("tensor", lambda e, uu=u, jx=jj, b_=br,
                                       r=r0_, ab_=a_sb: e.matmul(
                                    psav[:, jx * 18 + b_ * 9:jx * 18 + b_ * 9 + 9],
                                    ab_[:, uu * 128:(uu + 1) * 128],
                                    vtsb[:, r:r + 9], start=True, stop=True),
                                    [w_exp, w_vt, S.at(sev)]
                                    if (j == 0 and br == 0) else [],
                                    [S.inc(smm, 1)])

                    def emit_pairpost(bgodd):
                        w_av = S.at(smm)
                        S.emit("vector", lambda e: e.reciprocal(
                            rsb[:, 0:16].rearrange("p (a b) -> p a b", b=2),
                            apx(psav, 8, [[18, 8], [1, 2]])), [w_av],
                            [S.inc(sev, 1)])
                        S.emit("vector", lambda e: e.tensor_tensor(
                            att1.rearrange("p (j b c) -> p j b c", b=2, c=8),
                            apx(psav, 0, [[18, 8], [10, 2], [1, 8]]),
                            apx(rsb, 0, [[2, 8], [1, 2], [0, 8]]),
                            ALU.mult), [S.at(smm), S.at(sev)], [S.inc(sev, 1)])
                        w_n = S.at(sev)
                        if t_att1 is not None and seg == 0 and s4 == 0 and bgodd == 1:
                            dma("gpsimd", gdma, t_att1, att1[:], [w_n])
                        S.emit("tensor", lambda e: e.transpose(
                            pstr[:, 512:640], att1[:], ident[:]),
                            [w_n], [S.inc(smm, 1)])
                        S.emit("vector", lambda e: e.tensor_copy(
                            cmsb[:], pstr[:, 512:640]), [S.at(smm)],
                            [S.inc(sev, 1)])
                        S.emit("tensor", lambda e: e.matmul(
                            psm2[:, 0:128], projbd[:], cmsb[:],
                            start=True, stop=True), [S.at(sev)], [S.inc(smm, 1)])
                        S.emit("vector", lambda e: e.tensor_copy(
                            avps[:], psm2[:, 0:128]), [S.at(smm), S.at(sdma)],
                            [S.inc(sev, 1)])
                        bgp = bgodd // 2
                        dst = avp.rearrange("c (u b t) -> u b c t", u=NS, t=128)[
                            sub, bgp * 8:(bgp + 1) * 8, :, :]
                        dma("sync", sdma, dst, avps[:], [S.at(sev)])

                    pend = None
                    for bg in range(8):
                        info = emit_grams(bg)
                        if pend is not None:
                            emit_av(pend)
                            if pend[0] % 2 == 1:
                                emit_pairpost(pend[0])
                        pend = info
                    emit_av(pend)
                    emit_pairpost(7)
                for cc in range(8):
                    sl_ = slice(cc * 512, (cc + 1) * 512)
                    S.emit("tensor", lambda e, s=sl_: e.matmul(
                        psm[0:64, 0:512], lepmap[:], lepb[:, s], start=True,
                        stop=True), [w_lep, w_qkv, S.at(sev)], [S.inc(smm, 1)])
                    S.emit("vector", lambda e: e.tensor_copy(
                        lepps[0:64, :], psm[0:64, 0:512]), [S.at(smm), S.at(sdma)],
                        [S.inc(sev, 1)])
                    dst = lepp.rearrange("c (u t) -> u c t", t=4096)[
                        sub0:sub0 + 4, :, cc * 512:(cc + 1) * 512]
                    dma("sync", sdma, dst, lepps[0:64, :], [S.at(sev)])


            barrier()

            if t_avp is not None:
                dma("gpsimd", gdma, t_avp, avp)
            if t_lepp is not None:
                dma("gpsimd", gdma, t_lepp, lepp)

            for g in range(NS // L16S if p.stage >= 7 else 0):
                u0 = g * L16S
                pr = L16S * 16
                srcA = avp.rearrange("c (u t) -> u c t", t=4096)[u0:u0 + L16S]
                srcB = lepp.rearrange("c (u t) -> u c t", t=4096)[u0:u0 + L16S]
                wa = dma("gpsimd", gdma, l16a[0:pr, :], srcA, [S.at(sev)])
                wb_ = dma("gpsimd", gdma, l16d[0:pr, :], srcB, [S.at(sev)])
                S.emit("vector", lambda e: e.scalar_tensor_tensor(
                    l16w[0:pr, :], l16a[0:pr, :], lnc[0:pr, 2:3], l16d[0:pr, :],
                    ALU.add, ALU.add), [wa, wb_], [S.inc(sev, 1)])
                w_w = S.at(sev)
                for cc in range(8):
                    sl_ = slice(cc * 512, (cc + 1) * 512)
                    S.emit("tensor", lambda e, s=sl_: e.matmul(
                        psm[:, 0:512], bd16[:], l16w[:, s], start=True, stop=True),
                        [w_w, S.at(sev)], [S.inc(smm, 1)])
                    S.emit("vector", lambda e, s=sl_: e.tensor_sub(
                        l16d[:, s], l16w[:, s], psm[:, 0:512]), [S.at(smm)],
                        [S.inc(sev, 1)])
                    S.emit("scalar", lambda e, s=sl_: e.activation(
                        l16a[:, s], l16d[:, s], AF.Square), [S.at(sev)],
                        [S.inc(ssc, 1)])
                    S.emit("tensor", lambda e, s=sl_: e.matmul(
                        psm2[:, 0:512], bd16[:], l16a[:, s], start=True,
                        stop=True), [S.at(ssc)], [S.inc(smm, 1)])
                    S.emit("scalar", lambda e, s=sl_: e.activation(
                        l16r[:, s], psm2[:, 0:512], AF.Ln, bias=epsc[:, 0:1]),
                        [S.at(smm)], [S.inc(ssc, 1)])
                    S.emit("scalar", lambda e, s=sl_: e.activation(
                        l16r[:, s], l16r[:, s], AF.Exp, scale=-0.5), [S.at(ssc)],
                        [S.inc(ssc, 1)])
                    S.emit("vector", lambda e, s=sl_: e.tensor_mul(
                        l16d[:, s], l16d[:, s], l16r[:, s]), [S.at(ssc)],
                        [S.inc(sev, 1)])
                    lam = cc
                    o4 = l16o.rearrange("p (d cz wh wl e) -> p d cz wh wl e",
                                        d=8, cz=4, wh=2, wl=8, e=8)
                    ov = o4[:, :, lam >> 1, lam & 1, :, :].transpose([0, 2, 1, 3])
                    S.emit("vector", lambda e, o=ov, s=sl_: e.tensor_scalar(
                        o, l16d[:, s].rearrange("p (w d e) -> p w d e", d=8, e=8),
                        lnc[:, 0:1], lnc[:, 1:2], ALU.mult, ALU.add),
                        [S.at(sev)] + ([S.at(sdma)] if cc == 0 else []),
                        [S.inc(sev, 1)])
                dst = attnd.rearrange("c (u t) -> u c t", t=4096)[u0:u0 + L16S]
                dma("sync", sdma, dst, l16o[0:pr, :], [S.at(sev)])
            barrier()
            if t_attn is not None:
                dma("gpsimd", gdma, t_attn, attnd)
                barrier()

            ncv = 128 // MT
            p3s1 = A(B0, 2048)
            p3s2 = A(B0 + 2048, 2048)
            fm_rdy = []
            x1_rdy = []
            for rt in range(RT if p.stage >= 8 else 0):
                r0 = rt * 128
                bi = r0 // p.rpb
                ci = (r0 % p.rpb) // MT
                w_pm = []
                for j in range(ncv):
                    c = ci + j
                    asrc = attnd[c:c + 1, :].rearrange(
                        "o (b m w) -> o b m w", b=p.nbatch, w=DM)[0, bi]
                    w_pm.append(dma("sync", sdma,
                                    p3s1[j * MT:(j + 1) * MT, :], asrc,
                                    [S.at(sev)]))
                    lsrc = lz.rearrange("(b r) (c w) -> b r c w",
                                        b=p.nbatch, c=16)[bi, :, c, :].rearrange(
                        "(m h) w -> m h w", h=16)
                    w_pm.append(dma("scalar", tdma,
                                    p3s2[j * MT:(j + 1) * MT, :].rearrange(
                                        "p (h w) -> p h w", w=128), lsrc,
                                    [S.at(sev)]))
                w_x0 = dma("sync", sdma, rows[:], xperm[r0:r0 + 128, :],
                           [S.at(sev)])
                S.emit("vector", lambda e: e.tensor_add(
                    rows2[:], p3s1[:], p3s2[:]), w_pm, [S.inc(sev, 1)])
                w_rstd = ln_stats(rows2[:], rows3[:], [S.at(sev)])
                S.emit("vector", lambda e: e.tensor_scalar(
                    rows2[:], rows2[:], stat[:, 2:3], stat[:, 7:8],
                    ALU.subtract, ALU.mult), [w_rstd], [S.inc(sev, 1)])
                S.emit("vector", lambda e: e.tensor_add(
                    rows[:], rows[:], rows2[:]), [w_x0, S.at(sev)],
                    [S.inc(sev, 1)])
                w_x1 = S.at(sev)
                w_x1d = dma("sync", sdma, x1d[r0:r0 + 128, :], rows[:], [w_x1])
                if t_x1 is not None:
                    dma("sync", sdma, t_x1[r0:r0 + 128, :], rows[:], [w_x1])
                w_rstd = ln_stats(rows[:], rows3[:])
                S.emit("vector", lambda e: e.tensor_scalar(
                    rowsb[:], rows[:], stat[:, 2:3], stat[:, 7:8],
                    ALU.subtract, ALU.mult), [w_rstd, S.at(tdma)],
                    [S.inc(sev, 1)])
                w_ln2 = S.at(sev)
                for kc in range(16):
                    S.emit("scalar", lambda e, k=kc, r=rt: e.dma_start_transpose(
                        out=fmsb[:, k * NR + r * 128:k * NR + (r + 1) * 128],
                        in_=rowsb[:, k * 128:(k + 1) * 128]),
                        [w_ln2], [S.inc(tdma, 16)])
                x1_rdy.append(w_x1d)
                if rt % 4 == 3:
                    fm_rdy.append(S.at(tdma))
            w_p3t = S.at(tdma) if (RT if p.stage >= 8 else 0) else None
            barrier()
            if t_fm is not None:
                dma("gpsimd", gdma, t_fm, fmsb[:], [w_p3t])
                barrier()

            NTT = (NR // 512) if p.stage >= 9 else 0
            wgub = [wg0, wg1]
            wgi = [wgi0, wgi1]
            wdb = [wd0, wd1]
            wdi = [wdi0, wdi1]
            w_giwar = [[], []]
            w_gcast = [[], []]
            w_gmm = [[], []]
            w_wiwar = [[], []]
            w_wdwar = [[w_p3t], [w_p3t]] if p.stage >= 8 else [[], []]
            psg_par = [pb0, pb2]
            psu_par = [pb1, pb7]
            w_sil_par = [None, None]
            w_stt_par = [None, None]
            gc = {}

            def g_loadcast(tt, ic):
                par = ic % 2
                wla = dma("sync", sdma, wgi[par][:, 0:2048],
                          wgu_d[ic * 128:(ic + 1) * 128, 0:2048],
                          w_giwar[par])
                wlb = dma("scalar", tdma, wgi[par][:, 2048:4096],
                          wgu_d[ic * 128:(ic + 1) * 128, 2048:4096],
                          w_giwar[par])
                S.emit("vector", lambda e, b=par: e.tensor_copy(
                    wgub[b][:], wgi[b][:]), [wla, wlb] + w_gmm[par],
                    [S.inc(sev, 1)])
                w_giwar[par] = [S.at(sev)]
                gc[(tt, ic)] = S.at(sev)

            for tt in range(NTT):
                g_loadcast(tt, 0)
                g_loadcast(tt, 1)
                for ic in range(64):
                    par = ic % 2
                    pg, pu = psg_par[par], psu_par[par]
                    war = [w for w in (w_sil_par[par], w_stt_par[par]) if w]
                    for kc in range(16):
                        S.emit("tensor", lambda e, k=kc, b=par, t=tt, pp=pg: e.matmul(
                            pp[:, 0:512], wgub[b][:, k * 128:(k + 1) * 128],
                            fmsb[:, k * NR + t * 512:k * NR + (t + 1) * 512],
                            start=(k == 0), stop=(k == 15)),
                            ([gc[(tt, ic)]] + war +
                             ([fm_rdy[tt]] if ic == 0 else []))
                            if kc == 0 else [], [])
                        S.emit("tensor", lambda e, k=kc, b=par, t=tt, pp=pu: e.matmul(
                            pp[:, 0:512], wgub[b][:, 2048 + k * 128:2048 + (k + 1) * 128],
                            fmsb[:, k * NR + t * 512:k * NR + (t + 1) * 512],
                            start=(k == 0), stop=(k == 15)),
                            [], [S.inc(smm, 1)] if kc == 15 else [])
                    wmm = S.at(smm)
                    w_gmm[par] = [wmm]
                    S.emit("scalar", lambda e, i=ic, pp=pg: e.activation(
                        gsil[:], pp[:, 0:512], AF.Silu, scale=sg[:, i:i + 1]),
                        [wmm, S.at(sev)], [S.inc(ssc, 1)])
                    w_sil_par[par] = S.at(ssc)
                    if ic + 2 < 64:
                        g_loadcast(tt, ic + 2)
                    S.emit("vector", lambda e, i=ic, pp=pu: e.scalar_tensor_tensor(
                        fuse[:, i * 512:(i + 1) * 512], pp[:, 0:512],
                        su[:, i:i + 1], gsil[:], ALU.mult, ALU.mult),
                        [S.at(ssc)], [S.inc(sev, 1)])
                    w_stt_par[par] = S.at(sev)
                w_fuse = S.at(sev)
                for oc in range(4):
                    for icp in range(32):
                        par = icp % 2
                        wld = dma("sync" if par == 0 else "scalar",
                                  sdma if par == 0 else tdma, wdi[par][:],
                                  wdn_d[(oc * 32 + icp) * 128:
                                        (oc * 32 + icp + 1) * 128, :],
                                  w_wiwar[par])
                        S.emit("vector", lambda e, b=par: e.tensor_copy(
                            wdb[b][:], wdi[b][:]), [wld] + w_wdwar[par],
                            [S.inc(sev, 1)])
                        w_wiwar[par] = [S.at(sev)]
                        wcp = S.at(sev)
                        for half in range(2):
                            ic = icp * 2 + half
                            for tb in range(4):
                                S.emit("tensor", lambda e, i=ic, t=tb, b=par,
                                       h=half: e.matmul(
                                    psgrp[:, t * 512:(t + 1) * 512],
                                    fuse[:, i * 512 + t * 128:i * 512 + (t + 1) * 128],
                                    wdb[b][:, h * 512:(h + 1) * 512],
                                    start=(i == 0), stop=(i == 63)),
                                    ([wcp, w_fuse, S.at(sev)]
                                     if (icp == 0 and half == 0 and tb == 0) else
                                     ([wcp] if (half == 0 and tb == 0) else [])),
                                    [S.inc(smm, 1)] if (half == 1 and tb == 3)
                                    else [])
                        w_wdwar[par] = [S.at(smm)]
                    wmm = S.at(smm)
                    for tb in range(4):
                        S.emit("vector", lambda e, t=tb, o=oc: e.tensor_copy(
                            ffnsb[:, t * DM + o * 512:t * DM + (o + 1) * 512],
                            psgrp[:, t * 512:(t + 1) * 512]), [wmm],
                            [S.inc(sev, 1)])
                w_dn = S.at(sev)

                for tb in range(4):
                    rt = tt * 4 + tb
                    r0 = rt * 128
                    S.emit("vector", lambda e, t=tb: e.tensor_mul(
                        rows2[:], ffnsb[:, t * DM:(t + 1) * DM], sdn[:]),
                        [w_dn], [S.inc(sev, 1)])
                    w_xr = dma("gpsimd", gdma, rows[:], x1d[r0:r0 + 128, :],
                               [S.at(sev), x1_rdy[rt]])
                    w_rstd = ln_stats(rows2[:], rows3[:], [S.at(sev)])
                    S.emit("vector", lambda e: e.tensor_scalar(
                        rows2[:], rows2[:], stat[:, 2:3], stat[:, 7:8],
                        ALU.subtract, ALU.mult), [w_rstd], [S.inc(sev, 1)])
                    S.emit("vector", lambda e: e.tensor_add(
                        rows[:], rows[:], rows2[:]), [w_xr, S.at(sev)],
                        [S.inc(sev, 1)])
                    dma("gpsimd", gdma, out_d[r0:r0 + 128, :], rows[:],
                        [S.at(sev)])
            S.emit("sync", lambda e: e.nop(), [S.at(sdma)], [])
            S.emit("gpsimd", lambda e: e.nop(), [S.at(gdma)], [])
            S.emit("scalar", lambda e: e.nop(), [S.at(tdma)], [])

            @blk.sync
            def _(eng):
                S.run("sync", eng)
            @blk.gpsimd
            def _(eng):
                S.run("gpsimd", eng)
            @blk.tensor
            def _(eng):
                S.run("tensor", eng)
            @blk.vector
            def _(eng):
                S.run("vector", eng)
            @blk.scalar
            def _(eng):
                S.run("scalar", eng)
        return nc, tapd



def host_consts(inputs):
    qkv_w = np.asarray(inputs["L0_attn_qkv_w"], np.float32)
    proj_w = np.asarray(inputs["L0_attn_proj_w"], np.float32)
    qm, km = qk_maps(qkv_w)
    wgu, wdn = ffn_weight_tiles(np.asarray(inputs["L0_ffn_w_gate"]),
                                np.asarray(inputs["L0_ffn_w_up"]),
                                np.asarray(inputs["L0_ffn_w_down"]))
    return {
        "qmapw": qm, "kmapw": km, "vmapw": v_map(qkv_w),
        "auxin": aux_input_rows(),
        "identw": np.eye(128, dtype=np.float32),
        "lepmapw": lep_proj_map(proj_w),
        "projbdw": proj_bd8(proj_w),
        "bd16w": bd16_avg(),
        "cwcols": cw_cols(np.asarray(inputs["L0_attn_conv_w0"], np.float32),
                          np.asarray(inputs["L0_attn_conv_b0"], np.float32),
                          np.asarray(inputs["L0_attn_conv_w1"], np.float32),
                          np.asarray(inputs["L0_attn_conv_b1"], np.float32)),
        "lncols": ln_cols(np.asarray(inputs["L0_attn_ln_w"], np.float32),
                          np.asarray(inputs["L0_attn_ln_b"], np.float32),
                          np.asarray(inputs["L0_attn_proj_b"], np.float32)),
        "sgcol": np.ascontiguousarray(
            np.asarray(inputs["L0_ffn_s_gate"], np.float32).reshape(64, 128).T),
        "sucol": np.ascontiguousarray(
            np.asarray(inputs["L0_ffn_s_up"], np.float32).reshape(64, 128).T),
        "epscol": np.full((128, 1), EPS, np.float32),
        "sdnw": np.ascontiguousarray(np.broadcast_to(np.asarray(inputs["L0_ffn_s_down"], np.float32), (128, DM))),
        "wgu": wgu, "wdn": wdn,
    }


def perm_rows(k, nbatch=4, rpb=512):
    mt = rpb // 16
    ridx = np.empty(nbatch * rpb, np.int64)
    i = 0
    for b in range(nbatch):
        for c in range(16):
            for m in range(mt):
                ridx[i] = b * 4096 + 256 * c + mt * k + m
                i += 1
    return ridx


def kernel(**inputs):
    x = np.asarray(inputs["L0_attn_x"], np.float32)
    B, H, Wd = x.shape
    consts = host_consts(inputs)
    p = Prog(nrows=2048, nbatch=4)
    nc, _ = p.build()
    xf = x.reshape(B * H, Wd)
    in_maps = []
    for k in range(8):
        xin = np.ascontiguousarray(np.concatenate(
            [x[b, 512 * k:512 * (k + 1), :] for b in range(B)], axis=0))
        ridx = perm_rows(k)
        xperm = np.ascontiguousarray(xf[ridx])
        m = {"xin": xin, "xperm": xperm}
        m.update(consts)
        in_maps.append(m)
    res = run_bass_kernel_spmd(nc, in_maps, core_ids=list(range(8)))
    out = np.zeros((B * H, Wd), np.float32)
    for k in range(8):
        out[perm_rows(k)] = res.results[k]["out"]
    return out.reshape(B, H, Wd)


# version 14
# speedup vs baseline: 1.0871x; 1.0281x over previous
import contextlib
import numpy as np
import ml_dtypes
import concourse.bass as bass
import concourse.mybir as mybir
from concourse.bass_utils import run_bass_kernel_spmd

F32 = mybir.dt.float32
BF16 = mybir.dt.bfloat16
I8 = mybir.dt.int8
AF = mybir.ActivationFunctionType
ALU = mybir.AluOpType
AX = mybir.AxisListType
bfd = ml_dtypes.bfloat16

MB = 100.0
EPS = 1e-5
DM = 2048
INTER = 8192
SQK = 8.0 ** -0.5


def sigma_perm(n=4096):
    t = np.arange(n)
    ht, wb = t // 128, t % 128
    d, c0 = ht // 4, ht % 4
    wbb, e = wb // 8, wb % 8
    return ((c0 * 16 + wbb) * 8 + d) * 8 + e

SIG = sigma_perm()

def aux_input_rows():
    tss = SIG[np.arange(4096)]
    w1 = (tss >> 3) & 15
    w2 = (tss & 7) | (((tss >> 6) & 1) << 3)
    rows = [np.ones(4096)]
    rows += [(w1 >> k) & 1 for k in range(4)]
    rows += [(w2 >> k) & 1 for k in range(4)]
    return np.asarray(rows, dtype=np.float32)

def qk_maps(qkv_w):
    maps = []
    for which in (0, 1):
        m = np.zeros((64, 128), np.float32)
        wsub = qkv_w[:, which * 16:(which + 1) * 16]
        for s in range(2):
            r0, cc = 32 * s, 64 * s
            for br in range(2):
                cb = cc + 32 * br
                m[r0:r0 + 16, cb:cb + 8] = wsub[:, br * 8:br * 8 + 8]
                for k in range(4):
                    bitrow = r0 + 17 + 4 * br + k
                    onerow = r0 + 16
                    a, b_, c_ = cb + 8 + 3 * k, cb + 9 + 3 * k, cb + 10 + 3 * k
                    if which == 0:
                        m[bitrow, a] = -MB
                        m[onerow, b_] = -MB
                        m[bitrow, c_] = 2 * MB
                    else:
                        m[onerow, a] = 1.0
                        m[bitrow, b_] = 1.0
                        m[bitrow, c_] = 1.0
        maps.append(m)
    return maps

def v_map(qkv_w):
    m = np.zeros((64, 128), np.float32)
    wv = qkv_w[:, 32:48]
    for s in range(2):
        r0, cc = 32 * s, 32 * s
        m[r0:r0 + 16, cc:cc + 8] = wv[:, 0:8]
        m[r0 + 16, cc + 8] = 1.0
        m[r0:r0 + 16, cc + 9:cc + 17] = wv[:, 8:16]
    return m

def lep_proj_map(proj_w):
    m = np.zeros((128, 64), np.float32)
    for s in range(4):
        r0, cc = 32 * s, 16 * s
        m[r0:r0 + 8, cc:cc + 16] = proj_w[0:8, :]
        m[r0 + 9:r0 + 17, cc:cc + 16] = proj_w[8:16, :]
    return m

def proj_bd8(proj_w):
    m = np.zeros((128, 128), np.float32)
    for g in range(8):
        m[g * 16:(g + 1) * 16, g * 16:(g + 1) * 16] = proj_w
    return m

def bd16_avg():
    m = np.zeros((128, 128), np.float32)
    for g in range(8):
        m[g * 16:(g + 1) * 16, g * 16:(g + 1) * 16] = 1.0 / 16.0
    return m

def cw_cols(cw0, cb0, cw1, cb1):
    cols = np.zeros((128, 6), np.float32)
    for s in range(4):
        r0 = 32 * s
        for c in range(8):
            cols[r0 + c, 0] = cw0[c, 0, 1, 1]
            cols[r0 + c, 1] = cb0[c]
            cols[r0 + c, 2] = cw0[c, 0, 1, 0]
            cols[r0 + c, 3] = cw0[c, 0, 1, 2]
            cols[r0 + 9 + c, 0] = cw1[c, 0, 1, 1]
            cols[r0 + 9 + c, 1] = cb1[c]
            cols[r0 + 9 + c, 4] = cw1[c, 0, 0, 1]
            cols[r0 + 9 + c, 5] = cw1[c, 0, 2, 1]
    return cols

def ln_cols(ln_w, ln_b, proj_b):
    cols = np.zeros((128, 3), np.float32)
    for p_ in range(128):
        c = p_ % 16
        cols[p_] = (ln_w[c], ln_b[c], proj_b[c])
    return cols

def ffn_weight_tiles(wg, wu, wd):
    gu = np.zeros((64, 128, 2, 16, 128), np.int8)
    for ic in range(64):
        gu[ic, :, 0] = wg[:, ic * 128:(ic + 1) * 128].reshape(16, 128, 128).transpose(1, 0, 2)
        gu[ic, :, 1] = wu[:, ic * 128:(ic + 1) * 128].reshape(16, 128, 128).transpose(1, 0, 2)
    dn = np.zeros((4, 32, 128, 2, 512), np.int8)
    for oc in range(4):
        for ic in range(64):
            dn[oc, ic // 2, :, ic % 2] = wd[ic * 128:(ic + 1) * 128,
                                            oc * 512:(oc + 1) * 512]
    return (np.ascontiguousarray(gu.reshape(64 * 128, 4096)),
            np.ascontiguousarray(dn.reshape(4 * 32 * 128, 1024)))


class Sched:
    def __init__(self):
        self.ops = []
        self.counts = {}

    def emit(self, engine, fn, waits=(), incs=()):
        self.ops.append((engine, fn, [w for w in waits if w is not None], list(incs)))

    def inc(self, sem, amt):
        k = id(sem)
        self.counts[k] = self.counts.get(k, 0) + amt
        return (sem, amt)

    def at(self, sem):
        return (sem, self.counts.get(id(sem), 0))

    def run(self, engine, eng):
        for e, fn, waits, incs in self.ops:
            if e != engine:
                continue
            for sem, val in waits:
                if val > 0:
                    eng.wait_ge(sem, val)
            inst = fn(eng)
            for sem, amt in incs:
                inst.then_inc(sem, amt)


def apx(base, offset, dims):
    t = getattr(base, "tensor", base)
    fsz = int(np.prod(t.shape[1:]))
    np_ = int(t.shape[0])
    ap = [[fsz, np_]] + [list(d) for d in dims]
    return bass.AP(t, offset, ap)


class Prog:
    def __init__(self, nrows=2048, nbatch=4, taps=(), stage=9, rd=False):
        assert nrows % (32 * nbatch) == 0 and nrows % 512 == 0
        self.stage = stage
        self.rd = rd
        self.nrows = nrows
        self.nbatch = nbatch
        self.rpb = nrows // nbatch
        self.nsub = nrows // 32
        self.ntok = nrows * 128
        self.taps = set(taps)

    def build(self):
        p = self
        NR, NS, NT = p.nrows, p.nsub, p.ntok
        NSEG = NS // 4
        L16S = min(8, NS)
        RT = NR // 128
        MT = p.rpb // 16
        nc = bass.Bass("TRN2", target_bir_lowering=False, detect_race_conditions=self.rd)

        ein, eout = "ExternalInput", "ExternalOutput"
        xin = nc.dram_tensor("xin", [NR, DM], F32, kind=ein).ap()
        xperm = nc.dram_tensor("xperm", [NR, DM], F32, kind=ein).ap()
        qmap_d = nc.dram_tensor("qmapw", [64, 128], F32, kind=ein).ap()
        kmap_d = nc.dram_tensor("kmapw", [64, 128], F32, kind=ein).ap()
        vmap_d = nc.dram_tensor("vmapw", [64, 128], F32, kind=ein).ap()
        auxin_d = nc.dram_tensor("auxin", [9, 4096], F32, kind=ein).ap()
        ident_d = nc.dram_tensor("identw", [128, 128], F32, kind=ein).ap()
        lepmap_d = nc.dram_tensor("lepmapw", [128, 64], F32, kind=ein).ap()
        projbd_d = nc.dram_tensor("projbdw", [128, 128], F32, kind=ein).ap()
        bd16_d = nc.dram_tensor("bd16w", [128, 128], F32, kind=ein).ap()
        cw_d = nc.dram_tensor("cwcols", [128, 6], F32, kind=ein).ap()
        lnc_d = nc.dram_tensor("lncols", [128, 3], F32, kind=ein).ap()
        wgu_d = nc.dram_tensor("wgu", [64 * 128, 4096], I8, kind=ein).ap()
        wdn_d = nc.dram_tensor("wdn", [4 * 32 * 128, 1024], I8, kind=ein).ap()
        sg_d = nc.dram_tensor("sgcol", [128, 64], F32, kind=ein).ap()
        su_d = nc.dram_tensor("sucol", [128, 64], F32, kind=ein).ap()
        sdn_d = nc.dram_tensor("sdnw", [128, DM], F32, kind=ein).ap()
        eps_d = nc.dram_tensor("epscol", [128, 1], F32, kind=ein).ap()

        out_d = nc.dram_tensor("out", [NR, DM], F32, kind=eout).ap()

        lz = nc.dram_tensor("lz", [NR, DM], BF16, kind="Internal").ap()
        avp = nc.dram_tensor("avp", [16, NT], BF16, kind="Internal").ap()
        lepp = nc.dram_tensor("lepp", [16, NT], BF16, kind="Internal").ap()
        attnd = nc.dram_tensor("attnd", [16, NT], BF16, kind="Internal").ap()
        x1d = nc.dram_tensor("x1d", [NR, DM], F32, kind="Internal").ap()

        tapd = {}
        def mktap(name, shape, dtype=F32):
            if name in p.taps:
                tapd[name] = nc.dram_tensor("tap_" + name, shape, dtype, kind=eout).ap()
            return tapd.get(name)

        t_lz = mktap("lz", [NR, DM])
        t_qbuf = mktap("qbuf", [128, 4096])
        t_kbuf = mktap("kbuf", [128, 4096])
        t_vsb = mktap("vsb", [128, 4096])
        t_lep = mktap("lep", [128, 4096])
        t_asb = mktap("asb", [128, 2048])
        t_att1 = mktap("att1", [128, 128])
        t_avp = mktap("avp", [16, NT])
        t_lepp = mktap("lepp", [16, NT])
        t_attn = mktap("attn", [16, NT])
        t_x1 = mktap("x1", [NR, DM])
        t_fm = mktap("fm", [128, 16 * NR])

        S = Sched()
        FMW = 16 * NR

        with contextlib.ExitStack() as _st:
            _en = _st.enter_context
            qmap = _en(nc.sbuf_tensor("c_qmap", [64, 128], BF16))
            kmap = _en(nc.sbuf_tensor("c_kmap", [64, 128], BF16))
            vmap = _en(nc.sbuf_tensor("c_vmap", [64, 128], BF16))
            ident = _en(nc.sbuf_tensor("c_ident", [128, 128], BF16))
            lepmap = _en(nc.sbuf_tensor("c_lepmap", [128, 64], BF16))
            projbd = _en(nc.sbuf_tensor("c_projbd", [128, 128], BF16))
            bd16 = _en(nc.sbuf_tensor("c_bd16", [128, 128], BF16))
            cw = _en(nc.sbuf_tensor("c_cw", [128, 6], F32))
            lnc = _en(nc.sbuf_tensor("c_lnc", [128, 3], F32))
            epsc = _en(nc.sbuf_tensor("c_eps", [128, 1], F32))
            sg = _en(nc.sbuf_tensor("c_sg", [128, 64], F32))
            su = _en(nc.sbuf_tensor("c_su", [128, 64], F32))
            sdn = _en(nc.sbuf_tensor("c_sdn", [128, DM], BF16))
            af32 = _en(nc.sbuf_tensor("af32", [128, 6176], F32))
            ab = _en(nc.sbuf_tensor("ab", [128, 75776], BF16))
            gsil = _en(nc.sbuf_tensor("gsil", [128, 512], BF16))
            wdi0 = _en(nc.sbuf_tensor("wdi0", [128, 1024], I8))
            wdi1 = _en(nc.sbuf_tensor("wdi1", [128, 1024], I8))
            wgi0 = _en(nc.sbuf_tensor("wgi0", [128, 4096], I8))
            wgi1 = _en(nc.sbuf_tensor("wgi1", [128, 4096], I8))
            rsb = _en(nc.sbuf_tensor("rsb", [128, 16], F32))
            wg0 = _en(nc.sbuf_tensor("wg0", [128, 4096], BF16))
            wg1 = _en(nc.sbuf_tensor("wg1", [128, 4096], BF16))

            pb0 = _en(nc.psum_tensor("pb0", [128, 512], F32))
            pb1 = _en(nc.psum_tensor("pb1", [128, 512], F32))
            pb2 = _en(nc.psum_tensor("pb2", [128, 512], F32))
            psgrp = _en(nc.psum_tensor("psgrp", [128, 2048], F32))
            pb7 = _en(nc.psum_tensor("pb7", [128, 512], F32))
            sdma = _en(nc.semaphore("sdma"))
            gdma = _en(nc.semaphore("gdma"))
            tdma = _en(nc.semaphore("tdma"))
            smm = _en(nc.semaphore("smm"))
            sev = _en(nc.semaphore("sev"))
            ssc = _en(nc.semaphore("ssc"))
            sgp = _en(nc.semaphore("sgp"))
            wcv = _en(nc.semaphore("wcv"))
            sbar = _en(nc.semaphore("sbar"))
            blk = _en(nc.Block())

            def A(c0_, w):
                return ab[:, c0_:c0_ + w]
            fmsb = A(0, FMW)
            B0 = FMW
            fuse = A(B0, 32768)
            qb0 = A(B0, 4096); qb1 = A(B0 + 4096, 4096)
            kb0 = A(B0 + 8192, 4096); kb1 = A(B0 + 12288, 4096)
            vsb = A(B0 + 16384, 4096); lepb = A(B0 + 20480, 4096)
            vtsb = A(B0 + 24576, 4096)
            asb0 = A(B0 + 28672, 1024); asb1 = A(B0 + 29696, 1024)

            l16a = A(B0, 4096); l16w = A(B0 + 4096, 4096)
            l16d = A(B0 + 8192, 4096); l16o = A(B0 + 12288, 4096)
            C0 = B0 + 32768
            ffnsb = A(C0, 4 * DM)
            fla = ab[0:64, C0:C0 + 4096]
            flb = ab[0:64, C0 + 4096:C0 + 8192]
            rowsb = A(C0 + 8192, 2048)
            att1 = A(C0 + 8192, 128)
            cmsb = A(C0 + 8320, 128)
            avps = A(C0 + 8448, 128)
            lepps = A(C0 + 8576, 512)
            wd0 = A(C0 + 8192, 1024)
            wd1 = A(C0 + 9216, 1024)
            rows = af32[:, 0:2048]
            rows2 = af32[:, 2048:4096]
            rows3 = af32[:, 4096:6144]
            stat = af32[:, 6144:6176]
            l16r = af32[:, 0:4096]

            psq, psk, psv = pb0, pb1, pb2
            psav, psm, psm2 = pb0, pb2, pb7
            pstr = pb1[:, :].bitcast(BF16)

            nbar = [0]

            def barrier():
                nbar[0] += 1
                tgt = 5 * nbar[0]
                ws = {"sync": [S.at(sdma)], "gpsimd": [S.at(gdma)],
                      "scalar": [S.at(tdma)], "tensor": [], "vector": []}
                for en in ("sync", "gpsimd", "tensor", "vector", "scalar"):
                    S.emit(en, lambda e: e.nop(), ws[en], [S.inc(sbar, 1)])
                for en in ("sync", "gpsimd", "tensor", "vector", "scalar"):
                    S.emit(en, lambda e: e.nop(), [(sbar, tgt)], [])

            def dma(engine, sem, dst, src, waits=()):
                S.emit(engine, lambda e, d=dst, s=src: e.dma_start(out=d, in_=s),
                       waits, [S.inc(sem, 16)])
                return S.at(sem)

            for d, s_ in ((qmap, qmap_d), (kmap, kmap_d), (vmap, vmap_d),
                          (ident, ident_d), (lepmap, lepmap_d),
                          (projbd, projbd_d), (bd16, bd16_d),
                          (sdn, sdn_d)):
                dma("gpsimd", gdma, d[:], s_)
            for d, s_ in ((cw, cw_d), (lnc, lnc_d), (sg, sg_d), (su, su_d), (epsc, eps_d)):
                dma("sync", sdma, d[:], s_)
            S.emit("vector", lambda e: e.memset(fla[:], 0.0), [], [S.inc(sev, 1)])
            S.emit("vector", lambda e: e.memset(flb[:], 0.0), [], [S.inc(sev, 1)])
            mz = S.at(sev)
            for buf in (fla, flb):
                for st in (0, 32):
                    dma("gpsimd", gdma, buf[st + 16:st + 25, :], auxin_d, [mz])
            barrier()

            def ln_stats(src, junk, waits=()):
                waits = list(waits)
                S.emit("vector", lambda e: e.reduce_sum(stat[:, 0:1], src, AX.X),
                       waits + [S.at(ssc)], [S.inc(sev, 1)])
                S.emit("scalar", lambda e: e.activation(
                    junk, src, AF.Square, accum_out=stat[:, 1:2]),
                    waits + [S.at(sev)], [S.inc(ssc, 1)])
                S.emit("vector", lambda e: e.tensor_scalar(
                    stat[:, 2:4], stat[:, 0:2], 1.0 / DM, None, ALU.mult),
                    [S.at(ssc), S.at(sev)], [S.inc(sev, 1)])
                S.emit("vector", lambda e: e.tensor_tensor(
                    stat[:, 4:5], stat[:, 2:3], stat[:, 2:3], ALU.mult),
                    [S.at(sev)], [S.inc(sev, 1)])
                S.emit("vector", lambda e: e.tensor_sub(
                    stat[:, 5:6], stat[:, 3:4], stat[:, 4:5]), [S.at(sev)],
                    [S.inc(sev, 1)])
                S.emit("scalar", lambda e: e.activation(
                    stat[:, 6:7], stat[:, 5:6], AF.Ln, bias=epsc[:, 0:1]),
                    [S.at(sev)], [S.inc(ssc, 1)])
                S.emit("scalar", lambda e: e.activation(
                    stat[:, 7:8], stat[:, 6:7], AF.Exp, scale=-0.5), [S.at(ssc)],
                    [S.inc(ssc, 1)])
                return S.at(ssc)

            for rt in range(RT):
                w_ld = dma("sync", sdma, rows[:], xin[rt * 128:(rt + 1) * 128, :],
                           [S.at(sev)])
                w_rstd = ln_stats(rows[:], rows2[:], [w_ld])
                S.emit("vector", lambda e: e.tensor_scalar(
                    rowsb.rearrange("p (c w) -> p w c", c=16),
                    rows.rearrange("p (w c) -> p w c", c=16),
                    stat[:, 2:3], stat[:, 7:8], ALU.subtract, ALU.mult),
                    [w_rstd, S.at(sdma), S.at(gdma)], [S.inc(sev, 1)])
                dma("sync", sdma, lz[rt * 128:(rt + 1) * 128, :], rowsb[:],
                    [S.at(sev)])
                if t_lz is not None:
                    dma("gpsimd", gdma, t_lz[rt * 128:(rt + 1) * 128, :], rowsb[:],
                        [S.at(sev)])
            barrier()

            def lz_cAP(sub):
                return lz.rearrange("r (c w) -> c r w", c=16)[
                    :, sub * 32:(sub + 1) * 32, :]

            def sig_out(buf, cc):
                v = buf.rearrange("p (a b d e) -> p a b d e", a=4, b=16, d=8, e=8)
                return v[:, :, :, cc, :]

            w_qkv_mm_end = None
            w_exp_par = [None, None]
            for seg in range(NSEG if p.stage >= 2 else 0):
                sub0 = seg * 4
                wfl = []
                flw = [w_qkv_mm_end] if w_qkv_mm_end else []
                for ipair, buf in ((0, fla), (1, flb)):
                    for sl_ in range(2):
                        sub = sub0 + 2 * ipair + sl_
                        wfl.append(dma("gpsimd", gdma,
                                       buf[sl_ * 32:sl_ * 32 + 16, :], lz_cAP(sub),
                                       flw))
                psq_r = [psq[:, 0:512], psgrp[:, 0:512]]
                psk_r = [psk[:, 0:512], psgrp[:, 512:1024]]
                psv_r = [psv, psgrp[:, 1024:1536]]
                wq_war = [None, None]
                wk_war = [None, None]
                wv_war = [None, None]
                first_ev = True
                for ipair, buf, qb, kb in ((0, fla, qb0, kb0), (1, flb, qb1, kb1)):
                    for cc in range(8):
                        par = cc % 2
                        gcc = ipair * 8 + cc
                        sl_ = slice(cc * 512, (cc + 1) * 512)
                        qw = [wfl[2 * ipair], wfl[2 * ipair + 1]]
                        if gcc == 0:
                            qw.append(S.at(sev))
                        elif gcc == 1:
                            qw += [w_exp_par[0]] if w_exp_par[0] else []
                        qw += wq_war[par] or []
                        S.emit("tensor", lambda e, b=buf, s=sl_, pp=psq_r[par]:
                               e.matmul(pp, qmap[:], b[:, s],
                                        start=True, stop=True),
                               qw, [S.inc(smm, 1)])
                        S.emit("tensor", lambda e, b=buf, s=sl_, pp=psk_r[par]:
                               e.matmul(pp, kmap[:], b[:, s],
                                        start=True, stop=True),
                               wk_war[par] or [], [S.inc(smm, 1)])
                        wmm = S.at(smm)
                        evw = [wmm, S.at(smm)] if first_ev else [wmm]
                        first_ev = False
                        S.emit("vector", lambda e, d_=qb, c=cc, pp=psq_r[par]:
                               e.tensor_copy(sig_out(d_, c), pp), evw,
                               [S.inc(sev, 1)])
                        wq_war[par] = [S.at(sev)]
                        S.emit("vector", lambda e, d_=kb, c=cc, pp=psk_r[par]:
                               e.tensor_copy(sig_out(d_, c), pp), [],
                               [S.inc(sev, 1)])
                        wk_war[par] = [S.at(sev)]
                for cc in range(8):
                    par = cc % 2
                    sl_ = slice(cc * 512, (cc + 1) * 512)
                    vw = []
                    if cc == 0:
                        vw.append(S.at(sev))
                    elif cc == 1:
                        vw += [w_exp_par[1]] if w_exp_par[1] else []
                    vw += wv_war[par] or []
                    S.emit("tensor", lambda e, s=sl_, pp=psv_r[par]: e.matmul(
                        pp[0:64, 0:512], vmap[0:64, 0:64], fla[:, s],
                        start=True, stop=True), vw, [S.inc(smm, 1)])
                    S.emit("tensor", lambda e, s=sl_, pp=psv_r[par]: e.matmul(
                        pp[64:128, 0:512], vmap[0:64, 0:64], flb[:, s],
                        start=True, stop=True, tile_position=(0, 64)),
                        [], [S.inc(smm, 1)])
                    S.emit("vector", lambda e, c=cc, pp=psv_r[par]: e.tensor_copy(
                        sig_out(vsb, c), pp[:, 0:512]), [S.at(smm)],
                        [S.inc(sev, 1)])
                    wv_war[par] = [S.at(sev)]
                w_qkv_mm_end = S.at(smm)
                w_qkv = S.at(sev)
                w_psgrp_extra = {0: [w_qkv], 1: [w_qkv]}
                if t_qbuf is not None and seg == 0:
                    dma("gpsimd", gdma, t_qbuf, qb0[:], [w_qkv])
                    dma("gpsimd", gdma, t_kbuf, kb0[:], [w_qkv])
                    dma("gpsimd", gdma, t_vsb, vsb[:], [w_qkv])

                for b4 in range(8):
                    for j in range(4):
                        beta = b4 * 4 + j
                        S.emit("tensor", lambda e, b=beta, jj=j: e.transpose(
                            pstr[:, jj * 128:(jj + 1) * 128],
                            vsb[:, b * 128:(b + 1) * 128], ident[:]),
                            [w_qkv] + ([S.at(sev)] if j == 0 else []),
                            [S.inc(smm, 1)])
                    S.emit("vector", lambda e, b=b4: e.tensor_copy(
                        vtsb[:, b * 512:(b + 1) * 512], pstr[:, 0:512]),
                        [S.at(smm)], [S.inc(sev, 1)])
                w_vt = S.at(sev)
                S.emit("vector", lambda e: e.tensor_scalar(
                    lepb[:], vsb[:], cw[:, 0:1], cw[:, 1:2], ALU.mult, ALU.add),
                    [w_qkv, S.at(smm)], [S.inc(sev, 1)])
                vv = vsb.rearrange("p (w e) -> p w e", e=8)
                ll = lepb.rearrange("p (w e) -> p w e", e=8)
                S.emit("vector", lambda e: e.scalar_tensor_tensor(
                    ll[:, :, 1:8], vv[:, :, 0:7], cw[:, 2:3], ll[:, :, 1:8],
                    ALU.mult, ALU.add), [S.at(sev)], [S.inc(sev, 1)])
                S.emit("vector", lambda e: e.scalar_tensor_tensor(
                    ll[:, :, 0:7], vv[:, :, 1:8], cw[:, 3:4], ll[:, :, 0:7],
                    ALU.mult, ALU.add), [S.at(sev)], [S.inc(sev, 1)])
                vv8 = vsb.rearrange("p (b d e) -> p b d e", d=8, e=8)
                ll8 = lepb.rearrange("p (b d e) -> p b d e", d=8, e=8)
                S.emit("vector", lambda e: e.scalar_tensor_tensor(
                    ll8[:, :, 1:8, :], vv8[:, :, 0:7, :], cw[:, 4:5],
                    ll8[:, :, 1:8, :], ALU.mult, ALU.add), [S.at(sev)],
                    [S.inc(sev, 1)])
                S.emit("vector", lambda e: e.scalar_tensor_tensor(
                    ll8[:, :, 0:7, :], vv8[:, :, 1:8, :], cw[:, 5:6],
                    ll8[:, :, 0:7, :], ALU.mult, ALU.add), [S.at(sev)],
                    [S.inc(sev, 1)])
                w_lep = S.at(sev)
                if t_lep is not None and seg == 0:
                    dma("gpsimd", gdma, t_lep, lepb[:], [w_lep])

                for s4 in range(4 if p.stage >= 3 else 0):
                    sub = sub0 + s4
                    qb, kb = (qb0, kb0) if s4 < 2 else (qb1, kb1)
                    st1 = (s4 % 2) * 64
                    st2 = st1 + 32

                    def emit_grams(bg):
                        half = bg % 2
                        hof = half * 1024
                        a_sb = asb0 if half == 0 else asb1
                        w_war = [w_exp_par[half]] if w_exp_par[half] else []
                        w_war += w_psgrp_extra.pop(half, [])
                        for j in range(4):
                            beta = bg * 4 + j
                            csl = slice(beta * 128, (beta + 1) * 128)
                            for br, st in ((0, st1), (1, st2)):
                                u = br * 4 + j
                                S.emit("tensor", lambda e, k_=kb, q_=qb, s=csl,
                                       r0=st, uu=u, ho=hof: e.matmul(
                                    psgrp[:, ho + uu * 128:ho + (uu + 1) * 128],
                                    k_[r0:r0 + 20, s], q_[r0:r0 + 20, s],
                                    start=True, stop=True, tile_position=(r0, 0)),
                                    w_war if (j == 0 and br == 0) else [],
                                    [S.inc(smm, 1)])
                        S.emit("scalar", lambda e, ho=hof, ab_=a_sb: e.activation(
                            ab_[:, 0:1024], psgrp[:, ho:ho + 1024], AF.Exp,
                            scale=SQK), [S.at(smm)], [S.inc(ssc, 1)])
                        w_exp_par[half] = S.at(ssc)
                        return (bg, S.at(ssc), a_sb)

                    def emit_av(info):
                        bg, w_exp, a_sb = info
                        if t_asb is not None and seg == 0 and s4 == 0 and bg < 2:
                            dma("gpsimd", gdma, t_asb[:, bg * 1024:(bg + 1) * 1024],
                                a_sb[:, 0:1024], [w_exp])
                        for j in range(4):
                            beta = bg * 4 + j
                            jj = (bg % 2) * 4 + j
                            for br in range(2):
                                u = br * 4 + j
                                r0_ = beta * 128 + s4 * 32 + br * 8
                                S.emit("tensor", lambda e, uu=u, jx=jj, b_=br,
                                       r=r0_, ab_=a_sb: e.matmul(
                                    psav[:, jx * 18 + b_ * 9:jx * 18 + b_ * 9 + 9],
                                    ab_[:, uu * 128:(uu + 1) * 128],
                                    vtsb[:, r:r + 9], start=True, stop=True),
                                    [w_exp, w_vt, S.at(sev)]
                                    if (j == 0 and br == 0) else [],
                                    [S.inc(smm, 1)])

                    def emit_ppd(bgodd):
                        w_av = S.at(smm)
                        S.emit("vector", lambda e: e.reciprocal(
                            rsb[:, 0:16].rearrange("p (a b) -> p a b", b=2),
                            apx(psav, 8, [[18, 8], [1, 2]])), [w_av],
                            [S.inc(sev, 1)])
                        S.emit("vector", lambda e: e.tensor_tensor(
                            att1.rearrange("p (j b c) -> p j b c", b=2, c=8),
                            apx(psav, 0, [[18, 8], [10, 2], [1, 8]]),
                            apx(rsb, 0, [[2, 8], [1, 2], [0, 8]]),
                            ALU.mult), [S.at(smm), S.at(sev)], [S.inc(sev, 1)])
                        w_n = S.at(sev)
                        if t_att1 is not None and seg == 0 and s4 == 0 and bgodd == 1:
                            dma("gpsimd", gdma, t_att1, att1[:], [w_n])
                        return (bgodd, w_n)

                    def emit_ppt(info):
                        bgodd, w_n = info
                        S.emit("tensor", lambda e: e.transpose(
                            pstr[:, 512:640], att1[:], ident[:]),
                            [w_n], [S.inc(smm, 1)])
                        S.emit("vector", lambda e: e.tensor_copy(
                            cmsb[:], pstr[:, 512:640]), [S.at(smm)],
                            [S.inc(sev, 1)])
                        S.emit("tensor", lambda e: e.matmul(
                            psm2[:, 0:128], projbd[:], cmsb[:],
                            start=True, stop=True), [S.at(sev)], [S.inc(smm, 1)])
                        S.emit("vector", lambda e: e.tensor_copy(
                            avps[:], psm2[:, 0:128]), [S.at(smm), S.at(sdma)],
                            [S.inc(sev, 1)])
                        bgp = bgodd // 2
                        dst = avp.rearrange("c (u b t) -> u b c t", u=NS, t=128)[
                            sub, bgp * 8:(bgp + 1) * 8, :, :]
                        dma("sync", sdma, dst, avps[:], [S.at(sev)])

                    pend = None
                    ptp = None
                    for bg in range(8):
                        info = emit_grams(bg)
                        if pend is not None:
                            emit_av(pend)
                            if pend[0] % 2 == 1:
                                if ptp is not None:
                                    emit_ppt(ptp)
                                ptp = emit_ppd(pend[0])
                        pend = info
                    emit_av(pend)
                    emit_ppt(ptp)
                    emit_ppt(emit_ppd(7))
                for cc in range(8):
                    sl_ = slice(cc * 512, (cc + 1) * 512)
                    S.emit("tensor", lambda e, s=sl_: e.matmul(
                        psm[0:64, 0:512], lepmap[:], lepb[:, s], start=True,
                        stop=True), [w_lep, w_qkv, S.at(sev)], [S.inc(smm, 1)])
                    S.emit("vector", lambda e: e.tensor_copy(
                        lepps[0:64, :], psm[0:64, 0:512]), [S.at(smm), S.at(sdma)],
                        [S.inc(sev, 1)])
                    dst = lepp.rearrange("c (u t) -> u c t", t=4096)[
                        sub0:sub0 + 4, :, cc * 512:(cc + 1) * 512]
                    dma("sync", sdma, dst, lepps[0:64, :], [S.at(sev)])


            barrier()

            if t_avp is not None:
                dma("gpsimd", gdma, t_avp, avp)
            if t_lepp is not None:
                dma("gpsimd", gdma, t_lepp, lepp)

            for g in range(NS // L16S if p.stage >= 7 else 0):
                u0 = g * L16S
                pr = L16S * 16
                srcA = avp.rearrange("c (u t) -> u c t", t=4096)[u0:u0 + L16S]
                srcB = lepp.rearrange("c (u t) -> u c t", t=4096)[u0:u0 + L16S]
                wa = dma("gpsimd", gdma, l16a[0:pr, :], srcA, [S.at(sev)])
                wb_ = dma("gpsimd", gdma, l16d[0:pr, :], srcB, [S.at(sev)])
                S.emit("vector", lambda e: e.scalar_tensor_tensor(
                    l16w[0:pr, :], l16a[0:pr, :], lnc[0:pr, 2:3], l16d[0:pr, :],
                    ALU.add, ALU.add), [wa, wb_], [S.inc(sev, 1)])
                w_w = S.at(sev)
                for cc in range(8):
                    sl_ = slice(cc * 512, (cc + 1) * 512)
                    S.emit("tensor", lambda e, s=sl_: e.matmul(
                        psm[:, 0:512], bd16[:], l16w[:, s], start=True, stop=True),
                        [w_w, S.at(sev)], [S.inc(smm, 1)])
                    S.emit("vector", lambda e, s=sl_: e.tensor_sub(
                        l16d[:, s], l16w[:, s], psm[:, 0:512]), [S.at(smm)],
                        [S.inc(sev, 1)])
                    S.emit("scalar", lambda e, s=sl_: e.activation(
                        l16a[:, s], l16d[:, s], AF.Square), [S.at(sev)],
                        [S.inc(ssc, 1)])
                    S.emit("tensor", lambda e, s=sl_: e.matmul(
                        psm2[:, 0:512], bd16[:], l16a[:, s], start=True,
                        stop=True), [S.at(ssc)], [S.inc(smm, 1)])
                    S.emit("scalar", lambda e, s=sl_: e.activation(
                        l16r[:, s], psm2[:, 0:512], AF.Ln, bias=epsc[:, 0:1]),
                        [S.at(smm)], [S.inc(ssc, 1)])
                    S.emit("scalar", lambda e, s=sl_: e.activation(
                        l16r[:, s], l16r[:, s], AF.Exp, scale=-0.5), [S.at(ssc)],
                        [S.inc(ssc, 1)])
                    S.emit("vector", lambda e, s=sl_: e.tensor_mul(
                        l16d[:, s], l16d[:, s], l16r[:, s]), [S.at(ssc)],
                        [S.inc(sev, 1)])
                    lam = cc
                    o4 = l16o.rearrange("p (d cz wh wl e) -> p d cz wh wl e",
                                        d=8, cz=4, wh=2, wl=8, e=8)
                    ov = o4[:, :, lam >> 1, lam & 1, :, :].transpose([0, 2, 1, 3])
                    S.emit("vector", lambda e, o=ov, s=sl_: e.tensor_scalar(
                        o, l16d[:, s].rearrange("p (w d e) -> p w d e", d=8, e=8),
                        lnc[:, 0:1], lnc[:, 1:2], ALU.mult, ALU.add),
                        [S.at(sev)] + ([S.at(sdma)] if cc == 0 else []),
                        [S.inc(sev, 1)])
                dst = attnd.rearrange("c (u t) -> u c t", t=4096)[u0:u0 + L16S]
                dma("sync", sdma, dst, l16o[0:pr, :], [S.at(sev)])
            barrier()
            if t_attn is not None:
                dma("gpsimd", gdma, t_attn, attnd)
                barrier()

            ncv = 128 // MT
            p3s1 = A(B0, 2048)
            p3s2 = A(B0 + 2048, 2048)
            fm_rdy = []
            x1_rdy = []
            for rt in range(RT if p.stage >= 8 else 0):
                r0 = rt * 128
                bi = r0 // p.rpb
                ci = (r0 % p.rpb) // MT
                w_pm = []
                for j in range(ncv):
                    c = ci + j
                    asrc = attnd[c:c + 1, :].rearrange(
                        "o (b m w) -> o b m w", b=p.nbatch, w=DM)[0, bi]
                    w_pm.append(dma("sync", sdma,
                                    p3s1[j * MT:(j + 1) * MT, :], asrc,
                                    [S.at(sev)]))
                    lsrc = lz.rearrange("(b r) (c w) -> b r c w",
                                        b=p.nbatch, c=16)[bi, :, c, :].rearrange(
                        "(m h) w -> m h w", h=16)
                    w_pm.append(dma("scalar", tdma,
                                    p3s2[j * MT:(j + 1) * MT, :].rearrange(
                                        "p (h w) -> p h w", w=128), lsrc,
                                    [S.at(sev)]))
                w_x0 = dma("sync", sdma, rows[:], xperm[r0:r0 + 128, :],
                           [S.at(sev)])
                S.emit("vector", lambda e: e.tensor_add(
                    rows2[:], p3s1[:], p3s2[:]), w_pm, [S.inc(sev, 1)])
                w_rstd = ln_stats(rows2[:], rows3[:], [S.at(sev)])
                S.emit("vector", lambda e: e.tensor_scalar(
                    rows2[:], rows2[:], stat[:, 2:3], stat[:, 7:8],
                    ALU.subtract, ALU.mult), [w_rstd], [S.inc(sev, 1)])
                S.emit("vector", lambda e: e.tensor_add(
                    rows[:], rows[:], rows2[:]), [w_x0, S.at(sev)],
                    [S.inc(sev, 1)])
                w_x1 = S.at(sev)
                w_x1d = dma("sync", sdma, x1d[r0:r0 + 128, :], rows[:], [w_x1])
                if t_x1 is not None:
                    dma("sync", sdma, t_x1[r0:r0 + 128, :], rows[:], [w_x1])
                w_rstd = ln_stats(rows[:], rows3[:])
                S.emit("vector", lambda e: e.tensor_scalar(
                    rowsb[:], rows[:], stat[:, 2:3], stat[:, 7:8],
                    ALU.subtract, ALU.mult), [w_rstd, S.at(tdma)],
                    [S.inc(sev, 1)])
                w_ln2 = S.at(sev)
                for kc in range(16):
                    S.emit("scalar", lambda e, k=kc, r=rt: e.dma_start_transpose(
                        out=fmsb[:, k * NR + r * 128:k * NR + (r + 1) * 128],
                        in_=rowsb[:, k * 128:(k + 1) * 128]),
                        [w_ln2], [S.inc(tdma, 16)])
                x1_rdy.append(w_x1d)
                if rt % 4 == 3:
                    fm_rdy.append(S.at(tdma))
            w_p3t = S.at(tdma) if (RT if p.stage >= 8 else 0) else None
            barrier()
            if t_fm is not None:
                dma("gpsimd", gdma, t_fm, fmsb[:], [w_p3t])
                barrier()

            NTT = (NR // 512) if p.stage >= 9 else 0
            wgub = [wg0, wg1]
            wgi = [wgi0, wgi1]
            wdb = [wd0, wd1]
            wdi = [wdi0, wdi1]
            w_giwar = [[], []]
            w_gcast = [[], []]
            w_gmm = [[], []]
            w_wiwar = [[], []]
            w_wdwar = [[w_p3t], [w_p3t]] if p.stage >= 8 else [[], []]
            psg_par = [pb0, pb2]
            psu_par = [pb1, pb7]
            w_sil_par = [None, None]
            w_stt_par = [None, None]
            gc = {}

            def g_loadcast(tt, ic):
                par = ic % 2
                wla = dma("sync", sdma, wgi[par][:, 0:2048],
                          wgu_d[ic * 128:(ic + 1) * 128, 0:2048],
                          w_giwar[par])
                wlb = dma("scalar", tdma, wgi[par][:, 2048:4096],
                          wgu_d[ic * 128:(ic + 1) * 128, 2048:4096],
                          w_giwar[par])
                S.emit("vector", lambda e, b=par: e.tensor_copy(
                    wgub[b][:], wgi[b][:]), [wla, wlb] + w_gmm[par],
                    [S.inc(sev, 1)])
                w_giwar[par] = [S.at(sev)]
                gc[(tt, ic)] = S.at(sev)

            for tt in range(NTT):
                g_loadcast(tt, 0)
                g_loadcast(tt, 1)
                for ic in range(64):
                    par = ic % 2
                    pg, pu = psg_par[par], psu_par[par]
                    war = [w for w in (w_sil_par[par], w_stt_par[par]) if w]
                    for kc in range(16):
                        S.emit("tensor", lambda e, k=kc, b=par, t=tt, pp=pg: e.matmul(
                            pp[:, 0:512], wgub[b][:, k * 128:(k + 1) * 128],
                            fmsb[:, k * NR + t * 512:k * NR + (t + 1) * 512],
                            start=(k == 0), stop=(k == 15)),
                            ([gc[(tt, ic)]] + war +
                             ([fm_rdy[tt]] if ic == 0 else []))
                            if kc == 0 else [], [])
                        S.emit("tensor", lambda e, k=kc, b=par, t=tt, pp=pu: e.matmul(
                            pp[:, 0:512], wgub[b][:, 2048 + k * 128:2048 + (k + 1) * 128],
                            fmsb[:, k * NR + t * 512:k * NR + (t + 1) * 512],
                            start=(k == 0), stop=(k == 15)),
                            [], [S.inc(smm, 1)] if kc == 15 else [])
                    wmm = S.at(smm)
                    w_gmm[par] = [wmm]
                    S.emit("scalar", lambda e, i=ic, pp=pg: e.activation(
                        gsil[:], pp[:, 0:512], AF.Silu, scale=sg[:, i:i + 1]),
                        [wmm, S.at(sev)], [S.inc(ssc, 1)])
                    w_sil_par[par] = S.at(ssc)
                    if ic + 2 < 64:
                        g_loadcast(tt, ic + 2)
                    S.emit("vector", lambda e, i=ic, pp=pu: e.scalar_tensor_tensor(
                        fuse[:, i * 512:(i + 1) * 512], pp[:, 0:512],
                        su[:, i:i + 1], gsil[:], ALU.mult, ALU.mult),
                        [S.at(ssc)], [S.inc(sev, 1)])
                    w_stt_par[par] = S.at(sev)
                w_fuse = S.at(sev)
                for oc in range(4):
                    for icp in range(32):
                        par = icp % 2
                        wld = dma("sync" if par == 0 else "scalar",
                                  sdma if par == 0 else tdma, wdi[par][:],
                                  wdn_d[(oc * 32 + icp) * 128:
                                        (oc * 32 + icp + 1) * 128, :],
                                  w_wiwar[par])
                        S.emit("vector", lambda e, b=par: e.tensor_copy(
                            wdb[b][:], wdi[b][:]), [wld] + w_wdwar[par],
                            [S.inc(sev, 1)])
                        w_wiwar[par] = [S.at(sev)]
                        wcp = S.at(sev)
                        for half in range(2):
                            ic = icp * 2 + half
                            for tb in range(4):
                                S.emit("tensor", lambda e, i=ic, t=tb, b=par,
                                       h=half: e.matmul(
                                    psgrp[:, t * 512:(t + 1) * 512],
                                    fuse[:, i * 512 + t * 128:i * 512 + (t + 1) * 128],
                                    wdb[b][:, h * 512:(h + 1) * 512],
                                    start=(i == 0), stop=(i == 63)),
                                    ([wcp, w_fuse, S.at(sev)]
                                     if (icp == 0 and half == 0 and tb == 0) else
                                     ([wcp] if (half == 0 and tb == 0) else [])),
                                    [S.inc(smm, 1)] if (half == 1 and tb == 3)
                                    else [])
                        w_wdwar[par] = [S.at(smm)]
                    wmm = S.at(smm)
                    for tb in range(4):
                        S.emit("vector", lambda e, t=tb, o=oc: e.tensor_copy(
                            ffnsb[:, t * DM + o * 512:t * DM + (o + 1) * 512],
                            psgrp[:, t * 512:(t + 1) * 512]), [wmm],
                            [S.inc(sev, 1)])
                w_dn = S.at(sev)

                for tb in range(4):
                    rt = tt * 4 + tb
                    r0 = rt * 128
                    S.emit("vector", lambda e, t=tb: e.tensor_mul(
                        rows2[:], ffnsb[:, t * DM:(t + 1) * DM], sdn[:]),
                        [w_dn], [S.inc(sev, 1)])
                    w_xr = dma("gpsimd", gdma, rows[:], x1d[r0:r0 + 128, :],
                               [S.at(sev), x1_rdy[rt]])
                    w_rstd = ln_stats(rows2[:], rows3[:], [S.at(sev)])
                    S.emit("vector", lambda e: e.tensor_scalar(
                        rows2[:], rows2[:], stat[:, 2:3], stat[:, 7:8],
                        ALU.subtract, ALU.mult), [w_rstd], [S.inc(sev, 1)])
                    S.emit("vector", lambda e: e.tensor_add(
                        rows[:], rows[:], rows2[:]), [w_xr, S.at(sev)],
                        [S.inc(sev, 1)])
                    dma("gpsimd", gdma, out_d[r0:r0 + 128, :], rows[:],
                        [S.at(sev)])
            S.emit("sync", lambda e: e.nop(), [S.at(sdma)], [])
            S.emit("gpsimd", lambda e: e.nop(), [S.at(gdma)], [])
            S.emit("scalar", lambda e: e.nop(), [S.at(tdma)], [])

            @blk.sync
            def _(eng):
                S.run("sync", eng)
            @blk.gpsimd
            def _(eng):
                S.run("gpsimd", eng)
            @blk.tensor
            def _(eng):
                S.run("tensor", eng)
            @blk.vector
            def _(eng):
                S.run("vector", eng)
            @blk.scalar
            def _(eng):
                S.run("scalar", eng)
        return nc, tapd



def host_consts(inputs):
    qkv_w = np.asarray(inputs["L0_attn_qkv_w"], np.float32)
    proj_w = np.asarray(inputs["L0_attn_proj_w"], np.float32)
    qm, km = qk_maps(qkv_w)
    wgu, wdn = ffn_weight_tiles(np.asarray(inputs["L0_ffn_w_gate"]),
                                np.asarray(inputs["L0_ffn_w_up"]),
                                np.asarray(inputs["L0_ffn_w_down"]))
    return {
        "qmapw": qm, "kmapw": km, "vmapw": v_map(qkv_w),
        "auxin": aux_input_rows(),
        "identw": np.eye(128, dtype=np.float32),
        "lepmapw": lep_proj_map(proj_w),
        "projbdw": proj_bd8(proj_w),
        "bd16w": bd16_avg(),
        "cwcols": cw_cols(np.asarray(inputs["L0_attn_conv_w0"], np.float32),
                          np.asarray(inputs["L0_attn_conv_b0"], np.float32),
                          np.asarray(inputs["L0_attn_conv_w1"], np.float32),
                          np.asarray(inputs["L0_attn_conv_b1"], np.float32)),
        "lncols": ln_cols(np.asarray(inputs["L0_attn_ln_w"], np.float32),
                          np.asarray(inputs["L0_attn_ln_b"], np.float32),
                          np.asarray(inputs["L0_attn_proj_b"], np.float32)),
        "sgcol": np.ascontiguousarray(
            np.asarray(inputs["L0_ffn_s_gate"], np.float32).reshape(64, 128).T),
        "sucol": np.ascontiguousarray(
            np.asarray(inputs["L0_ffn_s_up"], np.float32).reshape(64, 128).T),
        "epscol": np.full((128, 1), EPS, np.float32),
        "sdnw": np.ascontiguousarray(np.broadcast_to(np.asarray(inputs["L0_ffn_s_down"], np.float32), (128, DM))),
        "wgu": wgu, "wdn": wdn,
    }


def perm_rows(k, nbatch=4, rpb=512):
    mt = rpb // 16
    ridx = np.empty(nbatch * rpb, np.int64)
    i = 0
    for b in range(nbatch):
        for c in range(16):
            for m in range(mt):
                ridx[i] = b * 4096 + 256 * c + mt * k + m
                i += 1
    return ridx


def kernel(**inputs):
    x = np.asarray(inputs["L0_attn_x"], np.float32)
    B, H, Wd = x.shape
    consts = host_consts(inputs)
    p = Prog(nrows=2048, nbatch=4)
    nc, _ = p.build()
    xf = x.reshape(B * H, Wd)
    in_maps = []
    for k in range(8):
        xin = np.ascontiguousarray(np.concatenate(
            [x[b, 512 * k:512 * (k + 1), :] for b in range(B)], axis=0))
        ridx = perm_rows(k)
        xperm = np.ascontiguousarray(xf[ridx])
        m = {"xin": xin, "xperm": xperm}
        m.update(consts)
        in_maps.append(m)
    res = run_bass_kernel_spmd(nc, in_maps, core_ids=list(range(8)))
    out = np.zeros((B * H, Wd), np.float32)
    for k in range(8):
        out[perm_rows(k)] = res.results[k]["out"]
    return out.reshape(B, H, Wd)


# version 15
# speedup vs baseline: 1.1013x; 1.0130x over previous
import contextlib
import numpy as np
import ml_dtypes
import concourse.bass as bass
import concourse.mybir as mybir
from concourse.bass_utils import run_bass_kernel_spmd

F32 = mybir.dt.float32
BF16 = mybir.dt.bfloat16
I8 = mybir.dt.int8
AF = mybir.ActivationFunctionType
ALU = mybir.AluOpType
AX = mybir.AxisListType
bfd = ml_dtypes.bfloat16

MB = 100.0
EPS = 1e-5
DM = 2048
INTER = 8192
SQK = 8.0 ** -0.5


def sigma_perm(n=4096):
    t = np.arange(n)
    ht, wb = t // 128, t % 128
    d, c0 = ht // 4, ht % 4
    wbb, e = wb // 8, wb % 8
    return ((c0 * 16 + wbb) * 8 + d) * 8 + e

SIG = sigma_perm()

def aux_input_rows():
    tss = SIG[np.arange(4096)]
    w1 = (tss >> 3) & 15
    w2 = (tss & 7) | (((tss >> 6) & 1) << 3)
    rows = [np.ones(4096)]
    rows += [(w1 >> k) & 1 for k in range(4)]
    rows += [(w2 >> k) & 1 for k in range(4)]
    return np.asarray(rows, dtype=np.float32)

def qk_maps(qkv_w):
    maps = []
    for which in (0, 1):
        m = np.zeros((64, 128), np.float32)
        wsub = qkv_w[:, which * 16:(which + 1) * 16]
        for s in range(2):
            r0, cc = 32 * s, 64 * s
            for br in range(2):
                cb = cc + 32 * br
                m[r0:r0 + 16, cb:cb + 8] = wsub[:, br * 8:br * 8 + 8]
                for k in range(4):
                    bitrow = r0 + 17 + 4 * br + k
                    onerow = r0 + 16
                    a, b_, c_ = cb + 8 + 3 * k, cb + 9 + 3 * k, cb + 10 + 3 * k
                    if which == 0:
                        m[bitrow, a] = -MB
                        m[onerow, b_] = -MB
                        m[bitrow, c_] = 2 * MB
                    else:
                        m[onerow, a] = 1.0
                        m[bitrow, b_] = 1.0
                        m[bitrow, c_] = 1.0
        maps.append(m)
    return maps

def v_map(qkv_w):
    m = np.zeros((64, 128), np.float32)
    wv = qkv_w[:, 32:48]
    for s in range(2):
        r0, cc = 32 * s, 32 * s
        m[r0:r0 + 16, cc:cc + 8] = wv[:, 0:8]
        m[r0 + 16, cc + 8] = 1.0
        m[r0:r0 + 16, cc + 9:cc + 17] = wv[:, 8:16]
    return m

def lep_proj_map(proj_w):
    m = np.zeros((128, 64), np.float32)
    for s in range(4):
        r0, cc = 32 * s, 16 * s
        m[r0:r0 + 8, cc:cc + 16] = proj_w[0:8, :]
        m[r0 + 9:r0 + 17, cc:cc + 16] = proj_w[8:16, :]
    return m

def proj_bd8(proj_w):
    m = np.zeros((128, 128), np.float32)
    for g in range(8):
        m[g * 16:(g + 1) * 16, g * 16:(g + 1) * 16] = proj_w
    return m

def bd16_avg():
    m = np.zeros((128, 128), np.float32)
    for g in range(8):
        m[g * 16:(g + 1) * 16, g * 16:(g + 1) * 16] = 1.0 / 16.0
    return m

def cw_cols(cw0, cb0, cw1, cb1):
    cols = np.zeros((128, 6), np.float32)
    for s in range(4):
        r0 = 32 * s
        for c in range(8):
            cols[r0 + c, 0] = cw0[c, 0, 1, 1]
            cols[r0 + c, 1] = cb0[c]
            cols[r0 + c, 2] = cw0[c, 0, 1, 0]
            cols[r0 + c, 3] = cw0[c, 0, 1, 2]
            cols[r0 + 9 + c, 0] = cw1[c, 0, 1, 1]
            cols[r0 + 9 + c, 1] = cb1[c]
            cols[r0 + 9 + c, 4] = cw1[c, 0, 0, 1]
            cols[r0 + 9 + c, 5] = cw1[c, 0, 2, 1]
    return cols

def ln_cols(ln_w, ln_b, proj_b):
    cols = np.zeros((128, 3), np.float32)
    for p_ in range(128):
        c = p_ % 16
        cols[p_] = (ln_w[c], ln_b[c], proj_b[c])
    return cols

def ffn_weight_tiles(wg, wu, wd):
    gu = np.zeros((64, 128, 2, 16, 128), np.int8)
    for ic in range(64):
        gu[ic, :, 0] = wg[:, ic * 128:(ic + 1) * 128].reshape(16, 128, 128).transpose(1, 0, 2)
        gu[ic, :, 1] = wu[:, ic * 128:(ic + 1) * 128].reshape(16, 128, 128).transpose(1, 0, 2)
    dn = np.zeros((4, 32, 128, 2, 512), np.int8)
    for oc in range(4):
        for ic in range(64):
            dn[oc, ic // 2, :, ic % 2] = wd[ic * 128:(ic + 1) * 128,
                                            oc * 512:(oc + 1) * 512]
    return (np.ascontiguousarray(gu.reshape(64 * 128, 4096)),
            np.ascontiguousarray(dn.reshape(4 * 32 * 128, 1024)))


class Sched:
    def __init__(self):
        self.ops = []
        self.counts = {}

    def emit(self, engine, fn, waits=(), incs=()):
        self.ops.append((engine, fn, [w for w in waits if w is not None], list(incs)))

    def inc(self, sem, amt):
        k = id(sem)
        self.counts[k] = self.counts.get(k, 0) + amt
        return (sem, amt)

    def at(self, sem):
        return (sem, self.counts.get(id(sem), 0))

    def run(self, engine, eng):
        for e, fn, waits, incs in self.ops:
            if e != engine:
                continue
            for sem, val in waits:
                if val > 0:
                    eng.wait_ge(sem, val)
            inst = fn(eng)
            for sem, amt in incs:
                inst.then_inc(sem, amt)


def apx(base, offset, dims):
    t = getattr(base, "tensor", base)
    fsz = int(np.prod(t.shape[1:]))
    np_ = int(t.shape[0])
    ap = [[fsz, np_]] + [list(d) for d in dims]
    return bass.AP(t, offset, ap)


class Prog:
    def __init__(self, nrows=2048, nbatch=4, taps=(), stage=9, rd=False):
        assert nrows % (32 * nbatch) == 0 and nrows % 512 == 0
        self.stage = stage
        self.rd = rd
        self.nrows = nrows
        self.nbatch = nbatch
        self.rpb = nrows // nbatch
        self.nsub = nrows // 32
        self.ntok = nrows * 128
        self.taps = set(taps)

    def build(self):
        p = self
        NR, NS, NT = p.nrows, p.nsub, p.ntok
        NSEG = NS // 4
        L16S = min(8, NS)
        RT = NR // 128
        MT = p.rpb // 16
        nc = bass.Bass("TRN2", target_bir_lowering=False, detect_race_conditions=self.rd)

        ein, eout = "ExternalInput", "ExternalOutput"
        xin = nc.dram_tensor("xin", [NR, DM], F32, kind=ein).ap()
        xperm = nc.dram_tensor("xperm", [NR, DM], F32, kind=ein).ap()
        qmap_d = nc.dram_tensor("qmapw", [64, 128], F32, kind=ein).ap()
        kmap_d = nc.dram_tensor("kmapw", [64, 128], F32, kind=ein).ap()
        vmap_d = nc.dram_tensor("vmapw", [64, 128], F32, kind=ein).ap()
        auxin_d = nc.dram_tensor("auxin", [9, 4096], F32, kind=ein).ap()
        ident_d = nc.dram_tensor("identw", [128, 128], F32, kind=ein).ap()
        lepmap_d = nc.dram_tensor("lepmapw", [128, 64], F32, kind=ein).ap()
        projbd_d = nc.dram_tensor("projbdw", [128, 128], F32, kind=ein).ap()
        bd16_d = nc.dram_tensor("bd16w", [128, 128], F32, kind=ein).ap()
        cw_d = nc.dram_tensor("cwcols", [128, 6], F32, kind=ein).ap()
        lnc_d = nc.dram_tensor("lncols", [128, 3], F32, kind=ein).ap()
        wgu_d = nc.dram_tensor("wgu", [64 * 128, 4096], I8, kind=ein).ap()
        wdn_d = nc.dram_tensor("wdn", [4 * 32 * 128, 1024], I8, kind=ein).ap()
        sg_d = nc.dram_tensor("sgcol", [128, 64], F32, kind=ein).ap()
        su_d = nc.dram_tensor("sucol", [128, 64], F32, kind=ein).ap()
        sdn_d = nc.dram_tensor("sdnw", [128, DM], F32, kind=ein).ap()
        eps_d = nc.dram_tensor("epscol", [128, 1], F32, kind=ein).ap()

        out_d = nc.dram_tensor("out", [NR, DM], F32, kind=eout).ap()

        lz = nc.dram_tensor("lz", [NR, DM], BF16, kind="Internal").ap()
        avp = nc.dram_tensor("avp", [16, NT], BF16, kind="Internal").ap()
        lepp = nc.dram_tensor("lepp", [16, NT], BF16, kind="Internal").ap()
        attnd = nc.dram_tensor("attnd", [16, NT], BF16, kind="Internal").ap()
        x1d = nc.dram_tensor("x1d", [NR, DM], F32, kind="Internal").ap()

        tapd = {}
        def mktap(name, shape, dtype=F32):
            if name in p.taps:
                tapd[name] = nc.dram_tensor("tap_" + name, shape, dtype, kind=eout).ap()
            return tapd.get(name)

        t_lz = mktap("lz", [NR, DM])
        t_qbuf = mktap("qbuf", [128, 4096])
        t_kbuf = mktap("kbuf", [128, 4096])
        t_vsb = mktap("vsb", [128, 4096])
        t_lep = mktap("lep", [128, 4096])
        t_asb = mktap("asb", [128, 2048])
        t_att1 = mktap("att1", [128, 128])
        t_avp = mktap("avp", [16, NT])
        t_lepp = mktap("lepp", [16, NT])
        t_attn = mktap("attn", [16, NT])
        t_x1 = mktap("x1", [NR, DM])
        t_fm = mktap("fm", [128, 16 * NR])

        S = Sched()
        FMW = 16 * NR

        with contextlib.ExitStack() as _st:
            _en = _st.enter_context
            qmap = _en(nc.sbuf_tensor("c_qmap", [64, 128], BF16))
            kmap = _en(nc.sbuf_tensor("c_kmap", [64, 128], BF16))
            vmap = _en(nc.sbuf_tensor("c_vmap", [64, 128], BF16))
            ident = _en(nc.sbuf_tensor("c_ident", [128, 128], BF16))
            lepmap = _en(nc.sbuf_tensor("c_lepmap", [128, 64], BF16))
            projbd = _en(nc.sbuf_tensor("c_projbd", [128, 128], BF16))
            bd16 = _en(nc.sbuf_tensor("c_bd16", [128, 128], BF16))
            cw = _en(nc.sbuf_tensor("c_cw", [128, 6], F32))
            lnc = _en(nc.sbuf_tensor("c_lnc", [128, 3], F32))
            epsc = _en(nc.sbuf_tensor("c_eps", [128, 1], F32))
            sg = _en(nc.sbuf_tensor("c_sg", [128, 64], F32))
            su = _en(nc.sbuf_tensor("c_su", [128, 64], F32))
            sdn = _en(nc.sbuf_tensor("c_sdn", [128, DM], BF16))
            af32 = _en(nc.sbuf_tensor("af32", [128, 6176], F32))
            ab = _en(nc.sbuf_tensor("ab", [128, 75776], BF16))
            gsil = _en(nc.sbuf_tensor("gsil", [128, 512], BF16))
            wdi0 = _en(nc.sbuf_tensor("wdi0", [128, 1024], I8))
            wdi1 = _en(nc.sbuf_tensor("wdi1", [128, 1024], I8))
            wgi0 = _en(nc.sbuf_tensor("wgi0", [128, 4096], I8))
            wgi1 = _en(nc.sbuf_tensor("wgi1", [128, 4096], I8))
            rsb = _en(nc.sbuf_tensor("rsb", [128, 16], F32))
            wg0 = _en(nc.sbuf_tensor("wg0", [128, 4096], BF16))
            wg1 = _en(nc.sbuf_tensor("wg1", [128, 4096], BF16))

            pb0 = _en(nc.psum_tensor("pb0", [128, 512], F32))
            pb1 = _en(nc.psum_tensor("pb1", [128, 512], F32))
            pb2 = _en(nc.psum_tensor("pb2", [128, 512], F32))
            psgrp = _en(nc.psum_tensor("psgrp", [128, 2048], F32))
            pb7 = _en(nc.psum_tensor("pb7", [128, 512], F32))
            sdma = _en(nc.semaphore("sdma"))
            gdma = _en(nc.semaphore("gdma"))
            tdma = _en(nc.semaphore("tdma"))
            smm = _en(nc.semaphore("smm"))
            sev = _en(nc.semaphore("sev"))
            ssc = _en(nc.semaphore("ssc"))
            sgp = _en(nc.semaphore("sgp"))
            wcv = _en(nc.semaphore("wcv"))
            sbar = _en(nc.semaphore("sbar"))
            blk = _en(nc.Block())

            def A(c0_, w):
                return ab[:, c0_:c0_ + w]
            fmsb = A(0, FMW)
            B0 = FMW
            fuse = A(B0, 32768)
            qb0 = A(B0, 4096); qb1 = A(B0 + 4096, 4096)
            kb0 = A(B0 + 8192, 4096); kb1 = A(B0 + 12288, 4096)
            vsb = A(B0 + 16384, 4096); lepb = A(B0 + 20480, 4096)
            vtsb = A(B0 + 24576, 4096)
            asb0 = A(B0 + 28672, 1024); asb1 = A(B0 + 29696, 1024)

            l16a = A(B0, 4096); l16w = A(B0 + 4096, 4096)
            l16d = A(B0 + 8192, 4096); l16o = A(B0 + 12288, 4096)
            C0 = B0 + 32768
            ffnsb = A(C0, 4 * DM)
            fla = ab[0:64, C0:C0 + 4096]
            flb = ab[0:64, C0 + 4096:C0 + 8192]
            rowsb = A(C0 + 8192, 2048)
            att1 = A(C0 + 8192, 128)
            cmsb = A(C0 + 8320, 128)
            avps = A(C0 + 8448, 128)
            lepps = A(C0 + 8576, 512)
            wd0 = A(C0 + 8192, 1024)
            wd1 = A(C0 + 9216, 1024)
            rows = af32[:, 0:2048]
            rows2 = af32[:, 2048:4096]
            rows3 = af32[:, 4096:6144]
            stat = af32[:, 6144:6176]
            l16r = af32[:, 0:4096]

            psq, psk, psv = pb0, pb1, pb2
            psav, psm, psm2 = pb0, pb2, pb7
            pstr = pb1[:, :].bitcast(BF16)

            nbar = [0]

            def barrier():
                nbar[0] += 1
                tgt = 5 * nbar[0]
                ws = {"sync": [S.at(sdma)], "gpsimd": [S.at(gdma)],
                      "scalar": [S.at(tdma)], "tensor": [], "vector": []}
                for en in ("sync", "gpsimd", "tensor", "vector", "scalar"):
                    S.emit(en, lambda e: e.nop(), ws[en], [S.inc(sbar, 1)])
                for en in ("sync", "gpsimd", "tensor", "vector", "scalar"):
                    S.emit(en, lambda e: e.nop(), [(sbar, tgt)], [])

            def dma(engine, sem, dst, src, waits=()):
                S.emit(engine, lambda e, d=dst, s=src: e.dma_start(out=d, in_=s),
                       waits, [S.inc(sem, 16)])
                return S.at(sem)

            for d, s_ in ((qmap, qmap_d), (kmap, kmap_d), (vmap, vmap_d),
                          (ident, ident_d), (lepmap, lepmap_d),
                          (projbd, projbd_d), (bd16, bd16_d),
                          (sdn, sdn_d)):
                dma("gpsimd", gdma, d[:], s_)
            for d, s_ in ((cw, cw_d), (lnc, lnc_d), (sg, sg_d), (su, su_d), (epsc, eps_d)):
                dma("sync", sdma, d[:], s_)
            S.emit("vector", lambda e: e.memset(fla[:], 0.0), [], [S.inc(sev, 1)])
            S.emit("vector", lambda e: e.memset(flb[:], 0.0), [], [S.inc(sev, 1)])
            mz = S.at(sev)
            for buf in (fla, flb):
                for st in (0, 32):
                    dma("gpsimd", gdma, buf[st + 16:st + 25, :], auxin_d, [mz])
            barrier()

            def ln_stats(src, junk, waits=()):
                waits = list(waits)
                S.emit("vector", lambda e: e.reduce_sum(stat[:, 0:1], src, AX.X),
                       waits + [S.at(ssc)], [S.inc(sev, 1)])
                S.emit("scalar", lambda e: e.activation(
                    junk, src, AF.Square, accum_out=stat[:, 1:2]),
                    waits + [S.at(sev)], [S.inc(ssc, 1)])
                S.emit("vector", lambda e: e.tensor_scalar(
                    stat[:, 2:4], stat[:, 0:2], 1.0 / DM, None, ALU.mult),
                    [S.at(ssc), S.at(sev)], [S.inc(sev, 1)])
                S.emit("vector", lambda e: e.tensor_tensor(
                    stat[:, 4:5], stat[:, 2:3], stat[:, 2:3], ALU.mult),
                    [S.at(sev)], [S.inc(sev, 1)])
                S.emit("vector", lambda e: e.tensor_sub(
                    stat[:, 5:6], stat[:, 3:4], stat[:, 4:5]), [S.at(sev)],
                    [S.inc(sev, 1)])
                S.emit("scalar", lambda e: e.activation(
                    stat[:, 6:7], stat[:, 5:6], AF.Ln, bias=epsc[:, 0:1]),
                    [S.at(sev)], [S.inc(ssc, 1)])
                S.emit("scalar", lambda e: e.activation(
                    stat[:, 7:8], stat[:, 6:7], AF.Exp, scale=-0.5), [S.at(ssc)],
                    [S.inc(ssc, 1)])
                return S.at(ssc)

            for rt in range(RT):
                w_ld = dma("sync", sdma, rows[:], xin[rt * 128:(rt + 1) * 128, :],
                           [S.at(sev)])
                w_rstd = ln_stats(rows[:], rows2[:], [w_ld])
                S.emit("vector", lambda e: e.tensor_scalar(
                    rowsb.rearrange("p (c w) -> p w c", c=16),
                    rows.rearrange("p (w c) -> p w c", c=16),
                    stat[:, 2:3], stat[:, 7:8], ALU.subtract, ALU.mult),
                    [w_rstd, S.at(sdma), S.at(gdma)], [S.inc(sev, 1)])
                dma("sync", sdma, lz[rt * 128:(rt + 1) * 128, :], rowsb[:],
                    [S.at(sev)])
                if t_lz is not None:
                    dma("gpsimd", gdma, t_lz[rt * 128:(rt + 1) * 128, :], rowsb[:],
                        [S.at(sev)])
            barrier()

            def lz_cAP(sub):
                return lz.rearrange("r (c w) -> c r w", c=16)[
                    :, sub * 32:(sub + 1) * 32, :]

            def sig_out(buf, cc):
                v = buf.rearrange("p (a b d e) -> p a b d e", a=4, b=16, d=8, e=8)
                return v[:, :, :, cc, :]

            w_qkv_mm_end = None
            w_exp_par = [None, None]
            for seg in range(NSEG if p.stage >= 2 else 0):
                sub0 = seg * 4
                wfl = []
                flw = [w_qkv_mm_end] if w_qkv_mm_end else []
                for ipair, buf in ((0, fla), (1, flb)):
                    for sl_ in range(2):
                        sub = sub0 + 2 * ipair + sl_
                        wfl.append(dma("gpsimd", gdma,
                                       buf[sl_ * 32:sl_ * 32 + 16, :], lz_cAP(sub),
                                       flw))
                psq_r = [psq[:, 0:512], psgrp[:, 0:512]]
                psk_r = [psk[:, 0:512], psgrp[:, 512:1024]]
                psv_r = [psv, psgrp[:, 1024:1536]]
                wq_war = [None, None]
                wk_war = [None, None]
                wv_war = [None, None]
                first_ev = True
                for ipair, buf, qb, kb in ((0, fla, qb0, kb0), (1, flb, qb1, kb1)):
                    for cc in range(8):
                        par = cc % 2
                        gcc = ipair * 8 + cc
                        sl_ = slice(cc * 512, (cc + 1) * 512)
                        qw = [wfl[2 * ipair], wfl[2 * ipair + 1]]
                        if gcc == 0:
                            qw.append(S.at(sev))
                        elif gcc == 1:
                            qw += [w_exp_par[0]] if w_exp_par[0] else []
                        qw += wq_war[par] or []
                        S.emit("tensor", lambda e, b=buf, s=sl_, pp=psq_r[par]:
                               e.matmul(pp, qmap[:], b[:, s],
                                        start=True, stop=True),
                               qw, [S.inc(smm, 1)])
                        S.emit("tensor", lambda e, b=buf, s=sl_, pp=psk_r[par]:
                               e.matmul(pp, kmap[:], b[:, s],
                                        start=True, stop=True),
                               wk_war[par] or [], [S.inc(smm, 1)])
                        wmm = S.at(smm)
                        evw = [wmm, S.at(smm)] if first_ev else [wmm]
                        first_ev = False
                        S.emit("vector", lambda e, d_=qb, c=cc, pp=psq_r[par]:
                               e.tensor_copy(sig_out(d_, c), pp), evw,
                               [S.inc(sev, 1)])
                        wq_war[par] = [S.at(sev)]
                        S.emit("vector", lambda e, d_=kb, c=cc, pp=psk_r[par]:
                               e.tensor_copy(sig_out(d_, c), pp), [],
                               [S.inc(sev, 1)])
                        wk_war[par] = [S.at(sev)]
                for cc in range(8):
                    par = cc % 2
                    sl_ = slice(cc * 512, (cc + 1) * 512)
                    vw = []
                    if cc == 0:
                        vw.append(S.at(sev))
                    elif cc == 1:
                        vw += [w_exp_par[1]] if w_exp_par[1] else []
                    vw += wv_war[par] or []
                    S.emit("tensor", lambda e, s=sl_, pp=psv_r[par]: e.matmul(
                        pp[0:64, 0:512], vmap[0:64, 0:64], fla[:, s],
                        start=True, stop=True), vw, [S.inc(smm, 1)])
                    S.emit("tensor", lambda e, s=sl_, pp=psv_r[par]: e.matmul(
                        pp[64:128, 0:512], vmap[0:64, 0:64], flb[:, s],
                        start=True, stop=True, tile_position=(0, 64)),
                        [], [S.inc(smm, 1)])
                    S.emit("vector", lambda e, c=cc, pp=psv_r[par]: e.tensor_copy(
                        sig_out(vsb, c), pp[:, 0:512]), [S.at(smm)],
                        [S.inc(sev, 1)])
                    wv_war[par] = [S.at(sev)]
                w_qkv_mm_end = S.at(smm)
                w_qkv = S.at(sev)
                w_psgrp_extra = {0: [w_qkv], 1: [w_qkv]}
                if t_qbuf is not None and seg == 0:
                    dma("gpsimd", gdma, t_qbuf, qb0[:], [w_qkv])
                    dma("gpsimd", gdma, t_kbuf, kb0[:], [w_qkv])
                    dma("gpsimd", gdma, t_vsb, vsb[:], [w_qkv])

                for b4 in range(8):
                    for j in range(4):
                        beta = b4 * 4 + j
                        S.emit("tensor", lambda e, b=beta, jj=j: e.transpose(
                            pstr[:, jj * 128:(jj + 1) * 128],
                            vsb[:, b * 128:(b + 1) * 128], ident[:]),
                            [w_qkv] + ([S.at(sev)] if j == 0 else []),
                            [S.inc(smm, 1)])
                    S.emit("vector", lambda e, b=b4: e.tensor_copy(
                        vtsb[:, b * 512:(b + 1) * 512], pstr[:, 0:512]),
                        [S.at(smm)], [S.inc(sev, 1)])
                w_vt = S.at(sev)
                S.emit("vector", lambda e: e.tensor_scalar(
                    lepb[:], vsb[:], cw[:, 0:1], cw[:, 1:2], ALU.mult, ALU.add),
                    [w_qkv, S.at(smm)], [S.inc(sev, 1)])
                vv = vsb.rearrange("p (w e) -> p w e", e=8)
                ll = lepb.rearrange("p (w e) -> p w e", e=8)
                S.emit("vector", lambda e: e.scalar_tensor_tensor(
                    ll[:, :, 1:8], vv[:, :, 0:7], cw[:, 2:3], ll[:, :, 1:8],
                    ALU.mult, ALU.add), [S.at(sev)], [S.inc(sev, 1)])
                S.emit("vector", lambda e: e.scalar_tensor_tensor(
                    ll[:, :, 0:7], vv[:, :, 1:8], cw[:, 3:4], ll[:, :, 0:7],
                    ALU.mult, ALU.add), [S.at(sev)], [S.inc(sev, 1)])
                vv8 = vsb.rearrange("p (b d e) -> p b d e", d=8, e=8)
                ll8 = lepb.rearrange("p (b d e) -> p b d e", d=8, e=8)
                S.emit("vector", lambda e: e.scalar_tensor_tensor(
                    ll8[:, :, 1:8, :], vv8[:, :, 0:7, :], cw[:, 4:5],
                    ll8[:, :, 1:8, :], ALU.mult, ALU.add), [S.at(sev)],
                    [S.inc(sev, 1)])
                S.emit("vector", lambda e: e.scalar_tensor_tensor(
                    ll8[:, :, 0:7, :], vv8[:, :, 1:8, :], cw[:, 5:6],
                    ll8[:, :, 0:7, :], ALU.mult, ALU.add), [S.at(sev)],
                    [S.inc(sev, 1)])
                w_lep = S.at(sev)
                if t_lep is not None and seg == 0:
                    dma("gpsimd", gdma, t_lep, lepb[:], [w_lep])

                for s4 in range(4 if p.stage >= 3 else 0):
                    sub = sub0 + s4
                    qb, kb = (qb0, kb0) if s4 < 2 else (qb1, kb1)
                    st1 = (s4 % 2) * 64
                    st2 = st1 + 32

                    def emit_grams(bg):
                        half = bg % 2
                        hof = half * 1024
                        a_sb = asb0 if half == 0 else asb1
                        w_war = [w_exp_par[half]] if w_exp_par[half] else []
                        w_war += w_psgrp_extra.pop(half, [])
                        for j in range(4):
                            beta = bg * 4 + j
                            csl = slice(beta * 128, (beta + 1) * 128)
                            for br, st in ((0, st1), (1, st2)):
                                u = br * 4 + j
                                S.emit("tensor", lambda e, k_=kb, q_=qb, s=csl,
                                       r0=st, uu=u, ho=hof: e.matmul(
                                    psgrp[:, ho + uu * 128:ho + (uu + 1) * 128],
                                    k_[r0:r0 + 20, s], q_[r0:r0 + 20, s],
                                    start=True, stop=True, tile_position=(r0, 0)),
                                    w_war if (j == 0 and br == 0) else [],
                                    [S.inc(smm, 1)])
                        S.emit("scalar", lambda e, ho=hof, ab_=a_sb: e.activation(
                            ab_[:, 0:1024], psgrp[:, ho:ho + 1024], AF.Exp,
                            scale=SQK), [S.at(smm)], [S.inc(ssc, 1)])
                        w_exp_par[half] = S.at(ssc)
                        return (bg, S.at(ssc), a_sb)

                    def emit_av(info):
                        bg, w_exp, a_sb = info
                        if t_asb is not None and seg == 0 and s4 == 0 and bg < 2:
                            dma("gpsimd", gdma, t_asb[:, bg * 1024:(bg + 1) * 1024],
                                a_sb[:, 0:1024], [w_exp])
                        for j in range(4):
                            beta = bg * 4 + j
                            jj = (bg % 2) * 4 + j
                            for br in range(2):
                                u = br * 4 + j
                                r0_ = beta * 128 + s4 * 32 + br * 8
                                S.emit("tensor", lambda e, uu=u, jx=jj, b_=br,
                                       r=r0_, ab_=a_sb: e.matmul(
                                    psav[:, jx * 18 + b_ * 9:jx * 18 + b_ * 9 + 9],
                                    ab_[:, uu * 128:(uu + 1) * 128],
                                    vtsb[:, r:r + 9], start=True, stop=True),
                                    [w_exp, w_vt, S.at(sev)]
                                    if (j == 0 and br == 0) else [],
                                    [S.inc(smm, 1)])

                    def emit_ppd(bgodd):
                        w_av = S.at(smm)
                        S.emit("vector", lambda e: e.reciprocal(
                            rsb[:, 0:16].rearrange("p (a b) -> p a b", b=2),
                            apx(psav, 8, [[18, 8], [1, 2]])), [w_av],
                            [S.inc(sev, 1)])
                        S.emit("vector", lambda e: e.tensor_tensor(
                            att1.rearrange("p (j b c) -> p j b c", b=2, c=8),
                            apx(psav, 0, [[18, 8], [10, 2], [1, 8]]),
                            apx(rsb, 0, [[2, 8], [1, 2], [0, 8]]),
                            ALU.mult), [S.at(smm), S.at(sev)], [S.inc(sev, 1)])
                        w_n = S.at(sev)
                        if t_att1 is not None and seg == 0 and s4 == 0 and bgodd == 1:
                            dma("gpsimd", gdma, t_att1, att1[:], [w_n])
                        return (bgodd, w_n)

                    def emit_ppt(info):
                        bgodd, w_n = info
                        S.emit("tensor", lambda e: e.transpose(
                            pstr[:, 512:640], att1[:], ident[:]),
                            [w_n], [S.inc(smm, 1)])
                        S.emit("vector", lambda e: e.tensor_copy(
                            cmsb[:], pstr[:, 512:640]), [S.at(smm)],
                            [S.inc(sev, 1)])
                        S.emit("tensor", lambda e: e.matmul(
                            psm2[:, 0:128], projbd[:], cmsb[:],
                            start=True, stop=True), [S.at(sev)], [S.inc(smm, 1)])
                        S.emit("vector", lambda e: e.tensor_copy(
                            avps[:], psm2[:, 0:128]), [S.at(smm), S.at(sdma)],
                            [S.inc(sev, 1)])
                        bgp = bgodd // 2
                        dst = avp.rearrange("c (u b t) -> u b c t", u=NS, t=128)[
                            sub, bgp * 8:(bgp + 1) * 8, :, :]
                        dma("sync", sdma, dst, avps[:], [S.at(sev)])

                    def emit_projlep(cc):
                        sl_ = slice(cc * 512, (cc + 1) * 512)
                        S.emit("tensor", lambda e, s=sl_: e.matmul(
                            psm[0:64, 0:512], lepmap[:], lepb[:, s], start=True,
                            stop=True), [w_lep, w_qkv, S.at(sev)],
                            [S.inc(smm, 1)])
                        S.emit("vector", lambda e: e.tensor_copy(
                            lepps[0:64, :], psm[0:64, 0:512]),
                            [S.at(smm), S.at(sdma)], [S.inc(sev, 1)])
                        dst = lepp.rearrange("c (u t) -> u c t", t=4096)[
                            sub0:sub0 + 4, :, cc * 512:(cc + 1) * 512]
                        dma("sync", sdma, dst, lepps[0:64, :], [S.at(sev)])

                    pend = None
                    ptp = None
                    for bg in range(8):
                        info = emit_grams(bg)
                        if pend is not None:
                            emit_av(pend)
                            if pend[0] % 2 == 1:
                                if ptp is not None:
                                    emit_ppt(ptp)
                                ptp = emit_ppd(pend[0])
                                if pend[0] in (1, 3):
                                    emit_projlep(s4 * 2 + pend[0] // 2)
                        pend = info
                    emit_av(pend)
                    emit_ppt(ptp)
                    emit_ppt(emit_ppd(7))


            barrier()

            if t_avp is not None:
                dma("gpsimd", gdma, t_avp, avp)
            if t_lepp is not None:
                dma("gpsimd", gdma, t_lepp, lepp)

            for g in range(NS // L16S if p.stage >= 7 else 0):
                u0 = g * L16S
                pr = L16S * 16
                srcA = avp.rearrange("c (u t) -> u c t", t=4096)[u0:u0 + L16S]
                srcB = lepp.rearrange("c (u t) -> u c t", t=4096)[u0:u0 + L16S]
                wa = dma("gpsimd", gdma, l16a[0:pr, :], srcA, [S.at(sev)])
                wb_ = dma("gpsimd", gdma, l16d[0:pr, :], srcB, [S.at(sev)])
                S.emit("vector", lambda e: e.scalar_tensor_tensor(
                    l16w[0:pr, :], l16a[0:pr, :], lnc[0:pr, 2:3], l16d[0:pr, :],
                    ALU.add, ALU.add), [wa, wb_], [S.inc(sev, 1)])
                w_w = S.at(sev)
                for cc in range(8):
                    sl_ = slice(cc * 512, (cc + 1) * 512)
                    S.emit("tensor", lambda e, s=sl_: e.matmul(
                        psm[:, 0:512], bd16[:], l16w[:, s], start=True, stop=True),
                        [w_w, S.at(sev)], [S.inc(smm, 1)])
                    S.emit("vector", lambda e, s=sl_: e.tensor_sub(
                        l16d[:, s], l16w[:, s], psm[:, 0:512]), [S.at(smm)],
                        [S.inc(sev, 1)])
                    S.emit("scalar", lambda e, s=sl_: e.activation(
                        l16a[:, s], l16d[:, s], AF.Square), [S.at(sev)],
                        [S.inc(ssc, 1)])
                    S.emit("tensor", lambda e, s=sl_: e.matmul(
                        psm2[:, 0:512], bd16[:], l16a[:, s], start=True,
                        stop=True), [S.at(ssc)], [S.inc(smm, 1)])
                    S.emit("scalar", lambda e, s=sl_: e.activation(
                        l16r[:, s], psm2[:, 0:512], AF.Ln, bias=epsc[:, 0:1]),
                        [S.at(smm)], [S.inc(ssc, 1)])
                    S.emit("scalar", lambda e, s=sl_: e.activation(
                        l16r[:, s], l16r[:, s], AF.Exp, scale=-0.5), [S.at(ssc)],
                        [S.inc(ssc, 1)])
                    S.emit("vector", lambda e, s=sl_: e.tensor_mul(
                        l16d[:, s], l16d[:, s], l16r[:, s]), [S.at(ssc)],
                        [S.inc(sev, 1)])
                    lam = cc
                    o4 = l16o.rearrange("p (d cz wh wl e) -> p d cz wh wl e",
                                        d=8, cz=4, wh=2, wl=8, e=8)
                    ov = o4[:, :, lam >> 1, lam & 1, :, :].transpose([0, 2, 1, 3])
                    S.emit("vector", lambda e, o=ov, s=sl_: e.tensor_scalar(
                        o, l16d[:, s].rearrange("p (w d e) -> p w d e", d=8, e=8),
                        lnc[:, 0:1], lnc[:, 1:2], ALU.mult, ALU.add),
                        [S.at(sev)] + ([S.at(sdma)] if cc == 0 else []),
                        [S.inc(sev, 1)])
                dst = attnd.rearrange("c (u t) -> u c t", t=4096)[u0:u0 + L16S]
                dma("sync", sdma, dst, l16o[0:pr, :], [S.at(sev)])
            barrier()
            if t_attn is not None:
                dma("gpsimd", gdma, t_attn, attnd)
                barrier()

            ncv = 128 // MT
            p3s1 = A(B0, 2048)
            p3s2 = A(B0 + 2048, 2048)
            fm_rdy = []
            x1_rdy = []
            for rt in range(RT if p.stage >= 8 else 0):
                r0 = rt * 128
                bi = r0 // p.rpb
                ci = (r0 % p.rpb) // MT
                w_pm = []
                for j in range(ncv):
                    c = ci + j
                    asrc = attnd[c:c + 1, :].rearrange(
                        "o (b m w) -> o b m w", b=p.nbatch, w=DM)[0, bi]
                    w_pm.append(dma("sync", sdma,
                                    p3s1[j * MT:(j + 1) * MT, :], asrc,
                                    [S.at(sev)]))
                    lsrc = lz.rearrange("(b r) (c w) -> b r c w",
                                        b=p.nbatch, c=16)[bi, :, c, :].rearrange(
                        "(m h) w -> m h w", h=16)
                    w_pm.append(dma("scalar", tdma,
                                    p3s2[j * MT:(j + 1) * MT, :].rearrange(
                                        "p (h w) -> p h w", w=128), lsrc,
                                    [S.at(sev)]))
                w_x0 = dma("sync", sdma, rows[:], xperm[r0:r0 + 128, :],
                           [S.at(sev)])
                S.emit("vector", lambda e: e.tensor_add(
                    rows2[:], p3s1[:], p3s2[:]), w_pm, [S.inc(sev, 1)])
                w_rstd = ln_stats(rows2[:], rows3[:], [S.at(sev)])
                S.emit("vector", lambda e: e.tensor_scalar(
                    rows2[:], rows2[:], stat[:, 2:3], stat[:, 7:8],
                    ALU.subtract, ALU.mult), [w_rstd], [S.inc(sev, 1)])
                S.emit("vector", lambda e: e.tensor_add(
                    rows[:], rows[:], rows2[:]), [w_x0, S.at(sev)],
                    [S.inc(sev, 1)])
                w_x1 = S.at(sev)
                w_x1d = dma("sync", sdma, x1d[r0:r0 + 128, :], rows[:], [w_x1])
                if t_x1 is not None:
                    dma("sync", sdma, t_x1[r0:r0 + 128, :], rows[:], [w_x1])
                w_rstd = ln_stats(rows[:], rows3[:])
                S.emit("vector", lambda e: e.tensor_scalar(
                    rowsb[:], rows[:], stat[:, 2:3], stat[:, 7:8],
                    ALU.subtract, ALU.mult), [w_rstd, S.at(tdma)],
                    [S.inc(sev, 1)])
                w_ln2 = S.at(sev)
                for kc in range(16):
                    S.emit("scalar", lambda e, k=kc, r=rt: e.dma_start_transpose(
                        out=fmsb[:, k * NR + r * 128:k * NR + (r + 1) * 128],
                        in_=rowsb[:, k * 128:(k + 1) * 128]),
                        [w_ln2], [S.inc(tdma, 16)])
                x1_rdy.append(w_x1d)
                if rt % 4 == 3:
                    fm_rdy.append(S.at(tdma))
            w_p3t = S.at(tdma) if (RT if p.stage >= 8 else 0) else None
            barrier()
            if t_fm is not None:
                dma("gpsimd", gdma, t_fm, fmsb[:], [w_p3t])
                barrier()

            NTT = (NR // 512) if p.stage >= 9 else 0
            wgub = [wg0, wg1]
            wgi = [wgi0, wgi1]
            wdb = [wd0, wd1]
            wdi = [wdi0, wdi1]
            w_giwar = [[], []]
            w_gcast = [[], []]
            w_gmm = [[], []]
            w_wiwar = [[], []]
            w_wdwar = [[w_p3t], [w_p3t]] if p.stage >= 8 else [[], []]
            psg_par = [pb0, pb2]
            psu_par = [pb1, pb7]
            w_sil_par = [None, None]
            w_stt_par = [None, None]
            gc = {}

            def g_loadcast(tt, ic):
                par = ic % 2
                wla = dma("sync", sdma, wgi[par][:, 0:2048],
                          wgu_d[ic * 128:(ic + 1) * 128, 0:2048],
                          w_giwar[par])
                wlb = dma("scalar", tdma, wgi[par][:, 2048:4096],
                          wgu_d[ic * 128:(ic + 1) * 128, 2048:4096],
                          w_giwar[par])
                S.emit("vector", lambda e, b=par: e.tensor_copy(
                    wgub[b][:], wgi[b][:]), [wla, wlb] + w_gmm[par],
                    [S.inc(sev, 1)])
                w_giwar[par] = [S.at(sev)]
                gc[(tt, ic)] = S.at(sev)

            for tt in range(NTT):
                g_loadcast(tt, 0)
                g_loadcast(tt, 1)
                for ic in range(64):
                    par = ic % 2
                    pg, pu = psg_par[par], psu_par[par]
                    war = [w for w in (w_sil_par[par], w_stt_par[par]) if w]
                    for kc in range(16):
                        S.emit("tensor", lambda e, k=kc, b=par, t=tt, pp=pg: e.matmul(
                            pp[:, 0:512], wgub[b][:, k * 128:(k + 1) * 128],
                            fmsb[:, k * NR + t * 512:k * NR + (t + 1) * 512],
                            start=(k == 0), stop=(k == 15)),
                            ([gc[(tt, ic)]] + war +
                             ([fm_rdy[tt]] if ic == 0 else []))
                            if kc == 0 else [], [])
                        S.emit("tensor", lambda e, k=kc, b=par, t=tt, pp=pu: e.matmul(
                            pp[:, 0:512], wgub[b][:, 2048 + k * 128:2048 + (k + 1) * 128],
                            fmsb[:, k * NR + t * 512:k * NR + (t + 1) * 512],
                            start=(k == 0), stop=(k == 15)),
                            [], [S.inc(smm, 1)] if kc == 15 else [])
                    wmm = S.at(smm)
                    w_gmm[par] = [wmm]
                    S.emit("scalar", lambda e, i=ic, pp=pg: e.activation(
                        gsil[:], pp[:, 0:512], AF.Silu, scale=sg[:, i:i + 1]),
                        [wmm, S.at(sev)], [S.inc(ssc, 1)])
                    w_sil_par[par] = S.at(ssc)
                    if ic + 2 < 64:
                        g_loadcast(tt, ic + 2)
                    S.emit("vector", lambda e, i=ic, pp=pu: e.scalar_tensor_tensor(
                        fuse[:, i * 512:(i + 1) * 512], pp[:, 0:512],
                        su[:, i:i + 1], gsil[:], ALU.mult, ALU.mult),
                        [S.at(ssc)], [S.inc(sev, 1)])
                    w_stt_par[par] = S.at(sev)
                w_fuse = S.at(sev)
                for oc in range(4):
                    for icp in range(32):
                        par = icp % 2
                        wld = dma("sync" if par == 0 else "scalar",
                                  sdma if par == 0 else tdma, wdi[par][:],
                                  wdn_d[(oc * 32 + icp) * 128:
                                        (oc * 32 + icp + 1) * 128, :],
                                  w_wiwar[par])
                        S.emit("vector", lambda e, b=par: e.tensor_copy(
                            wdb[b][:], wdi[b][:]), [wld] + w_wdwar[par],
                            [S.inc(sev, 1)])
                        w_wiwar[par] = [S.at(sev)]
                        wcp = S.at(sev)
                        for half in range(2):
                            ic = icp * 2 + half
                            for tb in range(4):
                                S.emit("tensor", lambda e, i=ic, t=tb, b=par,
                                       h=half: e.matmul(
                                    psgrp[:, t * 512:(t + 1) * 512],
                                    fuse[:, i * 512 + t * 128:i * 512 + (t + 1) * 128],
                                    wdb[b][:, h * 512:(h + 1) * 512],
                                    start=(i == 0), stop=(i == 63)),
                                    ([wcp, w_fuse, S.at(sev)]
                                     if (icp == 0 and half == 0 and tb == 0) else
                                     ([wcp] if (half == 0 and tb == 0) else [])),
                                    [S.inc(smm, 1)] if (half == 1 and tb == 3)
                                    else [])
                        w_wdwar[par] = [S.at(smm)]
                    wmm = S.at(smm)
                    for tb in range(4):
                        S.emit("vector", lambda e, t=tb, o=oc: e.tensor_copy(
                            ffnsb[:, t * DM + o * 512:t * DM + (o + 1) * 512],
                            psgrp[:, t * 512:(t + 1) * 512]), [wmm],
                            [S.inc(sev, 1)])
                w_dn = S.at(sev)

                for tb in range(4):
                    rt = tt * 4 + tb
                    r0 = rt * 128
                    S.emit("vector", lambda e, t=tb: e.tensor_mul(
                        rows2[:], ffnsb[:, t * DM:(t + 1) * DM], sdn[:]),
                        [w_dn], [S.inc(sev, 1)])
                    w_xr = dma("gpsimd", gdma, rows[:], x1d[r0:r0 + 128, :],
                               [S.at(sev), x1_rdy[rt]])
                    w_rstd = ln_stats(rows2[:], rows3[:], [S.at(sev)])
                    S.emit("vector", lambda e: e.tensor_scalar(
                        rows2[:], rows2[:], stat[:, 2:3], stat[:, 7:8],
                        ALU.subtract, ALU.mult), [w_rstd], [S.inc(sev, 1)])
                    S.emit("vector", lambda e: e.tensor_add(
                        rows[:], rows[:], rows2[:]), [w_xr, S.at(sev)],
                        [S.inc(sev, 1)])
                    dma("gpsimd", gdma, out_d[r0:r0 + 128, :], rows[:],
                        [S.at(sev)])
            S.emit("sync", lambda e: e.nop(), [S.at(sdma)], [])
            S.emit("gpsimd", lambda e: e.nop(), [S.at(gdma)], [])
            S.emit("scalar", lambda e: e.nop(), [S.at(tdma)], [])

            @blk.sync
            def _(eng):
                S.run("sync", eng)
            @blk.gpsimd
            def _(eng):
                S.run("gpsimd", eng)
            @blk.tensor
            def _(eng):
                S.run("tensor", eng)
            @blk.vector
            def _(eng):
                S.run("vector", eng)
            @blk.scalar
            def _(eng):
                S.run("scalar", eng)
        return nc, tapd



def host_consts(inputs):
    qkv_w = np.asarray(inputs["L0_attn_qkv_w"], np.float32)
    proj_w = np.asarray(inputs["L0_attn_proj_w"], np.float32)
    qm, km = qk_maps(qkv_w)
    wgu, wdn = ffn_weight_tiles(np.asarray(inputs["L0_ffn_w_gate"]),
                                np.asarray(inputs["L0_ffn_w_up"]),
                                np.asarray(inputs["L0_ffn_w_down"]))
    return {
        "qmapw": qm, "kmapw": km, "vmapw": v_map(qkv_w),
        "auxin": aux_input_rows(),
        "identw": np.eye(128, dtype=np.float32),
        "lepmapw": lep_proj_map(proj_w),
        "projbdw": proj_bd8(proj_w),
        "bd16w": bd16_avg(),
        "cwcols": cw_cols(np.asarray(inputs["L0_attn_conv_w0"], np.float32),
                          np.asarray(inputs["L0_attn_conv_b0"], np.float32),
                          np.asarray(inputs["L0_attn_conv_w1"], np.float32),
                          np.asarray(inputs["L0_attn_conv_b1"], np.float32)),
        "lncols": ln_cols(np.asarray(inputs["L0_attn_ln_w"], np.float32),
                          np.asarray(inputs["L0_attn_ln_b"], np.float32),
                          np.asarray(inputs["L0_attn_proj_b"], np.float32)),
        "sgcol": np.ascontiguousarray(
            np.asarray(inputs["L0_ffn_s_gate"], np.float32).reshape(64, 128).T),
        "sucol": np.ascontiguousarray(
            np.asarray(inputs["L0_ffn_s_up"], np.float32).reshape(64, 128).T),
        "epscol": np.full((128, 1), EPS, np.float32),
        "sdnw": np.ascontiguousarray(np.broadcast_to(np.asarray(inputs["L0_ffn_s_down"], np.float32), (128, DM))),
        "wgu": wgu, "wdn": wdn,
    }


def perm_rows(k, nbatch=4, rpb=512):
    mt = rpb // 16
    ridx = np.empty(nbatch * rpb, np.int64)
    i = 0
    for b in range(nbatch):
        for c in range(16):
            for m in range(mt):
                ridx[i] = b * 4096 + 256 * c + mt * k + m
                i += 1
    return ridx


def kernel(**inputs):
    x = np.asarray(inputs["L0_attn_x"], np.float32)
    B, H, Wd = x.shape
    consts = host_consts(inputs)
    p = Prog(nrows=2048, nbatch=4)
    nc, _ = p.build()
    xf = x.reshape(B * H, Wd)
    in_maps = []
    for k in range(8):
        xin = np.ascontiguousarray(np.concatenate(
            [x[b, 512 * k:512 * (k + 1), :] for b in range(B)], axis=0))
        ridx = perm_rows(k)
        xperm = np.ascontiguousarray(xf[ridx])
        m = {"xin": xin, "xperm": xperm}
        m.update(consts)
        in_maps.append(m)
    res = run_bass_kernel_spmd(nc, in_maps, core_ids=list(range(8)))
    out = np.zeros((B * H, Wd), np.float32)
    for k in range(8):
        out[perm_rows(k)] = res.results[k]["out"]
    return out.reshape(B, H, Wd)


# version 16
# speedup vs baseline: 1.1283x; 1.0245x over previous
import contextlib
import numpy as np
import ml_dtypes
import concourse.bass as bass
import concourse.mybir as mybir
from concourse.bass_utils import run_bass_kernel_spmd

F32 = mybir.dt.float32
BF16 = mybir.dt.bfloat16
I8 = mybir.dt.int8
AF = mybir.ActivationFunctionType
ALU = mybir.AluOpType
AX = mybir.AxisListType
bfd = ml_dtypes.bfloat16

MB = 100.0
EPS = 1e-5
DM = 2048
INTER = 8192
SQK = 8.0 ** -0.5


def sigma_perm(n=4096):
    t = np.arange(n)
    ht, wb = t // 128, t % 128
    d, c0 = ht // 4, ht % 4
    wbb, e = wb // 8, wb % 8
    return ((c0 * 16 + wbb) * 8 + d) * 8 + e

SIG = sigma_perm()

def aux_input_rows():
    tss = SIG[np.arange(4096)]
    w1 = (tss >> 3) & 15
    w2 = (tss & 7) | (((tss >> 6) & 1) << 3)
    rows = [np.ones(4096)]
    rows += [(w1 >> k) & 1 for k in range(4)]
    rows += [(w2 >> k) & 1 for k in range(4)]
    return np.asarray(rows, dtype=np.float32)

def qk_maps(qkv_w):
    maps = []
    for which in (0, 1):
        m = np.zeros((64, 128), np.float32)
        wsub = qkv_w[:, which * 16:(which + 1) * 16]
        for s in range(2):
            r0, cc = 32 * s, 64 * s
            for br in range(2):
                cb = cc + 32 * br
                m[r0:r0 + 16, cb:cb + 8] = wsub[:, br * 8:br * 8 + 8]
                for k in range(4):
                    bitrow = r0 + 17 + 4 * br + k
                    onerow = r0 + 16
                    a, b_, c_ = cb + 8 + 3 * k, cb + 9 + 3 * k, cb + 10 + 3 * k
                    if which == 0:
                        m[bitrow, a] = -MB
                        m[onerow, b_] = -MB
                        m[bitrow, c_] = 2 * MB
                    else:
                        m[onerow, a] = 1.0
                        m[bitrow, b_] = 1.0
                        m[bitrow, c_] = 1.0
        maps.append(m)
    return maps

def v_map(qkv_w):
    m = np.zeros((64, 128), np.float32)
    wv = qkv_w[:, 32:48]
    for s in range(2):
        r0, cc = 32 * s, 32 * s
        m[r0:r0 + 16, cc:cc + 8] = wv[:, 0:8]
        m[r0 + 16, cc + 8] = 1.0
        m[r0:r0 + 16, cc + 9:cc + 17] = wv[:, 8:16]
    return m

def lep_proj_map(proj_w):
    m = np.zeros((128, 64), np.float32)
    for s in range(4):
        r0, cc = 32 * s, 16 * s
        m[r0:r0 + 8, cc:cc + 16] = proj_w[0:8, :]
        m[r0 + 9:r0 + 17, cc:cc + 16] = proj_w[8:16, :]
    return m

def proj_bd8(proj_w):
    m = np.zeros((128, 128), np.float32)
    for g in range(8):
        m[g * 16:(g + 1) * 16, g * 16:(g + 1) * 16] = proj_w
    return m

def bd16_avg():
    m = np.zeros((128, 128), np.float32)
    for g in range(8):
        m[g * 16:(g + 1) * 16, g * 16:(g + 1) * 16] = 1.0 / 16.0
    return m

def cw_cols(cw0, cb0, cw1, cb1):
    cols = np.zeros((128, 6), np.float32)
    for s in range(4):
        r0 = 32 * s
        for c in range(8):
            cols[r0 + c, 0] = cw0[c, 0, 1, 1]
            cols[r0 + c, 1] = cb0[c]
            cols[r0 + c, 2] = cw0[c, 0, 1, 0]
            cols[r0 + c, 3] = cw0[c, 0, 1, 2]
            cols[r0 + 9 + c, 0] = cw1[c, 0, 1, 1]
            cols[r0 + 9 + c, 1] = cb1[c]
            cols[r0 + 9 + c, 4] = cw1[c, 0, 0, 1]
            cols[r0 + 9 + c, 5] = cw1[c, 0, 2, 1]
    return cols

def ln_cols(ln_w, ln_b, proj_b):
    cols = np.zeros((128, 3), np.float32)
    for p_ in range(128):
        c = p_ % 16
        cols[p_] = (ln_w[c], ln_b[c], proj_b[c])
    return cols

def ffn_weight_tiles(wg, wu, wd):
    gu = np.zeros((64, 128, 2, 16, 128), np.int8)
    for ic in range(64):
        gu[ic, :, 0] = wg[:, ic * 128:(ic + 1) * 128].reshape(16, 128, 128).transpose(1, 0, 2)
        gu[ic, :, 1] = wu[:, ic * 128:(ic + 1) * 128].reshape(16, 128, 128).transpose(1, 0, 2)
    dn = np.zeros((4, 32, 128, 2, 512), np.int8)
    for oc in range(4):
        for ic in range(64):
            dn[oc, ic // 2, :, ic % 2] = wd[ic * 128:(ic + 1) * 128,
                                            oc * 512:(oc + 1) * 512]
    return (np.ascontiguousarray(gu.reshape(64 * 128, 4096)),
            np.ascontiguousarray(dn.reshape(4 * 32 * 128, 1024)))


class Sched:
    def __init__(self):
        self.ops = []
        self.counts = {}

    def emit(self, engine, fn, waits=(), incs=()):
        self.ops.append((engine, fn, [w for w in waits if w is not None], list(incs)))

    def inc(self, sem, amt):
        k = id(sem)
        self.counts[k] = self.counts.get(k, 0) + amt
        return (sem, amt)

    def at(self, sem):
        return (sem, self.counts.get(id(sem), 0))

    def run(self, engine, eng):
        for e, fn, waits, incs in self.ops:
            if e != engine:
                continue
            for sem, val in waits:
                if val > 0:
                    eng.wait_ge(sem, val)
            inst = fn(eng)
            for sem, amt in incs:
                inst.then_inc(sem, amt)


def apx(base, offset, dims):
    t = getattr(base, "tensor", base)
    fsz = int(np.prod(t.shape[1:]))
    np_ = int(t.shape[0])
    ap = [[fsz, np_]] + [list(d) for d in dims]
    return bass.AP(t, offset, ap)


class Prog:
    def __init__(self, nrows=2048, nbatch=4, taps=(), stage=9, rd=False):
        assert nrows % (32 * nbatch) == 0 and nrows % 512 == 0
        self.stage = stage
        self.rd = rd
        self.nrows = nrows
        self.nbatch = nbatch
        self.rpb = nrows // nbatch
        self.nsub = nrows // 32
        self.ntok = nrows * 128
        self.taps = set(taps)

    def build(self):
        p = self
        NR, NS, NT = p.nrows, p.nsub, p.ntok
        NSEG = NS // 4
        L16S = min(8, NS)
        RT = NR // 128
        MT = p.rpb // 16
        nc = bass.Bass("TRN2", target_bir_lowering=False, detect_race_conditions=self.rd)

        ein, eout = "ExternalInput", "ExternalOutput"
        xin = nc.dram_tensor("xin", [NR, DM], F32, kind=ein).ap()
        xperm = nc.dram_tensor("xperm", [NR, DM], F32, kind=ein).ap()
        qmap_d = nc.dram_tensor("qmapw", [64, 128], F32, kind=ein).ap()
        kmap_d = nc.dram_tensor("kmapw", [64, 128], F32, kind=ein).ap()
        vmap_d = nc.dram_tensor("vmapw", [64, 128], F32, kind=ein).ap()
        auxin_d = nc.dram_tensor("auxin", [9, 4096], F32, kind=ein).ap()
        ident_d = nc.dram_tensor("identw", [128, 128], F32, kind=ein).ap()
        lepmap_d = nc.dram_tensor("lepmapw", [128, 64], F32, kind=ein).ap()
        projbd_d = nc.dram_tensor("projbdw", [128, 128], F32, kind=ein).ap()
        bd16_d = nc.dram_tensor("bd16w", [128, 128], F32, kind=ein).ap()
        cw_d = nc.dram_tensor("cwcols", [128, 6], F32, kind=ein).ap()
        lnc_d = nc.dram_tensor("lncols", [128, 3], F32, kind=ein).ap()
        wgu_d = nc.dram_tensor("wgu", [64 * 128, 4096], I8, kind=ein).ap()
        wdn_d = nc.dram_tensor("wdn", [4 * 32 * 128, 1024], I8, kind=ein).ap()
        sg_d = nc.dram_tensor("sgcol", [128, 64], F32, kind=ein).ap()
        su_d = nc.dram_tensor("sucol", [128, 64], F32, kind=ein).ap()
        sdn_d = nc.dram_tensor("sdnw", [128, DM], F32, kind=ein).ap()
        eps_d = nc.dram_tensor("epscol", [128, 1], F32, kind=ein).ap()

        out_d = nc.dram_tensor("out", [NR, DM], F32, kind=eout).ap()

        lz = nc.dram_tensor("lz", [NR, DM], BF16, kind="Internal").ap()
        avp = nc.dram_tensor("avp", [16, NT], BF16, kind="Internal").ap()
        lepp = nc.dram_tensor("lepp", [16, NT], BF16, kind="Internal").ap()
        attnd = nc.dram_tensor("attnd", [16, NT], BF16, kind="Internal").ap()
        x1d = nc.dram_tensor("x1d", [NR, DM], F32, kind="Internal").ap()

        tapd = {}
        def mktap(name, shape, dtype=F32):
            if name in p.taps:
                tapd[name] = nc.dram_tensor("tap_" + name, shape, dtype, kind=eout).ap()
            return tapd.get(name)

        t_lz = mktap("lz", [NR, DM])
        t_qbuf = mktap("qbuf", [128, 4096])
        t_kbuf = mktap("kbuf", [128, 4096])
        t_vsb = mktap("vsb", [128, 4096])
        t_lep = mktap("lep", [128, 4096])
        t_asb = mktap("asb", [128, 2048])
        t_att1 = mktap("att1", [128, 128])
        t_avp = mktap("avp", [16, NT])
        t_lepp = mktap("lepp", [16, NT])
        t_attn = mktap("attn", [16, NT])
        t_x1 = mktap("x1", [NR, DM])
        t_fm = mktap("fm", [128, 16 * NR])

        S = Sched()
        FMW = 16 * NR

        with contextlib.ExitStack() as _st:
            _en = _st.enter_context
            qmap = _en(nc.sbuf_tensor("c_qmap", [64, 128], BF16))
            kmap = _en(nc.sbuf_tensor("c_kmap", [64, 128], BF16))
            vmap = _en(nc.sbuf_tensor("c_vmap", [64, 128], BF16))
            ident = _en(nc.sbuf_tensor("c_ident", [128, 128], BF16))
            lepmap = _en(nc.sbuf_tensor("c_lepmap", [128, 64], BF16))
            projbd = _en(nc.sbuf_tensor("c_projbd", [128, 128], BF16))
            bd16 = _en(nc.sbuf_tensor("c_bd16", [128, 128], BF16))
            cw = _en(nc.sbuf_tensor("c_cw", [128, 6], F32))
            lnc = _en(nc.sbuf_tensor("c_lnc", [128, 3], F32))
            epsc = _en(nc.sbuf_tensor("c_eps", [128, 1], F32))
            sg = _en(nc.sbuf_tensor("c_sg", [128, 64], F32))
            su = _en(nc.sbuf_tensor("c_su", [128, 64], F32))
            sdn = _en(nc.sbuf_tensor("c_sdn", [128, DM], BF16))
            af32 = _en(nc.sbuf_tensor("af32", [128, 6176], F32))
            ab = _en(nc.sbuf_tensor("ab", [128, 75776], BF16))
            gsil = _en(nc.sbuf_tensor("gsil", [128, 512], BF16))
            wdi0 = _en(nc.sbuf_tensor("wdi0", [128, 1024], I8))
            wdi1 = _en(nc.sbuf_tensor("wdi1", [128, 1024], I8))
            wgi0 = _en(nc.sbuf_tensor("wgi0", [128, 4096], I8))
            wgi1 = _en(nc.sbuf_tensor("wgi1", [128, 4096], I8))
            rsb = _en(nc.sbuf_tensor("rsb", [128, 16], F32))
            wg0 = _en(nc.sbuf_tensor("wg0", [128, 4096], BF16))
            wg1 = _en(nc.sbuf_tensor("wg1", [128, 4096], BF16))

            pb0 = _en(nc.psum_tensor("pb0", [128, 512], F32))
            pb1 = _en(nc.psum_tensor("pb1", [128, 512], F32))
            pb2 = _en(nc.psum_tensor("pb2", [128, 512], F32))
            psgrp = _en(nc.psum_tensor("psgrp", [128, 2048], F32))
            pb7 = _en(nc.psum_tensor("pb7", [128, 512], F32))
            sdma = _en(nc.semaphore("sdma"))
            gdma = _en(nc.semaphore("gdma"))
            tdma = _en(nc.semaphore("tdma"))
            smm = _en(nc.semaphore("smm"))
            sev = _en(nc.semaphore("sev"))
            ssc = _en(nc.semaphore("ssc"))
            sgp = _en(nc.semaphore("sgp"))
            wcv = _en(nc.semaphore("wcv"))
            sbar = _en(nc.semaphore("sbar"))
            blk = _en(nc.Block())

            def A(c0_, w):
                return ab[:, c0_:c0_ + w]
            fmsb = A(0, FMW)
            B0 = FMW
            fuse = A(B0, 32768)
            qb0 = A(B0, 4096); qb1 = A(B0 + 4096, 4096)
            kb0 = A(B0 + 8192, 4096); kb1 = A(B0 + 12288, 4096)
            vsb = A(B0 + 16384, 4096); lepb = A(B0 + 20480, 4096)
            vtsb = A(B0 + 24576, 4096)
            asb0 = A(B0 + 28672, 1024); asb1 = A(B0 + 29696, 1024)

            l16a = A(B0, 4096); l16w = A(B0 + 4096, 4096)
            l16d = A(B0 + 8192, 4096); l16o = A(B0 + 12288, 4096)
            C0 = B0 + 32768
            ffnsb = A(C0, 4 * DM)
            fla = ab[0:64, C0:C0 + 4096]
            flb = ab[0:64, C0 + 4096:C0 + 8192]
            rowsb = A(C0 + 8192, 2048)
            att1 = A(C0 + 8192, 128)
            cmsb = A(C0 + 8320, 128)
            avps = A(C0 + 8448, 128)
            lepps = A(C0 + 8576, 512)
            wd0 = A(C0 + 8192, 1024)
            wd1 = A(C0 + 9216, 1024)
            rows = af32[:, 0:2048]
            rows2 = af32[:, 2048:4096]
            rows3 = af32[:, 4096:6144]
            stat = af32[:, 6144:6176]
            l16r = af32[:, 0:4096]

            psq, psk, psv = pb0, pb1, pb2
            psav, psm, psm2 = pb0, pb2, pb7
            pstr = pb1[:, :].bitcast(BF16)
            pstr2 = pb2[:, :].bitcast(BF16)

            nbar = [0]

            def barrier():
                nbar[0] += 1
                tgt = 5 * nbar[0]
                ws = {"sync": [S.at(sdma)], "gpsimd": [S.at(gdma)],
                      "scalar": [S.at(tdma)], "tensor": [], "vector": []}
                for en in ("sync", "gpsimd", "tensor", "vector", "scalar"):
                    S.emit(en, lambda e: e.nop(), ws[en], [S.inc(sbar, 1)])
                for en in ("sync", "gpsimd", "tensor", "vector", "scalar"):
                    S.emit(en, lambda e: e.nop(), [(sbar, tgt)], [])

            def dma(engine, sem, dst, src, waits=()):
                S.emit(engine, lambda e, d=dst, s=src: e.dma_start(out=d, in_=s),
                       waits, [S.inc(sem, 16)])
                return S.at(sem)

            for d, s_ in ((qmap, qmap_d), (kmap, kmap_d), (vmap, vmap_d),
                          (ident, ident_d), (lepmap, lepmap_d),
                          (projbd, projbd_d), (bd16, bd16_d),
                          (sdn, sdn_d)):
                dma("gpsimd", gdma, d[:], s_)
            for d, s_ in ((cw, cw_d), (lnc, lnc_d), (sg, sg_d), (su, su_d), (epsc, eps_d)):
                dma("sync", sdma, d[:], s_)
            S.emit("vector", lambda e: e.memset(fla[:], 0.0), [], [S.inc(sev, 1)])
            S.emit("vector", lambda e: e.memset(flb[:], 0.0), [], [S.inc(sev, 1)])
            mz = S.at(sev)
            for buf in (fla, flb):
                for st in (0, 32):
                    dma("gpsimd", gdma, buf[st + 16:st + 25, :], auxin_d, [mz])
            barrier()

            def ln_stats(src, junk, waits=()):
                waits = list(waits)
                S.emit("vector", lambda e: e.reduce_sum(stat[:, 0:1], src, AX.X),
                       waits + [S.at(ssc)], [S.inc(sev, 1)])
                S.emit("scalar", lambda e: e.activation(
                    junk, src, AF.Square, accum_out=stat[:, 1:2]),
                    waits + [S.at(sev)], [S.inc(ssc, 1)])
                S.emit("vector", lambda e: e.tensor_scalar(
                    stat[:, 2:4], stat[:, 0:2], 1.0 / DM, None, ALU.mult),
                    [S.at(ssc), S.at(sev)], [S.inc(sev, 1)])
                S.emit("vector", lambda e: e.tensor_tensor(
                    stat[:, 4:5], stat[:, 2:3], stat[:, 2:3], ALU.mult),
                    [S.at(sev)], [S.inc(sev, 1)])
                S.emit("vector", lambda e: e.tensor_sub(
                    stat[:, 5:6], stat[:, 3:4], stat[:, 4:5]), [S.at(sev)],
                    [S.inc(sev, 1)])
                S.emit("scalar", lambda e: e.activation(
                    stat[:, 6:7], stat[:, 5:6], AF.Ln, bias=epsc[:, 0:1]),
                    [S.at(sev)], [S.inc(ssc, 1)])
                S.emit("scalar", lambda e: e.activation(
                    stat[:, 7:8], stat[:, 6:7], AF.Exp, scale=-0.5), [S.at(ssc)],
                    [S.inc(ssc, 1)])
                return S.at(ssc)

            for rt in range(RT):
                w_ld = dma("sync", sdma, rows[:], xin[rt * 128:(rt + 1) * 128, :],
                           [S.at(sev)])
                w_rstd = ln_stats(rows[:], rows2[:], [w_ld])
                S.emit("vector", lambda e: e.tensor_scalar(
                    rowsb.rearrange("p (c w) -> p w c", c=16),
                    rows.rearrange("p (w c) -> p w c", c=16),
                    stat[:, 2:3], stat[:, 7:8], ALU.subtract, ALU.mult),
                    [w_rstd, S.at(sdma), S.at(gdma)], [S.inc(sev, 1)])
                dma("sync", sdma, lz[rt * 128:(rt + 1) * 128, :], rowsb[:],
                    [S.at(sev)])
                if t_lz is not None:
                    dma("gpsimd", gdma, t_lz[rt * 128:(rt + 1) * 128, :], rowsb[:],
                        [S.at(sev)])
            barrier()

            def lz_cAP(sub):
                return lz.rearrange("r (c w) -> c r w", c=16)[
                    :, sub * 32:(sub + 1) * 32, :]

            def sig_out(buf, cc):
                v = buf.rearrange("p (a b d e) -> p a b d e", a=4, b=16, d=8, e=8)
                return v[:, :, :, cc, :]

            w_qkv_mm_end = None
            w_exp_par = [None, None]
            for seg in range(NSEG if p.stage >= 2 else 0):
                sub0 = seg * 4
                wfl = []
                flw = [w_qkv_mm_end] if w_qkv_mm_end else []
                for ipair, buf in ((0, fla), (1, flb)):
                    for sl_ in range(2):
                        sub = sub0 + 2 * ipair + sl_
                        wfl.append(dma("gpsimd", gdma,
                                       buf[sl_ * 32:sl_ * 32 + 16, :], lz_cAP(sub),
                                       flw))
                psq_r = [psq[:, 0:512], psgrp[:, 0:512]]
                psk_r = [psk[:, 0:512], psgrp[:, 512:1024]]
                psv_r = [psv, psgrp[:, 1024:1536]]
                wq_war = [None, None]
                wk_war = [None, None]
                wv_war = [None, None]
                first_ev = True
                for ipair, buf, qb, kb in ((0, fla, qb0, kb0), (1, flb, qb1, kb1)):
                    for cc in range(8):
                        par = cc % 2
                        gcc = ipair * 8 + cc
                        sl_ = slice(cc * 512, (cc + 1) * 512)
                        qw = [wfl[2 * ipair], wfl[2 * ipair + 1]]
                        if gcc == 0:
                            qw.append(S.at(sev))
                        elif gcc == 1:
                            qw += [w_exp_par[0]] if w_exp_par[0] else []
                        qw += wq_war[par] or []
                        S.emit("tensor", lambda e, b=buf, s=sl_, pp=psq_r[par]:
                               e.matmul(pp, qmap[:], b[:, s],
                                        start=True, stop=True),
                               qw, [S.inc(smm, 1)])
                        S.emit("tensor", lambda e, b=buf, s=sl_, pp=psk_r[par]:
                               e.matmul(pp, kmap[:], b[:, s],
                                        start=True, stop=True),
                               wk_war[par] or [], [S.inc(smm, 1)])
                        wmm = S.at(smm)
                        evw = [wmm, S.at(smm)] if first_ev else [wmm]
                        first_ev = False
                        S.emit("vector", lambda e, d_=qb, c=cc, pp=psq_r[par]:
                               e.tensor_copy(sig_out(d_, c), pp), evw,
                               [S.inc(sev, 1)])
                        wq_war[par] = [S.at(sev)]
                        S.emit("vector", lambda e, d_=kb, c=cc, pp=psk_r[par]:
                               e.tensor_copy(sig_out(d_, c), pp), [],
                               [S.inc(sev, 1)])
                        wk_war[par] = [S.at(sev)]
                for cc in range(8):
                    par = cc % 2
                    sl_ = slice(cc * 512, (cc + 1) * 512)
                    vw = []
                    if cc == 0:
                        vw.append(S.at(sev))
                    elif cc == 1:
                        vw += [w_exp_par[1]] if w_exp_par[1] else []
                    vw += wv_war[par] or []
                    S.emit("tensor", lambda e, s=sl_, pp=psv_r[par]: e.matmul(
                        pp[0:64, 0:512], vmap[0:64, 0:64], fla[:, s],
                        start=True, stop=True), vw, [S.inc(smm, 1)])
                    S.emit("tensor", lambda e, s=sl_, pp=psv_r[par]: e.matmul(
                        pp[64:128, 0:512], vmap[0:64, 0:64], flb[:, s],
                        start=True, stop=True, tile_position=(0, 64)),
                        [], [S.inc(smm, 1)])
                    S.emit("vector", lambda e, c=cc, pp=psv_r[par]: e.tensor_copy(
                        sig_out(vsb, c), pp[:, 0:512]), [S.at(smm)],
                        [S.inc(sev, 1)])
                    wv_war[par] = [S.at(sev)]
                w_qkv_mm_end = S.at(smm)
                w_qkv = S.at(sev)
                w_psgrp_extra = {0: [w_qkv], 1: [w_qkv]}
                if t_qbuf is not None and seg == 0:
                    dma("gpsimd", gdma, t_qbuf, qb0[:], [w_qkv])
                    dma("gpsimd", gdma, t_kbuf, kb0[:], [w_qkv])
                    dma("gpsimd", gdma, t_vsb, vsb[:], [w_qkv])

                pstr_r = [pstr, pstr2]
                w_vt_war = [None, None]
                for b4 in range(8):
                    par = b4 % 2
                    for j in range(4):
                        beta = b4 * 4 + j
                        jw = ([w_qkv, S.at(sev)] if b4 < 2 else
                              [w_qkv] + (w_vt_war[par] or [])) if j == 0 else []
                        S.emit("tensor", lambda e, b=beta, jj=j, pp=pstr_r[par]:
                               e.transpose(pp[:, jj * 128:(jj + 1) * 128],
                                           vsb[:, b * 128:(b + 1) * 128],
                                           ident[:]), jw, [S.inc(smm, 1)])
                    S.emit("vector", lambda e, b=b4, pp=pstr_r[par]: e.tensor_copy(
                        vtsb[:, b * 512:(b + 1) * 512], pp[:, 0:512]),
                        [S.at(smm)], [S.inc(sev, 1)])
                    w_vt_war[par] = [S.at(sev)]
                w_vt = S.at(sev)
                S.emit("vector", lambda e: e.tensor_scalar(
                    lepb[:], vsb[:], cw[:, 0:1], cw[:, 1:2], ALU.mult, ALU.add),
                    [w_qkv, S.at(smm)], [S.inc(sev, 1)])
                vv = vsb.rearrange("p (w e) -> p w e", e=8)
                ll = lepb.rearrange("p (w e) -> p w e", e=8)
                S.emit("vector", lambda e: e.scalar_tensor_tensor(
                    ll[:, :, 1:8], vv[:, :, 0:7], cw[:, 2:3], ll[:, :, 1:8],
                    ALU.mult, ALU.add), [S.at(sev)], [S.inc(sev, 1)])
                S.emit("vector", lambda e: e.scalar_tensor_tensor(
                    ll[:, :, 0:7], vv[:, :, 1:8], cw[:, 3:4], ll[:, :, 0:7],
                    ALU.mult, ALU.add), [S.at(sev)], [S.inc(sev, 1)])
                vv8 = vsb.rearrange("p (b d e) -> p b d e", d=8, e=8)
                ll8 = lepb.rearrange("p (b d e) -> p b d e", d=8, e=8)
                S.emit("vector", lambda e: e.scalar_tensor_tensor(
                    ll8[:, :, 1:8, :], vv8[:, :, 0:7, :], cw[:, 4:5],
                    ll8[:, :, 1:8, :], ALU.mult, ALU.add), [S.at(sev)],
                    [S.inc(sev, 1)])
                S.emit("vector", lambda e: e.scalar_tensor_tensor(
                    ll8[:, :, 0:7, :], vv8[:, :, 1:8, :], cw[:, 5:6],
                    ll8[:, :, 0:7, :], ALU.mult, ALU.add), [S.at(sev)],
                    [S.inc(sev, 1)])
                w_lep = S.at(sev)
                if t_lep is not None and seg == 0:
                    dma("gpsimd", gdma, t_lep, lepb[:], [w_lep])

                for s4 in range(4 if p.stage >= 3 else 0):
                    sub = sub0 + s4
                    qb, kb = (qb0, kb0) if s4 < 2 else (qb1, kb1)
                    st1 = (s4 % 2) * 64
                    st2 = st1 + 32

                    def emit_grams(bg):
                        half = bg % 2
                        hof = half * 1024
                        a_sb = asb0 if half == 0 else asb1
                        w_war = [w_exp_par[half]] if w_exp_par[half] else []
                        w_war += w_psgrp_extra.pop(half, [])
                        for j in range(4):
                            beta = bg * 4 + j
                            csl = slice(beta * 128, (beta + 1) * 128)
                            for br, st in ((0, st1), (1, st2)):
                                u = br * 4 + j
                                S.emit("tensor", lambda e, k_=kb, q_=qb, s=csl,
                                       r0=st, uu=u, ho=hof: e.matmul(
                                    psgrp[:, ho + uu * 128:ho + (uu + 1) * 128],
                                    k_[r0:r0 + 20, s], q_[r0:r0 + 20, s],
                                    start=True, stop=True, tile_position=(r0, 0)),
                                    w_war if (j == 0 and br == 0) else [],
                                    [S.inc(smm, 1)])
                        S.emit("scalar", lambda e, ho=hof, ab_=a_sb: e.activation(
                            ab_[:, 0:1024], psgrp[:, ho:ho + 1024], AF.Exp,
                            scale=SQK), [S.at(smm)], [S.inc(ssc, 1)])
                        w_exp_par[half] = S.at(ssc)
                        return (bg, S.at(ssc), a_sb)

                    def emit_av(info):
                        bg, w_exp, a_sb = info
                        if t_asb is not None and seg == 0 and s4 == 0 and bg < 2:
                            dma("gpsimd", gdma, t_asb[:, bg * 1024:(bg + 1) * 1024],
                                a_sb[:, 0:1024], [w_exp])
                        for j in range(4):
                            beta = bg * 4 + j
                            jj = (bg % 2) * 4 + j
                            for br in range(2):
                                u = br * 4 + j
                                r0_ = beta * 128 + s4 * 32 + br * 8
                                S.emit("tensor", lambda e, uu=u, jx=jj, b_=br,
                                       r=r0_, ab_=a_sb: e.matmul(
                                    psav[:, jx * 18 + b_ * 9:jx * 18 + b_ * 9 + 9],
                                    ab_[:, uu * 128:(uu + 1) * 128],
                                    vtsb[:, r:r + 9], start=True, stop=True),
                                    [w_exp, w_vt, S.at(sev)]
                                    if (j == 0 and br == 0) else [],
                                    [S.inc(smm, 1)])

                    def emit_ppd(bgodd):
                        w_av = S.at(smm)
                        S.emit("vector", lambda e: e.reciprocal(
                            rsb[:, 0:16].rearrange("p (a b) -> p a b", b=2),
                            apx(psav, 8, [[18, 8], [1, 2]])), [w_av],
                            [S.inc(sev, 1)])
                        S.emit("vector", lambda e: e.tensor_tensor(
                            att1.rearrange("p (j b c) -> p j b c", b=2, c=8),
                            apx(psav, 0, [[18, 8], [10, 2], [1, 8]]),
                            apx(rsb, 0, [[2, 8], [1, 2], [0, 8]]),
                            ALU.mult), [S.at(smm), S.at(sev)], [S.inc(sev, 1)])
                        w_n = S.at(sev)
                        if t_att1 is not None and seg == 0 and s4 == 0 and bgodd == 1:
                            dma("gpsimd", gdma, t_att1, att1[:], [w_n])
                        return (bgodd, w_n)

                    def emit_ppt(info):
                        bgodd, w_n = info
                        S.emit("tensor", lambda e: e.transpose(
                            pstr[:, 512:640], att1[:], ident[:]),
                            [w_n], [S.inc(smm, 1)])
                        S.emit("vector", lambda e: e.tensor_copy(
                            cmsb[:], pstr[:, 512:640]), [S.at(smm)],
                            [S.inc(sev, 1)])
                        S.emit("tensor", lambda e: e.matmul(
                            psm2[:, 0:128], projbd[:], cmsb[:],
                            start=True, stop=True), [S.at(sev)], [S.inc(smm, 1)])
                        S.emit("vector", lambda e: e.tensor_copy(
                            avps[:], psm2[:, 0:128]), [S.at(smm), S.at(sdma)],
                            [S.inc(sev, 1)])
                        bgp = bgodd // 2
                        dst = avp.rearrange("c (u b t) -> u b c t", u=NS, t=128)[
                            sub, bgp * 8:(bgp + 1) * 8, :, :]
                        dma("sync", sdma, dst, avps[:], [S.at(sev)])

                    def emit_projlep(cc):
                        sl_ = slice(cc * 512, (cc + 1) * 512)
                        S.emit("tensor", lambda e, s=sl_: e.matmul(
                            psm[0:64, 0:512], lepmap[:], lepb[:, s], start=True,
                            stop=True), [w_lep, w_qkv, S.at(sev)],
                            [S.inc(smm, 1)])
                        S.emit("vector", lambda e: e.tensor_copy(
                            lepps[0:64, :], psm[0:64, 0:512]),
                            [S.at(smm), S.at(sdma)], [S.inc(sev, 1)])
                        dst = lepp.rearrange("c (u t) -> u c t", t=4096)[
                            sub0:sub0 + 4, :, cc * 512:(cc + 1) * 512]
                        dma("sync", sdma, dst, lepps[0:64, :], [S.at(sev)])

                    pend = None
                    ptp = None
                    for bg in range(8):
                        info = emit_grams(bg)
                        if pend is not None:
                            emit_av(pend)
                            if pend[0] % 2 == 1:
                                if ptp is not None:
                                    emit_ppt(ptp)
                                ptp = emit_ppd(pend[0])
                                if pend[0] in (1, 3):
                                    emit_projlep(s4 * 2 + pend[0] // 2)
                        pend = info
                    emit_av(pend)
                    emit_ppt(ptp)
                    emit_ppt(emit_ppd(7))


            barrier()

            if t_avp is not None:
                dma("gpsimd", gdma, t_avp, avp)
            if t_lepp is not None:
                dma("gpsimd", gdma, t_lepp, lepp)

            for g in range(NS // L16S if p.stage >= 7 else 0):
                u0 = g * L16S
                pr = L16S * 16
                srcA = avp.rearrange("c (u t) -> u c t", t=4096)[u0:u0 + L16S]
                srcB = lepp.rearrange("c (u t) -> u c t", t=4096)[u0:u0 + L16S]
                wa = dma("gpsimd", gdma, l16a[0:pr, :], srcA, [S.at(sev)])
                wb_ = dma("gpsimd", gdma, l16d[0:pr, :], srcB, [S.at(sev)])
                S.emit("vector", lambda e: e.scalar_tensor_tensor(
                    l16w[0:pr, :], l16a[0:pr, :], lnc[0:pr, 2:3], l16d[0:pr, :],
                    ALU.add, ALU.add), [wa, wb_], [S.inc(sev, 1)])
                w_w = S.at(sev)
                for cc in range(8):
                    sl_ = slice(cc * 512, (cc + 1) * 512)
                    S.emit("tensor", lambda e, s=sl_: e.matmul(
                        psm[:, 0:512], bd16[:], l16w[:, s], start=True, stop=True),
                        [w_w, S.at(sev)], [S.inc(smm, 1)])
                    S.emit("vector", lambda e, s=sl_: e.tensor_sub(
                        l16d[:, s], l16w[:, s], psm[:, 0:512]), [S.at(smm)],
                        [S.inc(sev, 1)])
                    S.emit("scalar", lambda e, s=sl_: e.activation(
                        l16a[:, s], l16d[:, s], AF.Square), [S.at(sev)],
                        [S.inc(ssc, 1)])
                    S.emit("tensor", lambda e, s=sl_: e.matmul(
                        psm2[:, 0:512], bd16[:], l16a[:, s], start=True,
                        stop=True), [S.at(ssc)], [S.inc(smm, 1)])
                    S.emit("scalar", lambda e, s=sl_: e.activation(
                        l16r[:, s], psm2[:, 0:512], AF.Ln, bias=epsc[:, 0:1]),
                        [S.at(smm)], [S.inc(ssc, 1)])
                    S.emit("scalar", lambda e, s=sl_: e.activation(
                        l16r[:, s], l16r[:, s], AF.Exp, scale=-0.5), [S.at(ssc)],
                        [S.inc(ssc, 1)])
                    S.emit("vector", lambda e, s=sl_: e.tensor_mul(
                        l16d[:, s], l16d[:, s], l16r[:, s]), [S.at(ssc)],
                        [S.inc(sev, 1)])
                    lam = cc
                    o4 = l16o.rearrange("p (d cz wh wl e) -> p d cz wh wl e",
                                        d=8, cz=4, wh=2, wl=8, e=8)
                    ov = o4[:, :, lam >> 1, lam & 1, :, :].transpose([0, 2, 1, 3])
                    S.emit("vector", lambda e, o=ov, s=sl_: e.tensor_scalar(
                        o, l16d[:, s].rearrange("p (w d e) -> p w d e", d=8, e=8),
                        lnc[:, 0:1], lnc[:, 1:2], ALU.mult, ALU.add),
                        [S.at(sev)] + ([S.at(sdma)] if cc == 0 else []),
                        [S.inc(sev, 1)])
                dst = attnd.rearrange("c (u t) -> u c t", t=4096)[u0:u0 + L16S]
                dma("sync", sdma, dst, l16o[0:pr, :], [S.at(sev)])
            barrier()
            if t_attn is not None:
                dma("gpsimd", gdma, t_attn, attnd)
                barrier()

            ncv = 128 // MT
            p3s1 = A(B0, 2048)
            p3s2 = A(B0 + 2048, 2048)
            fm_rdy = []
            x1_rdy = []
            for rt in range(RT if p.stage >= 8 else 0):
                r0 = rt * 128
                bi = r0 // p.rpb
                ci = (r0 % p.rpb) // MT
                w_pm = []
                for j in range(ncv):
                    c = ci + j
                    asrc = attnd[c:c + 1, :].rearrange(
                        "o (b m w) -> o b m w", b=p.nbatch, w=DM)[0, bi]
                    w_pm.append(dma("sync", sdma,
                                    p3s1[j * MT:(j + 1) * MT, :], asrc,
                                    [S.at(sev)]))
                    lsrc = lz.rearrange("(b r) (c w) -> b r c w",
                                        b=p.nbatch, c=16)[bi, :, c, :].rearrange(
                        "(m h) w -> m h w", h=16)
                    w_pm.append(dma("scalar", tdma,
                                    p3s2[j * MT:(j + 1) * MT, :].rearrange(
                                        "p (h w) -> p h w", w=128), lsrc,
                                    [S.at(sev)]))
                w_x0 = dma("sync", sdma, rows[:], xperm[r0:r0 + 128, :],
                           [S.at(sev)])
                S.emit("vector", lambda e: e.tensor_add(
                    rows2[:], p3s1[:], p3s2[:]), w_pm, [S.inc(sev, 1)])
                w_rstd = ln_stats(rows2[:], rows3[:], [S.at(sev)])
                S.emit("vector", lambda e: e.tensor_scalar(
                    rows2[:], rows2[:], stat[:, 2:3], stat[:, 7:8],
                    ALU.subtract, ALU.mult), [w_rstd], [S.inc(sev, 1)])
                S.emit("vector", lambda e: e.tensor_add(
                    rows[:], rows[:], rows2[:]), [w_x0, S.at(sev)],
                    [S.inc(sev, 1)])
                w_x1 = S.at(sev)
                w_x1d = dma("sync", sdma, x1d[r0:r0 + 128, :], rows[:], [w_x1])
                if t_x1 is not None:
                    dma("sync", sdma, t_x1[r0:r0 + 128, :], rows[:], [w_x1])
                w_rstd = ln_stats(rows[:], rows3[:])
                S.emit("vector", lambda e: e.tensor_scalar(
                    rowsb[:], rows[:], stat[:, 2:3], stat[:, 7:8],
                    ALU.subtract, ALU.mult), [w_rstd, S.at(tdma)],
                    [S.inc(sev, 1)])
                w_ln2 = S.at(sev)
                for kc in range(16):
                    S.emit("scalar", lambda e, k=kc, r=rt: e.dma_start_transpose(
                        out=fmsb[:, k * NR + r * 128:k * NR + (r + 1) * 128],
                        in_=rowsb[:, k * 128:(k + 1) * 128]),
                        [w_ln2], [S.inc(tdma, 16)])
                x1_rdy.append(w_x1d)
                if rt % 4 == 3:
                    fm_rdy.append(S.at(tdma))
            w_p3t = S.at(tdma) if (RT if p.stage >= 8 else 0) else None
            barrier()
            if t_fm is not None:
                dma("gpsimd", gdma, t_fm, fmsb[:], [w_p3t])
                barrier()

            NTT = (NR // 512) if p.stage >= 9 else 0
            wgub = [wg0, wg1]
            wgi = [wgi0, wgi1]
            wdb = [wd0, wd1]
            wdi = [wdi0, wdi1]
            w_giwar = [[], []]
            w_gcast = [[], []]
            w_gmm = [[], []]
            w_wiwar = [[], []]
            w_wdwar = [[w_p3t], [w_p3t]] if p.stage >= 8 else [[], []]
            psg_par = [pb0, pb2]
            psu_par = [pb1, pb7]
            w_sil_par = [None, None]
            w_stt_par = [None, None]
            gc = {}

            def g_loadcast(tt, ic):
                par = ic % 2
                wla = dma("sync", sdma, wgi[par][:, 0:2048],
                          wgu_d[ic * 128:(ic + 1) * 128, 0:2048],
                          w_giwar[par])
                wlb = dma("scalar", tdma, wgi[par][:, 2048:4096],
                          wgu_d[ic * 128:(ic + 1) * 128, 2048:4096],
                          w_giwar[par])
                S.emit("vector", lambda e, b=par: e.tensor_copy(
                    wgub[b][:], wgi[b][:]), [wla, wlb] + w_gmm[par],
                    [S.inc(sev, 1)])
                w_giwar[par] = [S.at(sev)]
                gc[(tt, ic)] = S.at(sev)

            for tt in range(NTT):
                g_loadcast(tt, 0)
                g_loadcast(tt, 1)
                for ic in range(64):
                    par = ic % 2
                    pg, pu = psg_par[par], psu_par[par]
                    war = [w for w in (w_sil_par[par], w_stt_par[par]) if w]
                    for kc in range(16):
                        S.emit("tensor", lambda e, k=kc, b=par, t=tt, pp=pg: e.matmul(
                            pp[:, 0:512], wgub[b][:, k * 128:(k + 1) * 128],
                            fmsb[:, k * NR + t * 512:k * NR + (t + 1) * 512],
                            start=(k == 0), stop=(k == 15)),
                            ([gc[(tt, ic)]] + war +
                             ([fm_rdy[tt]] if ic == 0 else []))
                            if kc == 0 else [], [])
                        S.emit("tensor", lambda e, k=kc, b=par, t=tt, pp=pu: e.matmul(
                            pp[:, 0:512], wgub[b][:, 2048 + k * 128:2048 + (k + 1) * 128],
                            fmsb[:, k * NR + t * 512:k * NR + (t + 1) * 512],
                            start=(k == 0), stop=(k == 15)),
                            [], [S.inc(smm, 1)] if kc == 15 else [])
                    wmm = S.at(smm)
                    w_gmm[par] = [wmm]
                    S.emit("scalar", lambda e, i=ic, pp=pg: e.activation(
                        gsil[:], pp[:, 0:512], AF.Silu, scale=sg[:, i:i + 1]),
                        [wmm, S.at(sev)], [S.inc(ssc, 1)])
                    w_sil_par[par] = S.at(ssc)
                    if ic + 2 < 64:
                        g_loadcast(tt, ic + 2)
                    S.emit("vector", lambda e, i=ic, pp=pu: e.scalar_tensor_tensor(
                        fuse[:, i * 512:(i + 1) * 512], pp[:, 0:512],
                        su[:, i:i + 1], gsil[:], ALU.mult, ALU.mult),
                        [S.at(ssc)], [S.inc(sev, 1)])
                    w_stt_par[par] = S.at(sev)
                w_fuse = S.at(sev)
                for oc in range(4):
                    for icp in range(32):
                        par = icp % 2
                        wld = dma("sync" if par == 0 else "scalar",
                                  sdma if par == 0 else tdma, wdi[par][:],
                                  wdn_d[(oc * 32 + icp) * 128:
                                        (oc * 32 + icp + 1) * 128, :],
                                  w_wiwar[par])
                        S.emit("vector", lambda e, b=par: e.tensor_copy(
                            wdb[b][:], wdi[b][:]), [wld] + w_wdwar[par],
                            [S.inc(sev, 1)])
                        w_wiwar[par] = [S.at(sev)]
                        wcp = S.at(sev)
                        for half in range(2):
                            ic = icp * 2 + half
                            for tb in range(4):
                                S.emit("tensor", lambda e, i=ic, t=tb, b=par,
                                       h=half: e.matmul(
                                    psgrp[:, t * 512:(t + 1) * 512],
                                    fuse[:, i * 512 + t * 128:i * 512 + (t + 1) * 128],
                                    wdb[b][:, h * 512:(h + 1) * 512],
                                    start=(i == 0), stop=(i == 63)),
                                    ([wcp, w_fuse, S.at(sev)]
                                     if (icp == 0 and half == 0 and tb == 0) else
                                     ([wcp] if (half == 0 and tb == 0) else [])),
                                    [S.inc(smm, 1)] if (half == 1 and tb == 3)
                                    else [])
                        w_wdwar[par] = [S.at(smm)]
                    wmm = S.at(smm)
                    for tb in range(4):
                        S.emit("vector", lambda e, t=tb, o=oc: e.tensor_copy(
                            ffnsb[:, t * DM + o * 512:t * DM + (o + 1) * 512],
                            psgrp[:, t * 512:(t + 1) * 512]), [wmm],
                            [S.inc(sev, 1)])
                w_dn = S.at(sev)

                for tb in range(4):
                    rt = tt * 4 + tb
                    r0 = rt * 128
                    S.emit("vector", lambda e, t=tb: e.tensor_mul(
                        rows2[:], ffnsb[:, t * DM:(t + 1) * DM], sdn[:]),
                        [w_dn], [S.inc(sev, 1)])
                    w_xr = dma("gpsimd", gdma, rows[:], x1d[r0:r0 + 128, :],
                               [S.at(sev), x1_rdy[rt]])
                    w_rstd = ln_stats(rows2[:], rows3[:], [S.at(sev)])
                    S.emit("vector", lambda e: e.tensor_scalar(
                        rows2[:], rows2[:], stat[:, 2:3], stat[:, 7:8],
                        ALU.subtract, ALU.mult), [w_rstd], [S.inc(sev, 1)])
                    S.emit("vector", lambda e: e.tensor_add(
                        rows[:], rows[:], rows2[:]), [w_xr, S.at(sev)],
                        [S.inc(sev, 1)])
                    dma("gpsimd", gdma, out_d[r0:r0 + 128, :], rows[:],
                        [S.at(sev)])
            S.emit("sync", lambda e: e.nop(), [S.at(sdma)], [])
            S.emit("gpsimd", lambda e: e.nop(), [S.at(gdma)], [])
            S.emit("scalar", lambda e: e.nop(), [S.at(tdma)], [])

            @blk.sync
            def _(eng):
                S.run("sync", eng)
            @blk.gpsimd
            def _(eng):
                S.run("gpsimd", eng)
            @blk.tensor
            def _(eng):
                S.run("tensor", eng)
            @blk.vector
            def _(eng):
                S.run("vector", eng)
            @blk.scalar
            def _(eng):
                S.run("scalar", eng)
        return nc, tapd



def host_consts(inputs):
    qkv_w = np.asarray(inputs["L0_attn_qkv_w"], np.float32)
    proj_w = np.asarray(inputs["L0_attn_proj_w"], np.float32)
    qm, km = qk_maps(qkv_w)
    wgu, wdn = ffn_weight_tiles(np.asarray(inputs["L0_ffn_w_gate"]),
                                np.asarray(inputs["L0_ffn_w_up"]),
                                np.asarray(inputs["L0_ffn_w_down"]))
    return {
        "qmapw": qm, "kmapw": km, "vmapw": v_map(qkv_w),
        "auxin": aux_input_rows(),
        "identw": np.eye(128, dtype=np.float32),
        "lepmapw": lep_proj_map(proj_w),
        "projbdw": proj_bd8(proj_w),
        "bd16w": bd16_avg(),
        "cwcols": cw_cols(np.asarray(inputs["L0_attn_conv_w0"], np.float32),
                          np.asarray(inputs["L0_attn_conv_b0"], np.float32),
                          np.asarray(inputs["L0_attn_conv_w1"], np.float32),
                          np.asarray(inputs["L0_attn_conv_b1"], np.float32)),
        "lncols": ln_cols(np.asarray(inputs["L0_attn_ln_w"], np.float32),
                          np.asarray(inputs["L0_attn_ln_b"], np.float32),
                          np.asarray(inputs["L0_attn_proj_b"], np.float32)),
        "sgcol": np.ascontiguousarray(
            np.asarray(inputs["L0_ffn_s_gate"], np.float32).reshape(64, 128).T),
        "sucol": np.ascontiguousarray(
            np.asarray(inputs["L0_ffn_s_up"], np.float32).reshape(64, 128).T),
        "epscol": np.full((128, 1), EPS, np.float32),
        "sdnw": np.ascontiguousarray(np.broadcast_to(np.asarray(inputs["L0_ffn_s_down"], np.float32), (128, DM))),
        "wgu": wgu, "wdn": wdn,
    }


def perm_rows(k, nbatch=4, rpb=512):
    mt = rpb // 16
    ridx = np.empty(nbatch * rpb, np.int64)
    i = 0
    for b in range(nbatch):
        for c in range(16):
            for m in range(mt):
                ridx[i] = b * 4096 + 256 * c + mt * k + m
                i += 1
    return ridx


def kernel(**inputs):
    x = np.asarray(inputs["L0_attn_x"], np.float32)
    B, H, Wd = x.shape
    consts = host_consts(inputs)
    p = Prog(nrows=2048, nbatch=4)
    nc, _ = p.build()
    xf = x.reshape(B * H, Wd)
    in_maps = []
    for k in range(8):
        xin = np.ascontiguousarray(np.concatenate(
            [x[b, 512 * k:512 * (k + 1), :] for b in range(B)], axis=0))
        ridx = perm_rows(k)
        xperm = np.ascontiguousarray(xf[ridx])
        m = {"xin": xin, "xperm": xperm}
        m.update(consts)
        in_maps.append(m)
    res = run_bass_kernel_spmd(nc, in_maps, core_ids=list(range(8)))
    out = np.zeros((B * H, Wd), np.float32)
    for k in range(8):
        out[perm_rows(k)] = res.results[k]["out"]
    return out.reshape(B, H, Wd)


# version 17 (final)
# speedup vs baseline: 1.1297x; 1.0013x over previous
import contextlib
import numpy as np
import ml_dtypes
import concourse.bass as bass
import concourse.mybir as mybir
from concourse.bass_utils import run_bass_kernel_spmd

F32 = mybir.dt.float32
BF16 = mybir.dt.bfloat16
I8 = mybir.dt.int8
AF = mybir.ActivationFunctionType
ALU = mybir.AluOpType
AX = mybir.AxisListType
bfd = ml_dtypes.bfloat16

MB = 100.0
EPS = 1e-5
DM = 2048
INTER = 8192
SQK = 8.0 ** -0.5


def sigma_perm(n=4096):
    t = np.arange(n)
    ht, wb = t // 128, t % 128
    d, c0 = ht // 4, ht % 4
    wbb, e = wb // 8, wb % 8
    return ((c0 * 16 + wbb) * 8 + d) * 8 + e

SIG = sigma_perm()

def aux_input_rows():
    tss = SIG[np.arange(4096)]
    w1 = (tss >> 3) & 15
    w2 = (tss & 7) | (((tss >> 6) & 1) << 3)
    rows = [np.ones(4096)]
    rows += [(w1 >> k) & 1 for k in range(4)]
    rows += [(w2 >> k) & 1 for k in range(4)]
    return np.asarray(rows, dtype=np.float32)

def qk_maps(qkv_w):
    maps = []
    for which in (0, 1):
        m = np.zeros((64, 128), np.float32)
        wsub = qkv_w[:, which * 16:(which + 1) * 16]
        for s in range(2):
            r0, cc = 32 * s, 64 * s
            for br in range(2):
                cb = cc + 32 * br
                m[r0:r0 + 16, cb:cb + 8] = wsub[:, br * 8:br * 8 + 8]
                for k in range(4):
                    bitrow = r0 + 17 + 4 * br + k
                    onerow = r0 + 16
                    a, b_, c_ = cb + 8 + 3 * k, cb + 9 + 3 * k, cb + 10 + 3 * k
                    if which == 0:
                        m[bitrow, a] = -MB
                        m[onerow, b_] = -MB
                        m[bitrow, c_] = 2 * MB
                    else:
                        m[onerow, a] = 1.0
                        m[bitrow, b_] = 1.0
                        m[bitrow, c_] = 1.0
        maps.append(m)
    return maps

def v_map(qkv_w):
    m = np.zeros((64, 128), np.float32)
    wv = qkv_w[:, 32:48]
    for s in range(2):
        r0, cc = 32 * s, 32 * s
        m[r0:r0 + 16, cc:cc + 8] = wv[:, 0:8]
        m[r0 + 16, cc + 8] = 1.0
        m[r0:r0 + 16, cc + 9:cc + 17] = wv[:, 8:16]
    return m

def lep_proj_map(proj_w):
    m = np.zeros((128, 64), np.float32)
    for s in range(4):
        r0, cc = 32 * s, 16 * s
        m[r0:r0 + 8, cc:cc + 16] = proj_w[0:8, :]
        m[r0 + 9:r0 + 17, cc:cc + 16] = proj_w[8:16, :]
    return m

def proj_bd8(proj_w):
    m = np.zeros((128, 128), np.float32)
    for g in range(8):
        m[g * 16:(g + 1) * 16, g * 16:(g + 1) * 16] = proj_w
    return m

def bd16_avg():
    m = np.zeros((128, 128), np.float32)
    for g in range(8):
        m[g * 16:(g + 1) * 16, g * 16:(g + 1) * 16] = 1.0 / 16.0
    return m

def cw_cols(cw0, cb0, cw1, cb1):
    cols = np.zeros((128, 6), np.float32)
    for s in range(4):
        r0 = 32 * s
        for c in range(8):
            cols[r0 + c, 0] = cw0[c, 0, 1, 1]
            cols[r0 + c, 1] = cb0[c]
            cols[r0 + c, 2] = cw0[c, 0, 1, 0]
            cols[r0 + c, 3] = cw0[c, 0, 1, 2]
            cols[r0 + 9 + c, 0] = cw1[c, 0, 1, 1]
            cols[r0 + 9 + c, 1] = cb1[c]
            cols[r0 + 9 + c, 4] = cw1[c, 0, 0, 1]
            cols[r0 + 9 + c, 5] = cw1[c, 0, 2, 1]
    return cols

def ln_cols(ln_w, ln_b, proj_b):
    cols = np.zeros((128, 3), np.float32)
    for p_ in range(128):
        c = p_ % 16
        cols[p_] = (ln_w[c], ln_b[c], proj_b[c])
    return cols

def ffn_weight_tiles(wg, wu, wd):
    gu = np.zeros((64, 128, 2, 16, 128), np.int8)
    for ic in range(64):
        gu[ic, :, 0] = wg[:, ic * 128:(ic + 1) * 128].reshape(16, 128, 128).transpose(1, 0, 2)
        gu[ic, :, 1] = wu[:, ic * 128:(ic + 1) * 128].reshape(16, 128, 128).transpose(1, 0, 2)
    dn = np.zeros((4, 32, 128, 2, 512), np.int8)
    for oc in range(4):
        for ic in range(64):
            dn[oc, ic // 2, :, ic % 2] = wd[ic * 128:(ic + 1) * 128,
                                            oc * 512:(oc + 1) * 512]
    return (np.ascontiguousarray(gu.reshape(64 * 128, 4096)),
            np.ascontiguousarray(dn.reshape(4 * 32 * 128, 1024)))


class Sched:
    def __init__(self):
        self.ops = []
        self.counts = {}

    def emit(self, engine, fn, waits=(), incs=()):
        self.ops.append((engine, fn, [w for w in waits if w is not None], list(incs)))

    def inc(self, sem, amt):
        k = id(sem)
        self.counts[k] = self.counts.get(k, 0) + amt
        return (sem, amt)

    def at(self, sem):
        return (sem, self.counts.get(id(sem), 0))

    def run(self, engine, eng):
        for e, fn, waits, incs in self.ops:
            if e != engine:
                continue
            for sem, val in waits:
                if val > 0:
                    eng.wait_ge(sem, val)
            inst = fn(eng)
            for sem, amt in incs:
                inst.then_inc(sem, amt)


def apx(base, offset, dims):
    t = getattr(base, "tensor", base)
    fsz = int(np.prod(t.shape[1:]))
    np_ = int(t.shape[0])
    ap = [[fsz, np_]] + [list(d) for d in dims]
    return bass.AP(t, offset, ap)


class Prog:
    def __init__(self, nrows=2048, nbatch=4, taps=(), stage=9, rd=False):
        assert nrows % (32 * nbatch) == 0 and nrows % 512 == 0
        self.stage = stage
        self.rd = rd
        self.nrows = nrows
        self.nbatch = nbatch
        self.rpb = nrows // nbatch
        self.nsub = nrows // 32
        self.ntok = nrows * 128
        self.taps = set(taps)

    def build(self):
        p = self
        NR, NS, NT = p.nrows, p.nsub, p.ntok
        NSEG = NS // 4
        L16S = min(8, NS)
        RT = NR // 128
        MT = p.rpb // 16
        nc = bass.Bass("TRN2", target_bir_lowering=False, detect_race_conditions=self.rd)

        ein, eout = "ExternalInput", "ExternalOutput"
        xin = nc.dram_tensor("xin", [NR, DM], F32, kind=ein).ap()
        xperm = nc.dram_tensor("xperm", [NR, DM], F32, kind=ein).ap()
        qmap_d = nc.dram_tensor("qmapw", [64, 128], F32, kind=ein).ap()
        kmap_d = nc.dram_tensor("kmapw", [64, 128], F32, kind=ein).ap()
        vmap_d = nc.dram_tensor("vmapw", [64, 128], F32, kind=ein).ap()
        auxin_d = nc.dram_tensor("auxin", [9, 4096], F32, kind=ein).ap()
        ident_d = nc.dram_tensor("identw", [128, 128], F32, kind=ein).ap()
        lepmap_d = nc.dram_tensor("lepmapw", [128, 64], F32, kind=ein).ap()
        projbd_d = nc.dram_tensor("projbdw", [128, 128], F32, kind=ein).ap()
        bd16_d = nc.dram_tensor("bd16w", [128, 128], F32, kind=ein).ap()
        cw_d = nc.dram_tensor("cwcols", [128, 6], F32, kind=ein).ap()
        lnc_d = nc.dram_tensor("lncols", [128, 3], F32, kind=ein).ap()
        wgu_d = nc.dram_tensor("wgu", [64 * 128, 4096], I8, kind=ein).ap()
        wdn_d = nc.dram_tensor("wdn", [4 * 32 * 128, 1024], I8, kind=ein).ap()
        sg_d = nc.dram_tensor("sgcol", [128, 64], F32, kind=ein).ap()
        su_d = nc.dram_tensor("sucol", [128, 64], F32, kind=ein).ap()
        sdn_d = nc.dram_tensor("sdnw", [128, DM], F32, kind=ein).ap()
        eps_d = nc.dram_tensor("epscol", [128, 1], F32, kind=ein).ap()

        out_d = nc.dram_tensor("out", [NR, DM], F32, kind=eout).ap()

        lz = nc.dram_tensor("lz", [NR, DM], BF16, kind="Internal").ap()
        avp = nc.dram_tensor("avp", [16, NT], BF16, kind="Internal").ap()
        lepp = nc.dram_tensor("lepp", [16, NT], BF16, kind="Internal").ap()
        attnd = nc.dram_tensor("attnd", [16, NT], BF16, kind="Internal").ap()
        x1d = nc.dram_tensor("x1d", [NR, DM], F32, kind="Internal").ap()

        tapd = {}
        def mktap(name, shape, dtype=F32):
            if name in p.taps:
                tapd[name] = nc.dram_tensor("tap_" + name, shape, dtype, kind=eout).ap()
            return tapd.get(name)

        t_lz = mktap("lz", [NR, DM])
        t_qbuf = mktap("qbuf", [128, 4096])
        t_kbuf = mktap("kbuf", [128, 4096])
        t_vsb = mktap("vsb", [128, 4096])
        t_lep = mktap("lep", [128, 4096])
        t_asb = mktap("asb", [128, 2048])
        t_att1 = mktap("att1", [128, 128])
        t_avp = mktap("avp", [16, NT])
        t_lepp = mktap("lepp", [16, NT])
        t_attn = mktap("attn", [16, NT])
        t_x1 = mktap("x1", [NR, DM])
        t_fm = mktap("fm", [128, 16 * NR])

        S = Sched()
        FMW = 16 * NR

        with contextlib.ExitStack() as _st:
            _en = _st.enter_context
            qmap = _en(nc.sbuf_tensor("c_qmap", [64, 128], BF16))
            kmap = _en(nc.sbuf_tensor("c_kmap", [64, 128], BF16))
            vmap = _en(nc.sbuf_tensor("c_vmap", [64, 128], BF16))
            ident = _en(nc.sbuf_tensor("c_ident", [128, 128], BF16))
            lepmap = _en(nc.sbuf_tensor("c_lepmap", [128, 64], BF16))
            projbd = _en(nc.sbuf_tensor("c_projbd", [128, 128], BF16))
            bd16 = _en(nc.sbuf_tensor("c_bd16", [128, 128], BF16))
            cw = _en(nc.sbuf_tensor("c_cw", [128, 6], F32))
            lnc = _en(nc.sbuf_tensor("c_lnc", [128, 3], F32))
            epsc = _en(nc.sbuf_tensor("c_eps", [128, 1], F32))
            sg = _en(nc.sbuf_tensor("c_sg", [128, 64], F32))
            su = _en(nc.sbuf_tensor("c_su", [128, 64], F32))
            sdn = _en(nc.sbuf_tensor("c_sdn", [128, DM], BF16))
            af32 = _en(nc.sbuf_tensor("af32", [128, 6176], F32))
            ab = _en(nc.sbuf_tensor("ab", [128, 75776], BF16))
            gsil = _en(nc.sbuf_tensor("gsil", [128, 512], BF16))
            wdi0 = _en(nc.sbuf_tensor("wdi0", [128, 1024], I8))
            wdi1 = _en(nc.sbuf_tensor("wdi1", [128, 1024], I8))
            wgi0 = _en(nc.sbuf_tensor("wgi0", [128, 4096], I8))
            wgi1 = _en(nc.sbuf_tensor("wgi1", [128, 4096], I8))
            rsb = _en(nc.sbuf_tensor("rsb", [128, 16], F32))
            wg0 = _en(nc.sbuf_tensor("wg0", [128, 4096], BF16))
            wg1 = _en(nc.sbuf_tensor("wg1", [128, 4096], BF16))

            pb0 = _en(nc.psum_tensor("pb0", [128, 512], F32))
            pb1 = _en(nc.psum_tensor("pb1", [128, 512], F32))
            pb2 = _en(nc.psum_tensor("pb2", [128, 512], F32))
            psgrp = _en(nc.psum_tensor("psgrp", [128, 2048], F32))
            pb7 = _en(nc.psum_tensor("pb7", [128, 512], F32))
            sdma = _en(nc.semaphore("sdma"))
            gdma = _en(nc.semaphore("gdma"))
            tdma = _en(nc.semaphore("tdma"))
            smm = _en(nc.semaphore("smm"))
            sev = _en(nc.semaphore("sev"))
            ssc = _en(nc.semaphore("ssc"))
            sgp = _en(nc.semaphore("sgp"))
            wcv = _en(nc.semaphore("wcv"))
            sbar = _en(nc.semaphore("sbar"))
            blk = _en(nc.Block())

            def A(c0_, w):
                return ab[:, c0_:c0_ + w]
            fmsb = A(0, FMW)
            B0 = FMW
            fuse = A(B0, 32768)
            qb0 = A(B0, 4096); qb1 = A(B0 + 4096, 4096)
            kb0 = A(B0 + 8192, 4096); kb1 = A(B0 + 12288, 4096)
            vsb = A(B0 + 16384, 4096); lepb = A(B0 + 20480, 4096)
            vtsb = A(B0 + 24576, 4096)
            asb0 = A(B0 + 28672, 1024); asb1 = A(B0 + 29696, 1024)

            l16a = A(B0, 4096); l16w = A(B0 + 4096, 4096)
            l16d = A(B0 + 8192, 4096); l16o = A(B0 + 12288, 4096)
            C0 = B0 + 32768
            ffnsb = A(C0, 4 * DM)
            fla = ab[0:64, C0:C0 + 4096]
            flb = ab[0:64, C0 + 4096:C0 + 8192]
            rowsb = A(C0 + 8192, 2048)
            att1 = A(C0 + 8192, 128)
            cmsb = A(C0 + 8320, 128)
            avps = A(C0 + 8448, 128)
            lepps = A(C0 + 8576, 512)
            wd0 = A(C0 + 8192, 1024)
            wd1 = A(C0 + 9216, 1024)
            rows = af32[:, 0:2048]
            rows2 = af32[:, 2048:4096]
            rows3 = af32[:, 4096:6144]
            stat = af32[:, 6144:6176]
            l16r = af32[:, 0:4096]

            psq, psk, psv = pb0, pb1, pb2
            psav, psm, psm2 = pb0, pb2, pb7
            pstr = pb1[:, :].bitcast(BF16)
            pstr2 = pb2[:, :].bitcast(BF16)

            nbar = [0]

            def barrier():
                nbar[0] += 1
                tgt = 5 * nbar[0]
                ws = {"sync": [S.at(sdma)], "gpsimd": [S.at(gdma)],
                      "scalar": [S.at(tdma)], "tensor": [], "vector": []}
                for en in ("sync", "gpsimd", "tensor", "vector", "scalar"):
                    S.emit(en, lambda e: e.nop(), ws[en], [S.inc(sbar, 1)])
                for en in ("sync", "gpsimd", "tensor", "vector", "scalar"):
                    S.emit(en, lambda e: e.nop(), [(sbar, tgt)], [])

            def dma(engine, sem, dst, src, waits=()):
                S.emit(engine, lambda e, d=dst, s=src: e.dma_start(out=d, in_=s),
                       waits, [S.inc(sem, 16)])
                return S.at(sem)

            for d, s_ in ((qmap, qmap_d), (kmap, kmap_d), (vmap, vmap_d),
                          (ident, ident_d), (lepmap, lepmap_d),
                          (projbd, projbd_d), (bd16, bd16_d),
                          (sdn, sdn_d)):
                dma("gpsimd", gdma, d[:], s_)
            for d, s_ in ((cw, cw_d), (lnc, lnc_d), (sg, sg_d), (su, su_d), (epsc, eps_d)):
                dma("sync", sdma, d[:], s_)
            S.emit("vector", lambda e: e.memset(fla[:], 0.0), [], [S.inc(sev, 1)])
            S.emit("vector", lambda e: e.memset(flb[:], 0.0), [], [S.inc(sev, 1)])
            mz = S.at(sev)
            for buf in (fla, flb):
                for st in (0, 32):
                    dma("gpsimd", gdma, buf[st + 16:st + 25, :], auxin_d, [mz])
            barrier()

            def ln_stats(src, junk, waits=()):
                waits = list(waits)
                S.emit("vector", lambda e: e.reduce_sum(stat[:, 0:1], src, AX.X),
                       waits + [S.at(ssc)], [S.inc(sev, 1)])
                S.emit("scalar", lambda e: e.activation(
                    junk, src, AF.Square, accum_out=stat[:, 1:2]),
                    waits + [S.at(sev)], [S.inc(ssc, 1)])
                S.emit("vector", lambda e: e.tensor_scalar(
                    stat[:, 2:4], stat[:, 0:2], 1.0 / DM, None, ALU.mult),
                    [S.at(ssc), S.at(sev)], [S.inc(sev, 1)])
                S.emit("vector", lambda e: e.tensor_tensor(
                    stat[:, 4:5], stat[:, 2:3], stat[:, 2:3], ALU.mult),
                    [S.at(sev)], [S.inc(sev, 1)])
                S.emit("vector", lambda e: e.tensor_sub(
                    stat[:, 5:6], stat[:, 3:4], stat[:, 4:5]), [S.at(sev)],
                    [S.inc(sev, 1)])
                S.emit("scalar", lambda e: e.activation(
                    stat[:, 6:7], stat[:, 5:6], AF.Ln, bias=epsc[:, 0:1]),
                    [S.at(sev)], [S.inc(ssc, 1)])
                S.emit("scalar", lambda e: e.activation(
                    stat[:, 7:8], stat[:, 6:7], AF.Exp, scale=-0.5), [S.at(ssc)],
                    [S.inc(ssc, 1)])
                return S.at(ssc)

            for rt in range(RT):
                w_ld = dma("sync", sdma, rows[:], xin[rt * 128:(rt + 1) * 128, :],
                           [S.at(sev)])
                w_rstd = ln_stats(rows[:], rows2[:], [w_ld])
                S.emit("vector", lambda e: e.tensor_scalar(
                    rowsb.rearrange("p (c w) -> p w c", c=16),
                    rows.rearrange("p (w c) -> p w c", c=16),
                    stat[:, 2:3], stat[:, 7:8], ALU.subtract, ALU.mult),
                    [w_rstd, S.at(sdma), S.at(gdma)], [S.inc(sev, 1)])
                dma("sync", sdma, lz[rt * 128:(rt + 1) * 128, :], rowsb[:],
                    [S.at(sev)])
                if t_lz is not None:
                    dma("gpsimd", gdma, t_lz[rt * 128:(rt + 1) * 128, :], rowsb[:],
                        [S.at(sev)])
            barrier()

            def lz_cAP(sub):
                return lz.rearrange("r (c w) -> c r w", c=16)[
                    :, sub * 32:(sub + 1) * 32, :]

            def sig_out(buf, cc):
                v = buf.rearrange("p (a b d e) -> p a b d e", a=4, b=16, d=8, e=8)
                return v[:, :, :, cc, :]

            w_qkv_mm_end = None
            w_exp_par = [None, None]
            for seg in range(NSEG if p.stage >= 2 else 0):
                sub0 = seg * 4
                wfl = []
                flw = [w_qkv_mm_end] if w_qkv_mm_end else []
                for ipair, buf in ((0, fla), (1, flb)):
                    for sl_ in range(2):
                        sub = sub0 + 2 * ipair + sl_
                        wfl.append(dma("gpsimd", gdma,
                                       buf[sl_ * 32:sl_ * 32 + 16, :], lz_cAP(sub),
                                       flw))
                psq_r = [psq[:, 0:512], psgrp[:, 0:512]]
                psk_r = [psk[:, 0:512], psgrp[:, 512:1024]]
                psv_r = [psv, psgrp[:, 1024:1536]]
                wq_war = [None, None]
                wk_war = [None, None]
                wv_war = [None, None]
                first_ev = True
                for ipair, buf, qb, kb in ((0, fla, qb0, kb0), (1, flb, qb1, kb1)):
                    for cc in range(8):
                        par = cc % 2
                        gcc = ipair * 8 + cc
                        sl_ = slice(cc * 512, (cc + 1) * 512)
                        qw = [wfl[2 * ipair], wfl[2 * ipair + 1]]
                        if gcc == 0:
                            qw.append(S.at(sev))
                        elif gcc == 1:
                            qw += [w_exp_par[0]] if w_exp_par[0] else []
                        qw += wq_war[par] or []
                        S.emit("tensor", lambda e, b=buf, s=sl_, pp=psq_r[par]:
                               e.matmul(pp, qmap[:], b[:, s],
                                        start=True, stop=True),
                               qw, [S.inc(smm, 1)])
                        S.emit("tensor", lambda e, b=buf, s=sl_, pp=psk_r[par]:
                               e.matmul(pp, kmap[:], b[:, s],
                                        start=True, stop=True),
                               wk_war[par] or [], [S.inc(smm, 1)])
                        wmm = S.at(smm)
                        evw = [wmm, S.at(smm)] if first_ev else [wmm]
                        first_ev = False
                        S.emit("vector", lambda e, d_=qb, c=cc, pp=psq_r[par]:
                               e.tensor_copy(sig_out(d_, c), pp), evw,
                               [S.inc(sev, 1)])
                        wq_war[par] = [S.at(sev)]
                        S.emit("vector", lambda e, d_=kb, c=cc, pp=psk_r[par]:
                               e.tensor_copy(sig_out(d_, c), pp), [],
                               [S.inc(sev, 1)])
                        wk_war[par] = [S.at(sev)]
                for cc in range(8):
                    par = cc % 2
                    sl_ = slice(cc * 512, (cc + 1) * 512)
                    vw = []
                    if cc == 0:
                        vw.append(S.at(sev))
                    elif cc == 1:
                        vw += [w_exp_par[1]] if w_exp_par[1] else []
                    vw += wv_war[par] or []
                    S.emit("tensor", lambda e, s=sl_, pp=psv_r[par]: e.matmul(
                        pp[0:64, 0:512], vmap[0:64, 0:64], fla[:, s],
                        start=True, stop=True), vw, [S.inc(smm, 1)])
                    S.emit("tensor", lambda e, s=sl_, pp=psv_r[par]: e.matmul(
                        pp[64:128, 0:512], vmap[0:64, 0:64], flb[:, s],
                        start=True, stop=True, tile_position=(0, 64)),
                        [], [S.inc(smm, 1)])
                    S.emit("vector", lambda e, c=cc, pp=psv_r[par]: e.tensor_copy(
                        sig_out(vsb, c), pp[:, 0:512]), [S.at(smm)],
                        [S.inc(sev, 1)])
                    wv_war[par] = [S.at(sev)]
                w_qkv_mm_end = S.at(smm)
                w_qkv = S.at(sev)
                w_psgrp_extra = {0: [w_qkv], 1: [w_qkv]}
                if t_qbuf is not None and seg == 0:
                    dma("gpsimd", gdma, t_qbuf, qb0[:], [w_qkv])
                    dma("gpsimd", gdma, t_kbuf, kb0[:], [w_qkv])
                    dma("gpsimd", gdma, t_vsb, vsb[:], [w_qkv])

                pstr_r = [pstr, pstr2]
                w_vt_war = [None, None]
                for b4 in range(8):
                    par = b4 % 2
                    for j in range(4):
                        beta = b4 * 4 + j
                        jw = ([w_qkv, S.at(sev)] if b4 < 2 else
                              [w_qkv] + (w_vt_war[par] or [])) if j == 0 else []
                        S.emit("tensor", lambda e, b=beta, jj=j, pp=pstr_r[par]:
                               e.transpose(pp[:, jj * 128:(jj + 1) * 128],
                                           vsb[:, b * 128:(b + 1) * 128],
                                           ident[:]), jw, [S.inc(smm, 1)])
                    S.emit("vector", lambda e, b=b4, pp=pstr_r[par]: e.tensor_copy(
                        vtsb[:, b * 512:(b + 1) * 512], pp[:, 0:512]),
                        [S.at(smm)], [S.inc(sev, 1)])
                    w_vt_war[par] = [S.at(sev)]
                w_vt = S.at(sev)
                S.emit("vector", lambda e: e.tensor_scalar(
                    lepb[:], vsb[:], cw[:, 0:1], cw[:, 1:2], ALU.mult, ALU.add),
                    [w_qkv, S.at(smm)], [S.inc(sev, 1)])
                vv = vsb.rearrange("p (w e) -> p w e", e=8)
                ll = lepb.rearrange("p (w e) -> p w e", e=8)
                S.emit("vector", lambda e: e.scalar_tensor_tensor(
                    ll[:, :, 1:8], vv[:, :, 0:7], cw[:, 2:3], ll[:, :, 1:8],
                    ALU.mult, ALU.add), [S.at(sev)], [S.inc(sev, 1)])
                S.emit("vector", lambda e: e.scalar_tensor_tensor(
                    ll[:, :, 0:7], vv[:, :, 1:8], cw[:, 3:4], ll[:, :, 0:7],
                    ALU.mult, ALU.add), [S.at(sev)], [S.inc(sev, 1)])
                vv8 = vsb.rearrange("p (b d e) -> p b d e", d=8, e=8)
                ll8 = lepb.rearrange("p (b d e) -> p b d e", d=8, e=8)
                S.emit("vector", lambda e: e.scalar_tensor_tensor(
                    ll8[:, :, 1:8, :], vv8[:, :, 0:7, :], cw[:, 4:5],
                    ll8[:, :, 1:8, :], ALU.mult, ALU.add), [S.at(sev)],
                    [S.inc(sev, 1)])
                S.emit("vector", lambda e: e.scalar_tensor_tensor(
                    ll8[:, :, 0:7, :], vv8[:, :, 1:8, :], cw[:, 5:6],
                    ll8[:, :, 0:7, :], ALU.mult, ALU.add), [S.at(sev)],
                    [S.inc(sev, 1)])
                w_lep = S.at(sev)
                if t_lep is not None and seg == 0:
                    dma("gpsimd", gdma, t_lep, lepb[:], [w_lep])

                for s4 in range(4 if p.stage >= 3 else 0):
                    sub = sub0 + s4
                    qb, kb = (qb0, kb0) if s4 < 2 else (qb1, kb1)
                    st1 = (s4 % 2) * 64
                    st2 = st1 + 32

                    def emit_grams(bg):
                        half = bg % 2
                        hof = half * 1024
                        a_sb = asb0 if half == 0 else asb1
                        w_war = [w_exp_par[half]] if w_exp_par[half] else []
                        w_war += w_psgrp_extra.pop(half, [])
                        for j in range(4):
                            beta = bg * 4 + j
                            csl = slice(beta * 128, (beta + 1) * 128)
                            for br, st in ((0, st1), (1, st2)):
                                u = br * 4 + j
                                S.emit("tensor", lambda e, k_=kb, q_=qb, s=csl,
                                       r0=st, uu=u, ho=hof: e.matmul(
                                    psgrp[:, ho + uu * 128:ho + (uu + 1) * 128],
                                    k_[r0:r0 + 20, s], q_[r0:r0 + 20, s],
                                    start=True, stop=True, tile_position=(r0, 0)),
                                    w_war if (j == 0 and br == 0) else [],
                                    [S.inc(smm, 1)])
                        S.emit("scalar", lambda e, ho=hof, ab_=a_sb: e.activation(
                            ab_[:, 0:1024], psgrp[:, ho:ho + 1024], AF.Exp,
                            scale=SQK), [S.at(smm)], [S.inc(ssc, 1)])
                        w_exp_par[half] = S.at(ssc)
                        return (bg, S.at(ssc), a_sb)

                    def emit_av(info):
                        bg, w_exp, a_sb = info
                        if t_asb is not None and seg == 0 and s4 == 0 and bg < 2:
                            dma("gpsimd", gdma, t_asb[:, bg * 1024:(bg + 1) * 1024],
                                a_sb[:, 0:1024], [w_exp])
                        for j in range(4):
                            beta = bg * 4 + j
                            jj = (bg % 2) * 4 + j
                            for br in range(2):
                                u = br * 4 + j
                                r0_ = beta * 128 + s4 * 32 + br * 8
                                S.emit("tensor", lambda e, uu=u, jx=jj, b_=br,
                                       r=r0_, ab_=a_sb: e.matmul(
                                    psav[:, jx * 18 + b_ * 9:jx * 18 + b_ * 9 + 9],
                                    ab_[:, uu * 128:(uu + 1) * 128],
                                    vtsb[:, r:r + 9], start=True, stop=True),
                                    [w_exp, w_vt, S.at(sev)]
                                    if (j == 0 and br == 0) else [],
                                    [S.inc(smm, 1)])

                    def emit_ppd(bgodd):
                        w_av = S.at(smm)
                        S.emit("vector", lambda e: e.reciprocal(
                            rsb[:, 0:16].rearrange("p (a b) -> p a b", b=2),
                            apx(psav, 8, [[18, 8], [1, 2]])), [w_av],
                            [S.inc(sev, 1)])
                        S.emit("vector", lambda e: e.tensor_tensor(
                            att1.rearrange("p (j b c) -> p j b c", b=2, c=8),
                            apx(psav, 0, [[18, 8], [10, 2], [1, 8]]),
                            apx(rsb, 0, [[2, 8], [1, 2], [0, 8]]),
                            ALU.mult), [S.at(smm), S.at(sev)], [S.inc(sev, 1)])
                        w_n = S.at(sev)
                        if t_att1 is not None and seg == 0 and s4 == 0 and bgodd == 1:
                            dma("gpsimd", gdma, t_att1, att1[:], [w_n])
                        return (bgodd, w_n)

                    def emit_ppt(info):
                        bgodd, w_n = info
                        S.emit("tensor", lambda e: e.transpose(
                            pstr[:, 512:640], att1[:], ident[:]),
                            [w_n], [S.inc(smm, 1)])
                        S.emit("vector", lambda e: e.tensor_copy(
                            cmsb[:], pstr[:, 512:640]), [S.at(smm)],
                            [S.inc(sev, 1)])
                        S.emit("tensor", lambda e: e.matmul(
                            psm2[:, 0:128], projbd[:], cmsb[:],
                            start=True, stop=True), [S.at(sev)], [S.inc(smm, 1)])
                        S.emit("vector", lambda e: e.tensor_copy(
                            avps[:], psm2[:, 0:128]), [S.at(smm), S.at(sdma)],
                            [S.inc(sev, 1)])
                        bgp = bgodd // 2
                        dst = avp.rearrange("c (u b t) -> u b c t", u=NS, t=128)[
                            sub, bgp * 8:(bgp + 1) * 8, :, :]
                        dma("sync", sdma, dst, avps[:], [S.at(sev)])

                    def emit_projlep(cc):
                        sl_ = slice(cc * 512, (cc + 1) * 512)
                        S.emit("tensor", lambda e, s=sl_: e.matmul(
                            psm[0:64, 0:512], lepmap[:], lepb[:, s], start=True,
                            stop=True), [w_lep, w_qkv, S.at(sev)],
                            [S.inc(smm, 1)])
                        S.emit("vector", lambda e: e.tensor_copy(
                            lepps[0:64, :], psm[0:64, 0:512]),
                            [S.at(smm), S.at(sdma)], [S.inc(sev, 1)])
                        dst = lepp.rearrange("c (u t) -> u c t", t=4096)[
                            sub0:sub0 + 4, :, cc * 512:(cc + 1) * 512]
                        dma("sync", sdma, dst, lepps[0:64, :], [S.at(sev)])

                    pend = None
                    ptp = None
                    for bg in range(8):
                        info = emit_grams(bg)
                        if pend is not None:
                            emit_av(pend)
                            if pend[0] % 2 == 1:
                                if ptp is not None:
                                    emit_ppt(ptp)
                                ptp = emit_ppd(pend[0])
                                if pend[0] in (1, 3):
                                    emit_projlep(s4 * 2 + pend[0] // 2)
                        pend = info
                    emit_av(pend)
                    emit_ppt(ptp)
                    emit_ppt(emit_ppd(7))


            barrier()

            if t_avp is not None:
                dma("gpsimd", gdma, t_avp, avp)
            if t_lepp is not None:
                dma("gpsimd", gdma, t_lepp, lepp)

            for g in range(NS // L16S if p.stage >= 7 else 0):
                u0 = g * L16S
                pr = L16S * 16
                srcA = avp.rearrange("c (u t) -> u c t", t=4096)[u0:u0 + L16S]
                srcB = lepp.rearrange("c (u t) -> u c t", t=4096)[u0:u0 + L16S]
                wa = dma("gpsimd", gdma, l16a[0:pr, :], srcA, [S.at(sev)])
                wb_ = dma("gpsimd", gdma, l16d[0:pr, :], srcB, [S.at(sev)])
                S.emit("vector", lambda e: e.scalar_tensor_tensor(
                    l16w[0:pr, :], l16a[0:pr, :], lnc[0:pr, 2:3], l16d[0:pr, :],
                    ALU.add, ALU.add), [wa, wb_], [S.inc(sev, 1)])
                w_w = S.at(sev)
                for cc in range(8):
                    sl_ = slice(cc * 512, (cc + 1) * 512)
                    S.emit("tensor", lambda e, s=sl_: e.matmul(
                        psm[:, 0:512], bd16[:], l16w[:, s], start=True, stop=True),
                        [w_w, S.at(sev)], [S.inc(smm, 1)])
                    S.emit("vector", lambda e, s=sl_: e.tensor_sub(
                        l16d[:, s], l16w[:, s], psm[:, 0:512]), [S.at(smm)],
                        [S.inc(sev, 1)])
                    S.emit("scalar", lambda e, s=sl_: e.activation(
                        l16a[:, s], l16d[:, s], AF.Square), [S.at(sev)],
                        [S.inc(ssc, 1)])
                    S.emit("tensor", lambda e, s=sl_: e.matmul(
                        psm2[:, 0:512], bd16[:], l16a[:, s], start=True,
                        stop=True), [S.at(ssc)], [S.inc(smm, 1)])
                    S.emit("scalar", lambda e, s=sl_: e.activation(
                        l16r[:, s], psm2[:, 0:512], AF.Ln, bias=epsc[:, 0:1]),
                        [S.at(smm)], [S.inc(ssc, 1)])
                    S.emit("scalar", lambda e, s=sl_: e.activation(
                        l16r[:, s], l16r[:, s], AF.Exp, scale=-0.5), [S.at(ssc)],
                        [S.inc(ssc, 1)])
                    S.emit("vector", lambda e, s=sl_: e.tensor_mul(
                        l16d[:, s], l16d[:, s], l16r[:, s]), [S.at(ssc)],
                        [S.inc(sev, 1)])
                    lam = cc
                    o4 = l16o.rearrange("p (d cz wh wl e) -> p d cz wh wl e",
                                        d=8, cz=4, wh=2, wl=8, e=8)
                    ov = o4[:, :, lam >> 1, lam & 1, :, :].transpose([0, 2, 1, 3])
                    S.emit("vector", lambda e, o=ov, s=sl_: e.tensor_scalar(
                        o, l16d[:, s].rearrange("p (w d e) -> p w d e", d=8, e=8),
                        lnc[:, 0:1], lnc[:, 1:2], ALU.mult, ALU.add),
                        [S.at(sev)] + ([S.at(sdma)] if cc == 0 else []),
                        [S.inc(sev, 1)])
                dst = attnd.rearrange("c (u t) -> u c t", t=4096)[u0:u0 + L16S]
                dma("sync", sdma, dst, l16o[0:pr, :], [S.at(sev)])
            barrier()
            if t_attn is not None:
                dma("gpsimd", gdma, t_attn, attnd)
                barrier()

            ncv = 128 // MT
            p3s1 = A(B0, 2048)
            p3s2 = A(B0 + 2048, 2048)
            fm_rdy = []
            x1_rdy = []
            for rt in range(RT if p.stage >= 8 else 0):
                r0 = rt * 128
                bi = r0 // p.rpb
                ci = (r0 % p.rpb) // MT
                w_pm = []
                for j in range(ncv):
                    c = ci + j
                    asrc = attnd[c:c + 1, :].rearrange(
                        "o (b m w) -> o b m w", b=p.nbatch, w=DM)[0, bi]
                    w_pm.append(dma("sync", sdma,
                                    p3s1[j * MT:(j + 1) * MT, :], asrc,
                                    [S.at(sev)]))
                    lsrc = lz.rearrange("(b r) (c w) -> b r c w",
                                        b=p.nbatch, c=16)[bi, :, c, :].rearrange(
                        "(m h) w -> m h w", h=16)
                    w_pm.append(dma("scalar", tdma,
                                    p3s2[j * MT:(j + 1) * MT, :].rearrange(
                                        "p (h w) -> p h w", w=128), lsrc,
                                    [S.at(sev)]))
                w_x0 = dma("sync", sdma, rows[:], xperm[r0:r0 + 128, :],
                           [S.at(sev)])
                S.emit("vector", lambda e: e.tensor_add(
                    rows2[:], p3s1[:], p3s2[:]), w_pm, [S.inc(sev, 1)])
                w_rstd = ln_stats(rows2[:], rows3[:], [S.at(sev)])
                S.emit("vector", lambda e: e.tensor_scalar(
                    rows2[:], rows2[:], stat[:, 2:3], stat[:, 7:8],
                    ALU.subtract, ALU.mult), [w_rstd], [S.inc(sev, 1)])
                S.emit("vector", lambda e: e.tensor_add(
                    rows[:], rows[:], rows2[:]), [w_x0, S.at(sev)],
                    [S.inc(sev, 1)])
                w_x1 = S.at(sev)
                w_x1d = dma("sync", sdma, x1d[r0:r0 + 128, :], rows[:], [w_x1])
                if t_x1 is not None:
                    dma("sync", sdma, t_x1[r0:r0 + 128, :], rows[:], [w_x1])
                w_rstd = ln_stats(rows[:], rows3[:])
                S.emit("vector", lambda e: e.tensor_scalar(
                    rowsb[:], rows[:], stat[:, 2:3], stat[:, 7:8],
                    ALU.subtract, ALU.mult), [w_rstd, S.at(tdma)],
                    [S.inc(sev, 1)])
                w_ln2 = S.at(sev)
                for kc in range(16):
                    S.emit("scalar", lambda e, k=kc, r=rt: e.dma_start_transpose(
                        out=fmsb[:, k * NR + r * 128:k * NR + (r + 1) * 128],
                        in_=rowsb[:, k * 128:(k + 1) * 128]),
                        [w_ln2], [S.inc(tdma, 16)])
                x1_rdy.append(w_x1d)
                if rt % 4 == 3:
                    fm_rdy.append(S.at(tdma))
            w_p3t = S.at(tdma) if (RT if p.stage >= 8 else 0) else None
            barrier()
            if t_fm is not None:
                dma("gpsimd", gdma, t_fm, fmsb[:], [w_p3t])
                barrier()

            NTT = (NR // 512) if p.stage >= 9 else 0
            wgub = [wg0, wg1]
            wgi = [wgi0, wgi1]
            wdb = [wd0, wd1]
            wdi = [wdi0, wdi1]
            w_giwar = [[], []]
            w_gcast = [[], []]
            w_gmm = [[], []]
            w_wiwar = [[], []]
            w_wdwar = [[w_p3t], [w_p3t]] if p.stage >= 8 else [[], []]
            psg_par = [pb0, pb2]
            psu_par = [pb1, pb7]
            w_sil_par = [None, None]
            w_stt_par = [None, None]
            gc = {}

            def g_loadcast(tt, ic):
                par = ic % 2
                wla = dma("sync", sdma, wgi[par][:, 0:2048],
                          wgu_d[ic * 128:(ic + 1) * 128, 0:2048],
                          w_giwar[par])
                wlb = dma("scalar", tdma, wgi[par][:, 2048:4096],
                          wgu_d[ic * 128:(ic + 1) * 128, 2048:4096],
                          w_giwar[par])
                S.emit("vector", lambda e, b=par: e.tensor_copy(
                    wgub[b][:], wgi[b][:]), [wla, wlb] + w_gmm[par],
                    [S.inc(sev, 1)])
                w_giwar[par] = [S.at(sev)]
                gc[(tt, ic)] = S.at(sev)

            for tt in range(NTT):
                if tt == 0:
                    g_loadcast(0, 0)
                    g_loadcast(0, 1)
                for ic in range(64):
                    par = ic % 2
                    pg, pu = psg_par[par], psu_par[par]
                    war = [w for w in (w_sil_par[par], w_stt_par[par]) if w]
                    for kc in range(16):
                        S.emit("tensor", lambda e, k=kc, b=par, t=tt, pp=pg: e.matmul(
                            pp[:, 0:512], wgub[b][:, k * 128:(k + 1) * 128],
                            fmsb[:, k * NR + t * 512:k * NR + (t + 1) * 512],
                            start=(k == 0), stop=(k == 15)),
                            ([gc[(tt, ic)]] + war +
                             ([fm_rdy[tt]] if ic == 0 else []))
                            if kc == 0 else [], [])
                        S.emit("tensor", lambda e, k=kc, b=par, t=tt, pp=pu: e.matmul(
                            pp[:, 0:512], wgub[b][:, 2048 + k * 128:2048 + (k + 1) * 128],
                            fmsb[:, k * NR + t * 512:k * NR + (t + 1) * 512],
                            start=(k == 0), stop=(k == 15)),
                            [], [S.inc(smm, 1)] if kc == 15 else [])
                    wmm = S.at(smm)
                    w_gmm[par] = [wmm]
                    S.emit("scalar", lambda e, i=ic, pp=pg: e.activation(
                        gsil[:], pp[:, 0:512], AF.Silu, scale=sg[:, i:i + 1]),
                        [wmm, S.at(sev)], [S.inc(ssc, 1)])
                    w_sil_par[par] = S.at(ssc)
                    if ic + 2 < 64:
                        g_loadcast(tt, ic + 2)
                    S.emit("vector", lambda e, i=ic, pp=pu: e.scalar_tensor_tensor(
                        fuse[:, i * 512:(i + 1) * 512], pp[:, 0:512],
                        su[:, i:i + 1], gsil[:], ALU.mult, ALU.mult),
                        [S.at(ssc)], [S.inc(sev, 1)])
                    w_stt_par[par] = S.at(sev)
                w_fuse = S.at(sev)
                for oc in range(4):
                    for icp in range(32):
                        par = icp % 2
                        wld = dma("sync" if par == 0 else "scalar",
                                  sdma if par == 0 else tdma, wdi[par][:],
                                  wdn_d[(oc * 32 + icp) * 128:
                                        (oc * 32 + icp + 1) * 128, :],
                                  w_wiwar[par])
                        S.emit("vector", lambda e, b=par: e.tensor_copy(
                            wdb[b][:], wdi[b][:]), [wld] + w_wdwar[par],
                            [S.inc(sev, 1)])
                        w_wiwar[par] = [S.at(sev)]
                        wcp = S.at(sev)
                        for half in range(2):
                            ic = icp * 2 + half
                            for tb in range(4):
                                S.emit("tensor", lambda e, i=ic, t=tb, b=par,
                                       h=half: e.matmul(
                                    psgrp[:, t * 512:(t + 1) * 512],
                                    fuse[:, i * 512 + t * 128:i * 512 + (t + 1) * 128],
                                    wdb[b][:, h * 512:(h + 1) * 512],
                                    start=(i == 0), stop=(i == 63)),
                                    ([wcp, w_fuse, S.at(sev)]
                                     if (icp == 0 and half == 0 and tb == 0) else
                                     ([wcp] if (half == 0 and tb == 0) else [])),
                                    [S.inc(smm, 1)] if (half == 1 and tb == 3)
                                    else [])
                        w_wdwar[par] = [S.at(smm)]
                    wmm = S.at(smm)
                    for tb in range(4):
                        S.emit("vector", lambda e, t=tb, o=oc: e.tensor_copy(
                            ffnsb[:, t * DM + o * 512:t * DM + (o + 1) * 512],
                            psgrp[:, t * 512:(t + 1) * 512]), [wmm],
                            [S.inc(sev, 1)])
                w_dn = S.at(sev)
                if tt + 1 < NTT:
                    for b_ in range(2):
                        w_giwar[b_] = (w_giwar[b_] or []) + [w_dn]
                        w_gmm[b_] = (w_gmm[b_] or []) + [w_dn]
                    g_loadcast(tt + 1, 0)
                    g_loadcast(tt + 1, 1)

                for tb in range(4):
                    rt = tt * 4 + tb
                    r0 = rt * 128
                    S.emit("vector", lambda e, t=tb: e.tensor_mul(
                        rows2[:], ffnsb[:, t * DM:(t + 1) * DM], sdn[:]),
                        [w_dn], [S.inc(sev, 1)])
                    w_xr = dma("gpsimd", gdma, rows[:], x1d[r0:r0 + 128, :],
                               [S.at(sev), x1_rdy[rt]])
                    w_rstd = ln_stats(rows2[:], rows3[:], [S.at(sev)])
                    S.emit("vector", lambda e: e.tensor_scalar(
                        rows2[:], rows2[:], stat[:, 2:3], stat[:, 7:8],
                        ALU.subtract, ALU.mult), [w_rstd], [S.inc(sev, 1)])
                    S.emit("vector", lambda e: e.tensor_add(
                        rows[:], rows[:], rows2[:]), [w_xr, S.at(sev)],
                        [S.inc(sev, 1)])
                    dma("gpsimd", gdma, out_d[r0:r0 + 128, :], rows[:],
                        [S.at(sev)])
            S.emit("sync", lambda e: e.nop(), [S.at(sdma)], [])
            S.emit("gpsimd", lambda e: e.nop(), [S.at(gdma)], [])
            S.emit("scalar", lambda e: e.nop(), [S.at(tdma)], [])

            @blk.sync
            def _(eng):
                S.run("sync", eng)
            @blk.gpsimd
            def _(eng):
                S.run("gpsimd", eng)
            @blk.tensor
            def _(eng):
                S.run("tensor", eng)
            @blk.vector
            def _(eng):
                S.run("vector", eng)
            @blk.scalar
            def _(eng):
                S.run("scalar", eng)
        return nc, tapd



def host_consts(inputs):
    qkv_w = np.asarray(inputs["L0_attn_qkv_w"], np.float32)
    proj_w = np.asarray(inputs["L0_attn_proj_w"], np.float32)
    qm, km = qk_maps(qkv_w)
    wgu, wdn = ffn_weight_tiles(np.asarray(inputs["L0_ffn_w_gate"]),
                                np.asarray(inputs["L0_ffn_w_up"]),
                                np.asarray(inputs["L0_ffn_w_down"]))
    return {
        "qmapw": qm, "kmapw": km, "vmapw": v_map(qkv_w),
        "auxin": aux_input_rows(),
        "identw": np.eye(128, dtype=np.float32),
        "lepmapw": lep_proj_map(proj_w),
        "projbdw": proj_bd8(proj_w),
        "bd16w": bd16_avg(),
        "cwcols": cw_cols(np.asarray(inputs["L0_attn_conv_w0"], np.float32),
                          np.asarray(inputs["L0_attn_conv_b0"], np.float32),
                          np.asarray(inputs["L0_attn_conv_w1"], np.float32),
                          np.asarray(inputs["L0_attn_conv_b1"], np.float32)),
        "lncols": ln_cols(np.asarray(inputs["L0_attn_ln_w"], np.float32),
                          np.asarray(inputs["L0_attn_ln_b"], np.float32),
                          np.asarray(inputs["L0_attn_proj_b"], np.float32)),
        "sgcol": np.ascontiguousarray(
            np.asarray(inputs["L0_ffn_s_gate"], np.float32).reshape(64, 128).T),
        "sucol": np.ascontiguousarray(
            np.asarray(inputs["L0_ffn_s_up"], np.float32).reshape(64, 128).T),
        "epscol": np.full((128, 1), EPS, np.float32),
        "sdnw": np.ascontiguousarray(np.broadcast_to(np.asarray(inputs["L0_ffn_s_down"], np.float32), (128, DM))),
        "wgu": wgu, "wdn": wdn,
    }


def perm_rows(k, nbatch=4, rpb=512):
    mt = rpb // 16
    ridx = np.empty(nbatch * rpb, np.int64)
    i = 0
    for b in range(nbatch):
        for c in range(16):
            for m in range(mt):
                ridx[i] = b * 4096 + 256 * c + mt * k + m
                i += 1
    return ridx


def kernel(**inputs):
    x = np.asarray(inputs["L0_attn_x"], np.float32)
    B, H, Wd = x.shape
    consts = host_consts(inputs)
    p = Prog(nrows=2048, nbatch=4)
    nc, _ = p.build()
    xf = x.reshape(B * H, Wd)
    in_maps = []
    for k in range(8):
        xin = np.ascontiguousarray(np.concatenate(
            [x[b, 512 * k:512 * (k + 1), :] for b in range(B)], axis=0))
        ridx = perm_rows(k)
        xperm = np.ascontiguousarray(xf[ridx])
        m = {"xin": xin, "xperm": xperm}
        m.update(consts)
        in_maps.append(m)
    res = run_bass_kernel_spmd(nc, in_maps, core_ids=list(range(8)))
    out = np.zeros((B * H, Wd), np.float32)
    for k in range(8):
        out[perm_rows(k)] = res.results[k]["out"]
    return out.reshape(B, H, Wd)
